# version 06
# baseline (speedup 1.0000x reference)
import math
import numpy as np
import ml_dtypes
import concourse.bass as bass
import concourse.mybir as mybir
from concourse.bass_utils import run_bass_kernel_spmd

F32 = mybir.dt.float32
BF16 = mybir.dt.bfloat16
AF = mybir.ActivationFunctionType
ALU = mybir.AluOpType
AX = mybir.AxisListType

B, L, E = 2, 8192, 1024
NTOK = 2048
NTC = 16
NH, HD = 16, 64
FF = 4096
LN_EPS = 1e-5
EPS_SC = float(np.float32(1e-6) * np.float32(256.0))
LN_EPS_SC = float(np.float32(np.log(np.float64(EPS_SC))))


class _Ins:
    __slots__ = ("ctx", "ins", "eng", "is_dma")

    def __init__(self, ctx, ins, eng=None, is_dma=False):
        self.ctx, self.ins = ctx, ins
        self.eng, self.is_dma = eng, is_dma

    def inc(self, semname, k=1):
        ctx = self.ctx
        ctx.counts[semname] = ctx.counts.get(semname, 0) + k
        if self.ins is not None:
            self.ins.then_inc(ctx.semh[semname], k)
        if self.is_dma:
            if ctx.live == self.eng:
                ctx.real[self.eng].wait_ge(ctx.semh[semname],
                                           ctx.counts[semname])
        return self

    def fence(self, semname):
        ctx = self.ctx
        ctx.counts[semname] = ctx.counts.get(semname, 0) + 1
        if self.ins is not None:
            self.ins.then_inc(ctx.semh[semname], 1)
        if ctx.live == self.eng:
            ctx.real[self.eng].wait_ge(ctx.semh[semname], ctx.counts[semname])
        return self


class _Eng:
    def __init__(self, ctx, name):
        self._ctx, self._name = ctx, name

    def wait(self, markname, delta=0):
        ctx = self._ctx
        if ctx.live == "count":
            return
        sem, val = ctx.marks[markname]
        if ctx.live == self._name:
            ctx.real[self._name].wait_ge(ctx.semh[sem], val + delta)

    def wait_raw(self, semname, val):
        if self._ctx.live == self._name:
            self._ctx.real[self._name].wait_ge(self._ctx.semh[semname], val)

    def __getattr__(self, attr):
        ctx, name = self._ctx, self._name
        is_dma = attr == "dma_start"

        def call(*a, **k):
            if ctx.live == name:
                return _Ins(ctx, getattr(ctx.real[name], attr)(*a, **k),
                            name, is_dma)
            return _Ins(ctx, None, name, is_dma)

        return call


class Ctx:
    ENGINES = ("sp", "gp", "pe", "act", "dve")

    def __init__(self):
        self.marks = {}
        self.counts = {}
        self.live = "count"
        self.real = {}
        self.semh = {}
        self.first_counts = None
        for e in self.ENGINES:
            setattr(self, e, _Eng(self, e))

    def mark(self, markname, semname):
        v = self.counts.get(semname, 0)
        if self.live == "count":
            assert markname not in self.marks, f"dup mark {markname}"
            self.marks[markname] = (semname, v)
        else:
            assert self.marks[markname] == (semname, v), (
                markname, self.marks[markname], (semname, v))

    def begin(self, live):
        if self.live == "count":
            self.first_counts = dict(self.counts)
        self.live = live
        self.counts = {}

    def finish(self):
        assert self.counts == self.first_counts, (
            {k: (self.counts.get(k), self.first_counts.get(k))
             for k in set(self.counts) | set(self.first_counts)
             if self.counts.get(k) != self.first_counts.get(k)})


class Arena:
    def __init__(self, esize):
        self.esize = esize
        self.pending = []
        self.regions = {}
        self.top = 0

    def add(self, name, cols, live):
        self.pending.append((name, cols, frozenset(live)))

    def finalize(self):
        for name, cols, live in sorted(self.pending,
                                       key=lambda t: -t[1]):
            spans = sorted((o, o + cc) for (o, cc, lv) in
                           self.regions.values() if lv & live)
            off = 0
            for (s, e) in spans:
                if off + cols <= s:
                    break
                off = max(off, e)
            self.regions[name] = (off, cols, live)
            self.top = max(self.top, off + cols)

    def view(self, tensor, name, rows=128):
        off, cols, _ = self.regions[name]
        return tensor[0:rows, off:off + cols]

    def view2(self, tensor, name, r0, r1):
        off, cols, _ = self.regions[name]
        return tensor[r0:r1, off:off + cols]


def _hadamard(n):
    h = np.array([[1.0]], dtype=np.float32)
    while h.shape[0] < n:
        h = np.block([[h, h], [h, -h]])
    return h.astype(np.float32)


def _rope_tables(core_id):
    freqs = 10000.0 ** (-np.arange(0, HD, 2, dtype=np.float64) / HD)
    t0 = (core_id % 4) * NTOK
    lpos = t0 + np.arange(NTOK)
    ang = (lpos - 1.0)[:, None] * freqs[None, :]
    cos = np.repeat(np.cos(ang), 2, axis=1)
    sin = np.repeat(np.sin(ang), 2, axis=1)
    z = lpos == 0
    cos[z, :] = 1.0
    sin[z, :] = 0.0
    sinpp = sin.copy()
    sinpp[:, 1::2] *= -1.0
    cosT = np.tile(cos.T, (2, 1)).astype(np.float32)
    sinT = np.tile(sinpp.T, (2, 1)).astype(np.float32)
    return np.ascontiguousarray(cosT), np.ascontiguousarray(sinT)


def build_program(debug=False, stop=10, kstop=4, p1n=NTC, router=3):
    nc = bass.Bass("TRN2", target_bir_lowering=False)

    def din(name, shape, dt=F32):
        return nc.dram_tensor(name, shape, dt, kind="ExternalInput").ap()

    def dnout(name, shape, dt=F32):
        return nc.dram_tensor(name, shape, dt, kind="ExternalOutput").ap()

    x_in = din("x_slice", [NTOK, E])
    wqkvT_in = din("wqkvT", [E, 3 * E])
    woutT_in = din("woutT", [E, E])
    w1_in = din("w1_0", [E, FF])
    w2_in = din("w2_0", [FF, E])
    b1c_in = din("b1c", [128, 32])
    b2row_in = din("b2row", [1, E])
    grow_in = din("grow", [1, E])
    brow_in = din("brow", [1, E])
    gt_in = din("g_tiled", [64, 4096])
    s1_in = din("s1_tiled", [64, 4096])
    s2_in = din("s2_tiled", [64, 4096])
    s3_in = din("s3_tiled", [64, 4096])
    ht_in = din("h_tiled", [64, 4096], BF16)
    hmat_in = din("hmat", [64, 64], BF16)
    hs_in = din("hs", [64, 64], BF16)
    swap_in = din("swapm", [64, 64], BF16)
    i128f_in = din("i128f", [128, 128])
    i128b_in = din("i128b", [128, 128], BF16)
    repl_in = din("repl", [16, 1024], BF16)
    nh2_in = din("neghalf2", [128, 2], BF16)
    ones1r_in = din("ones1r", [1, 128])
    actc_in = din("actconsts", [128, 4])
    rwT_in = din("rwT", [E, 16])
    rb_in = din("rbcol", [16, 1])
    cosT_in = din("cosT", [128, NTOK])
    sinT_in = din("sinT", [128, NTOK])

    y_out = dnout("y_out", [NTOK, E])

    x1_dram = nc.dram_tensor("x1_dram", [NTOK, E], F32).ap()
    ar_in = nc.dram_tensor("ar_in", [65, NH * 256], F32).ap()
    ar_out = nc.dram_tensor("ar_out", [65, NH * 256], F32).ap()

    dbg = {}
    if debug:
        for nm, shape, dt in [
            ("d_u1T", [128, 8 * 2048], BF16), ("d_v", [128, NTC * 1040], BF16),
            ("d_spre", [65, 4096], F32), ("d_ssb", [128, 2080], BF16),
            ("d_nsqk", [128, 256], F32), ("d_den", [16, 2048], F32),
            ("d_f", [16, 2048], F32), ("d_recip", [16, 2048], BF16),
            ("d_oh", [128, 16384], BF16), ("d_u2T", [128, 16384], BF16),
            ("d_kc", [128, 2048], BF16), ("d_nsqq", [128, 256], F32),
            ("d_x1", [NTOK, E], F32),
            ("d_xr", [128, 1024], F32), ("d_xsq", [128, 1024], F32),
            ("d_cols", [128, 32], F32), ("d_u1r", [128, 1024], BF16),
        ]:
            dbg[nm] = dnout(nm, shape, dt)

    ab = Arena(2)
    af = Arena(4)

    ab.add("u1T", 8 * 2048, {1, 2, 3, 4})
    ab.add("wqkv_kv", 8 * 2048, {0, 1, 2})
    ab.add("wqkv_q", 8 * 1024, {0, 1, 2, 3, 4})
    ab.add("Wc", 4096, {0, 1, 2, 3, 4, 5, 6})
    ab.add("Ws", 4096, {0, 1, 2, 3, 4, 5, 6})
    ab.add("Xa", 4096, {0})
    ab.add("Xb", 4096, {0})
    ab.add("gt", 4096, {0})
    ab.add("s1t", 4096, {0})
    ab.add("s2t", 4096, {0})
    ab.add("s3t", 4096, {0})
    ab.add("htl", 4096, {0})
    ab.add("v_sb", NTC * 1040, {2})
    ab.add("kc", 2048, {2})
    ab.add("ks", 2048, {2})
    ab.add("ksq", 2048, {2})
    ab.add("kphi0", 256, {2})
    ab.add("kphi1", 256, {2})
    ab.add("spostb", 4096, {3, 4})
    ab.add("S_sb", 32 * 65, {3, 4, 5, 6})
    ab.add("sumk", 512, {3, 4, 5, 6})
    ab.add("qc", 8 * 2048, {3, 4, 5, 6})
    ab.add("qs", 8 * 2048, {3, 4, 5, 6})
    ab.add("qsq", 2048, {3, 4})
    ab.add("Qp0", 4096, {5, 6})
    ab.add("Qp1", 4096, {5, 6})
    ab.add("numer0", 2048, {6})
    ab.add("numer1", 2048, {6})
    ab.add("recipb", 2048, {5, 6})
    ab.add("outheads", 8 * 2048, {6, 7})
    ab.add("wout", 8 * 1024, {7})
    ab.add("u2T", 8 * 2048, {7, 8})
    ab.add("w2full", 32 * 1024, {7, 8})
    ab.add("w1r0", 8 * 512, {8})
    ab.add("w1r1", 8 * 512, {8})
    ab.add("h_sb", 32 * 512, {8})
    ab.add("rlog", 2048, {8})
    ab.add("u1r0", 1024, {1})
    ab.add("u1r1", 1024, {1})
    ab.add("u2r0", 1024, {7})
    ab.add("u2r1", 1024, {7})

    af.add("cosT", NTOK, {0, 1, 2, 3, 4})
    af.add("sinT", NTOK, {0, 1, 2, 3, 4})
    af.add("gam_bc", 1024, set(range(9)))
    af.add("bet_bc", 1024, set(range(9)))
    af.add("grow", 1024, {0})
    af.add("brow", 1024, {0})
    af.add("b2row", 1024, set(range(9)))
    af.add("b1c", 32, set(range(9)))
    af.add("i128f", 128, set(range(9)))
    af.add("xr0", 1024, {1, 7})
    af.add("xr1", 1024, {1, 7})
    af.add("x1t0", 1024, {7})
    af.add("x1t1", 1024, {7})
    af.add("xsq", 1024, {1, 7, 8})
    af.add("cols", 32, {1, 7, 8})
    af.add("ln2cols", 32, {7, 8})
    af.add("nsqk", 256, {2})
    af.add("nsqq", 256, {3, 4})
    af.add("ftm0", 16, {3, 4})
    af.add("ftm1", 16, {3, 4})
    af.add("fcl0", 16, {3, 4})
    af.add("fcl1", 16, {3, 4})
    af.add("f_fm", 2048, {3, 4, 5})
    af.add("spre", 4096, {2, 3})
    af.add("den", 2048, {5, 6} if debug else {5})
    af.add("dmx", 2048, {5, 6} if debug else {5})
    af.add("x1r0", 1024, {8})
    af.add("x1r1", 1024, {8})
    af.add("ytile", 1024, {8})
    af.add("yn", 1024, {8})
    af.add("t2c", 1024, {8})
    af.add("x1b", 1024, {8})

    ab.finalize()
    af.finalize()
    cap = 206 * 1024
    need = ab.top * 2 + af.top * 4
    assert need <= cap, (f"SBUF over budget: {need} > {cap} "
                         f"(bf {ab.top * 2}, f32 {af.top * 4})")

    A_BF = nc.alloc_sbuf_tensor("arena_bf", [128, ab.top], BF16)
    A_F = nc.alloc_sbuf_tensor("arena_f32", [128, af.top], F32)

    def rbv(name, rows=128):
        return ab.view(A_BF, name, rows)

    def rfv(name, rows=128):
        return af.view(A_F, name, rows)

    hmat_sb = nc.alloc_sbuf_tensor("hmat_sb", [64, 64], BF16)
    hs_sb = nc.alloc_sbuf_tensor("hs_sb", [64, 64], BF16)
    swap_sb = nc.alloc_sbuf_tensor("swap_sb", [64, 64], BF16)
    i128b_sb = nc.alloc_sbuf_tensor("i128b_sb", [128, 128], BF16)
    repl_sb = nc.alloc_sbuf_tensor("repl_sb", [16, 1024], BF16)
    nh2_sb = nc.alloc_sbuf_tensor("nh2_sb", [128, 2], BF16)
    ones1r_sb = nc.alloc_sbuf_tensor("ones1r_sb", [1, 128], F32)
    actc_sb = nc.alloc_sbuf_tensor("actc_sb", [128, 4], F32)
    rwT_sb = nc.alloc_sbuf_tensor("rwT_sb", [128, 128], BF16)
    rb_sb = nc.alloc_sbuf_tensor("rb_sb", [16, 1], F32)
    rlog_sb = ab.view(A_BF, "rlog", 16)

    PB = [nc.alloc_psum_tensor(f"pb{i}", [128, 512], F32) for i in range(6)]
    PT = [nc.alloc_psum_tensor(f"pt{i}", [128, 512], BF16) for i in range(2)]
    assert nc.psum_banks_remaining >= 0

    ctx = Ctx()

    def program(c):
        sp, gp, pe, act, dve = c.sp, c.gp, c.pe, c.act, c.dve
        ln_prev = [None]

        for d, s in [
            (rfv("cosT"), cosT_in), (rfv("sinT"), sinT_in),
            (rfv("grow", 1), grow_in), (rfv("brow", 1), brow_in),
            (rfv("b2row", 1), b2row_in), (rfv("b1c"), b1c_in),
            (rfv("i128f"), i128f_in),
            (hmat_sb[:, :], hmat_in), (hs_sb[:, :], hs_in),
            (swap_sb[:, :], swap_in), (i128b_sb[:, :], i128b_in),
            (repl_sb[:, :], repl_in), (nh2_sb[:, :], nh2_in),
            (ones1r_sb[:, :], ones1r_in),
            (actc_sb[:, :], actc_in),
            (rb_sb[:, :], rb_in),
            (rbv("htl", 64), ht_in),
        ]:
            sp.dma_start(out=d, in_=s).inc("ld", 16)
        c.mark("consts_loaded", "ld")

        for nm, src in [("gt", gt_in), ("s1t", s1_in),
                        ("s2t", s2_in), ("s3t", s3_in)]:
            gp.dma_start(out=rbv(nm, 64), in_=src).inc("wd", 16)
        rwv = rwT_in.rearrange("(cp p) e -> p cp e", p=128)
        gp.dma_start(out=rwT_sb[:, :].rearrange("p (cp e) -> p cp e", e=16),
                     in_=rwv).inc("wd", 16)
        c.mark("tiled_loaded", "wd")
        kvv = wqkvT_in[:, 1024:3072].rearrange("(c p) o -> p c o", p=128)
        kvd = rbv("wqkv_kv").rearrange("p (c o) -> p c o", o=2048)
        gp.dma_start(out=kvd, in_=kvv).inc("wd", 16)
        c.mark("wqkv_kv_loaded", "wd")
        qv = wqkvT_in[:, 0:1024].rearrange("(c p) o -> p c o", p=128)
        qd = rbv("wqkv_q").rearrange("p (c o) -> p c o", o=1024)
        gp.dma_start(out=qd, in_=qv).inc("wd", 16)
        c.mark("wqkv_q_loaded", "wd")

        pe.wait("consts_loaded")
        for i, rowname in enumerate(["grow", "brow"]):
            for half in range(2):
                k = i * 2 + half
                if k >= 2:
                    pe.wait(f"gb_ev_{k - 2}")
                pe.matmul(PB[2 + (k % 2)][:, :], ones1r_sb[:, :],
                          rfv(rowname, 1)[:, half * 512:(half + 1) * 512],
                          start=True, stop=True).inc("pe", 1)
                c.mark(f"gb_mm_{k}", "pe")
                dve.wait(f"gb_mm_{k}")
                dst = rfv("gam_bc" if i == 0 else "bet_bc")
                dve.tensor_copy(dst[:, half * 512:(half + 1) * 512],
                                PB[2 + (k % 2)][:, :]).inc("dve", 1)
                c.mark(f"gb_ev_{k}", "dve")
        c.mark("gb_done", "dve")

        dve.wait("tiled_loaded")
        dve.wait("consts_loaded")
        dve.tensor_mul(rbv("Xa", 64), rbv("htl", 64),
                       rbv("gt", 64)).fence("dve")
        dve.tensor_mul(rbv("Xb", 64), rbv("Xa", 64), rbv("s3t", 64)).inc("dve", 1)
        c.mark("wpre_x2", "dve")

        def wpre_stage(tag, prev_done, lhsT, src_name, scale_name, dst_name):
            for j in range(8):
                pe.wait(prev_done if j < 2 else f"{tag}_ev_{j - 2}")
                pe.matmul(PB[j % 2][0:64, :], lhsT,
                          rbv(src_name, 64)[:, j * 512:(j + 1) * 512],
                          start=True, stop=True).inc("pe", 1)
                c.mark(f"{tag}_mm_{j}", "pe")
                dve.wait(f"{tag}_mm_{j}")
                if scale_name is None:
                    dve.tensor_copy(rbv(dst_name, 64)[:, j * 512:(j + 1) * 512],
                                    PB[j % 2][0:64, :]).inc("dve", 1)
                else:
                    dve.tensor_mul(rbv(dst_name, 64)[:, j * 512:(j + 1) * 512],
                                   PB[j % 2][0:64, :],
                                   rbv(scale_name, 64)[:, j * 512:(j + 1) * 512]
                                   ).inc("dve", 1)
                c.mark(f"{tag}_ev_{j}", "dve")
            c.mark(f"{tag}_done", "dve")

        wpre_stage("wx3", "wpre_x2", hmat_sb[:, :], "Xb", "s2t", "Xa")
        wpre_stage("wx5", "wx3_done", hs_sb[:, :], "Xa", "s1t", "Wc")
        wpre_stage("wx7", "wx5_done", swap_sb[:, :], "Wc", None, "Ws")
        c.mark("wpre_done", "dve")
        sp.wait("wpre_done")
        sp.dma_start(out=ab.view2(A_BF, "Wc", 64, 128),
                     in_=rbv("Wc", 64)).inc("ld", 16)
        sp.dma_start(out=ab.view2(A_BF, "Ws", 64, 128),
                     in_=rbv("Ws", 64)).inc("ld", 16)
        c.mark("wrep_done", "ld")

        def ln_tile(tag, x_ap, out_ap, save_cols=None):
            cols = rfv("cols")
            if ln_prev[0] is not None:
                act.wait(f"lnout_{ln_prev[0]}")
            act.activation(rfv("xsq"), x_ap, AF.Square).inc("act", 1)
            c.mark(f"lnsq_{tag}", "act")
            dve.wait(f"lnsq_{tag}")
            dve.reduce_sum(cols[:, 0:1], rfv("xsq"), axis=AX.X)
            dve.reduce_sum(cols[:, 4:5], x_ap, axis=AX.X).inc("dve", 1)
            c.mark(f"lnred_{tag}", "dve")
            act.wait(f"lnred_{tag}")
            act.mul(cols[:, 12:13], cols[:, 4:5], -1.0 / 1024.0)
            act.mul(cols[:, 8:9], cols[:, 4:5], 1.0 / 1024.0).fence("act")
            act.activation(cols[:, 16:17], cols[:, 8:9],
                           AF.Square).inc("act", 1)
            c.mark(f"lnmid_{tag}", "act")
            dve.wait(f"lnmid_{tag}")
            dve.scalar_tensor_tensor(cols[:, 20:21], cols[:, 0:1], 1.0 / 1024.0,
                                     cols[:, 16:17], ALU.mult,
                                     ALU.subtract).inc("dve", 1)
            c.mark(f"lnvar_{tag}", "dve")
            act.wait(f"lnvar_{tag}")
            act.activation(cols[:, 24:25], cols[:, 20:21], AF.Ln,
                           bias=actc_sb[:, 0:1]).fence("act")
            act.activation(cols[:, 28:29], cols[:, 24:25], AF.Exp,
                           scale=actc_sb[:, 1:2]).inc("act", 1)
            c.mark(f"lnrstd_{tag}", "act")
            dve.wait(f"lnrstd_{tag}")
            if save_cols is not None:
                dve.tensor_copy(save_cols[:, 0:1], cols[:, 12:13])
                dve.tensor_copy(save_cols[:, 1:2], cols[:, 28:29])
            dve.tensor_scalar(out_ap, x_ap, cols[:, 12:13], cols[:, 28:29],
                              ALU.add, ALU.mult).inc("dve", 1)
            c.mark(f"lnout_{tag}", "dve")
            ln_prev[0] = tag

        def u_transpose(tag, u_tm, uT3, tc):
            for bat in range(2):
                gidx = tc * 2 + bat
                pe.wait(f"lnout_{tag}_{tc}")
                if gidx >= 2:
                    pe.wait(f"{tag}_trev_{gidx - 2}")
                for dc in range(4):
                    d = bat * 4 + dc
                    pe.transpose(PT[gidx % 2][:, dc * 128:(dc + 1) * 128],
                                 u_tm[:, d * 128:(d + 1) * 128],
                                 i128b_sb[:, :]).inc("pe", 1)
                c.mark(f"{tag}_tr_{gidx}", "pe")
                dve.wait(f"{tag}_tr_{gidx}")
                src = PT[gidx % 2][:, :].rearrange("p (d t) -> p d t", t=128)
                dve.tensor_copy(uT3[:, bat * 4:(bat + 1) * 4,
                                    tc * 128:(tc + 1) * 128],
                                src).inc("dve", 1)
                c.mark(f"{tag}_trev_{gidx}", "dve")

        if stop >= 1:
            u1T3 = rbv("u1T").rearrange("p (d t) -> p d t", t=2048)
            for tc in range(p1n):
                xr = rfv("xr0") if tc % 2 == 0 else rfv("xr1")
                if tc >= 2:
                    sp.wait(f"lnout_ln1_{tc - 2}")
                sp.dma_start(out=xr, in_=x_in[tc * 128:(tc + 1) * 128, :]).inc("ld", 16)
                c.mark(f"x_tile_{tc}", "ld")
                act.wait(f"x_tile_{tc}")
                dve.wait(f"x_tile_{tc}")
                if tc >= 2:
                    dve.wait(f"ln1_tr_{(tc - 2) * 2 + 1}")
                u1r = rbv("u1r0") if tc % 2 == 0 else rbv("u1r1")
                ln_tile(f"ln1_{tc}", xr, u1r)
                u_transpose("ln1", u1r, u1T3, tc)
            c.mark("u1T_done", "dve")
            if debug:
                sp.wait("u1T_done")
                sp.dma_start(out=dbg["d_u1T"], in_=rbv("u1T")).inc("st", 16)
                if p1n <= 2:
                    sp.dma_start(out=dbg["d_xr"], in_=rfv("xr0")).inc("st", 16)
                    sp.dma_start(out=dbg["d_xsq"], in_=rfv("xsq")).inc("st", 16)
                    sp.dma_start(out=dbg["d_cols"], in_=rfv("cols")).inc("st", 16)
                    sp.dma_start(out=dbg["d_u1r"], in_=rbv("u1r0")).inc("st", 16)
                c.mark("dbg_u1t", "st")

        if stop >= 2:
            v_sb = rbv("v_sb")
            v4 = v_sb.rearrange("p (t h d) -> p t h d", h=16, d=65)
            dve.memset(v4[:, :, :, 64:65], 1.0).inc("dve", 1)
            c.mark("v_ones", "dve")
            u1Tv = u1T3
            pe.wait("u1T_done")
            pe.wait("wqkv_kv_loaded")
            for tc in range(NTC):
                for half in range(2):
                    i = tc * 2 + half
                    if i >= 2:
                        pe.wait(f"v_ev_{i - 2}")
                    for dc in range(8):
                        pe.matmul(PB[i % 2][:, :],
                                  u1Tv[:, dc, tc * 128:(tc + 1) * 128],
                                  rbv("wqkv_kv")[:, dc * 2048 + 1024 + half * 512:
                                                 dc * 2048 + 1024 + (half + 1) * 512],
                                  start=(dc == 0), stop=(dc == 7)).inc("pe", 1)
                    c.mark(f"v_mm_{i}", "pe")
                    dve.wait(f"v_mm_{i}")
                    dve.wait("v_ones")
                    dst = v_sb[:, tc * 1040 + half * 520:
                               tc * 1040 + (half + 1) * 520]
                    dst3 = dst.rearrange("p (h d) -> p h d", d=65)[:, :, 0:64]
                    src3 = PB[i % 2][:, :].rearrange("p (h d) -> p h d", d=64)
                    dve.tensor_copy(dst3, src3).inc("dve", 1)
                    c.mark(f"v_ev_{i}", "dve")
            c.mark("v_done", "dve")

        if stop >= 3:
            kc, ks, ksq = rbv("kc"), rbv("ks"), rbv("ksq")
            nsqk = rfv("nsqk")
            nsqk3 = nsqk.rearrange("p (t h) -> p t h", h=16)
            for cc in range(8):
                for nt in range(4):
                    i = cc * 4 + nt
                    if i >= 2:
                        pe.wait(f"k_rope_{i - 2}")
                        pe.wait(f"k_sq_{i - 2}")
                    else:
                        pe.wait(f"v_ev_{30 + i}")
                    for dc in range(8):
                        pe.matmul(PB[i % 2][:, :],
                                  rbv("wqkv_kv")[:, dc * 2048 + cc * 128:
                                                 dc * 2048 + (cc + 1) * 128],
                                  u1Tv[:, dc, nt * 512:(nt + 1) * 512],
                                  start=(dc == 0), stop=(dc == 7)).inc("pe", 1)
                    c.mark(f"k_mm_{i}", "pe")
                    dve.wait(f"k_mm_{i}")
                    if cc > 0 and nt == 0 and kstop >= 4:
                        dve.wait(f"s_acc_{cc - 1}")
                    elif cc > 0 and nt == 0 and kstop == 3:
                        dve.wait(f"k_phi_{cc - 1}_31")
                    dve.tensor_mul(kc[:, nt * 512:(nt + 1) * 512], PB[i % 2][:, :],
                                   rfv("cosT")[:, nt * 512:(nt + 1) * 512])
                    dve.tensor_mul(ks[:, nt * 512:(nt + 1) * 512], PB[i % 2][:, :],
                                   rfv("sinT")[:, nt * 512:(nt + 1) * 512]
                                   ).inc("dve", 1)
                    c.mark(f"k_rope_{i}", "dve")
                    act.wait(f"k_mm_{i}")
                    act.wait(f"k_rope_{i}")
                    if cc > 0 and nt == 0 and kstop >= 2:
                        act.wait(f"nsq_mm_k_{cc - 1}")
                    act.activation(ksq[:, nt * 512:(nt + 1) * 512], PB[i % 2][:, :],
                                   AF.Square).inc("act", 1)
                    c.mark(f"k_sq_{i}", "act")
                if kstop >= 2:
                    pe.wait(f"k_sq_{cc * 4 + 3}")
                    if cc > 0:
                        pe.wait(f"nsq_ev_k_{cc - 1}")
                    for tc in range(NTC):
                        pe.matmul(PB[4][:, 2 * tc:2 * tc + 2],
                                  ksq[:, tc * 128:(tc + 1) * 128], nh2_sb[:, :],
                                  start=True, stop=True).inc("pe", 1)
                    c.mark(f"nsq_mm_k_{cc}", "pe")
                    dve.wait(f"nsq_mm_k_{cc}")
                    dve.tensor_copy(
                        nsqk3[:, :, 2 * cc:2 * cc + 2],
                        PB[4][:, 0:32].rearrange("p (t h) -> p t h", h=2)).inc("dve", 1)
                    c.mark(f"nsq_ev_k_{cc}", "dve")
                if cc == 0:
                    pe.wait("wrep_done")
                    pe.wait("v_done")
                for tc in (range(NTC) if kstop >= 3 else []):
                    for h2 in range(2):
                        j = tc * 2 + h2
                        h = 2 * cc + h2
                        if j >= 2:
                            pe.wait(f"k_exp_{cc}_{j - 2}")
                        elif cc > 0:
                            pe.wait(f"k_exp_{cc - 1}_{30 + j}")
                        hb = h2 * 64
                        pe.matmul(PB[2 + (j % 2)][:, 0:256],
                                  kc[hb:hb + 64, tc * 128:(tc + 1) * 128],
                                  ab.view2(A_BF, "Wc", hb, hb + 64)
                                  [:, h * 256:(h + 1) * 256],
                                  start=True, stop=False).inc("pe", 1)
                        pe.matmul(PB[2 + (j % 2)][:, 0:256],
                                  ks[hb:hb + 64, tc * 128:(tc + 1) * 128],
                                  ab.view2(A_BF, "Ws", hb, hb + 64)
                                  [:, h * 256:(h + 1) * 256],
                                  start=False, stop=True).inc("pe", 1)
                        c.mark(f"k_phi_{cc}_{j}", "pe")
                        act.wait(f"k_phi_{cc}_{j}")
                        act.wait(f"nsq_ev_k_{cc}")
                        if kstop >= 4 and j >= 2:
                            act.wait(f"k_smm_{cc}_{j - 2}")
                        elif kstop >= 4 and cc > 0:
                            act.wait(f"k_smm_{cc - 1}_{30 + j}")
                        kphi = rbv("kphi0" if j % 2 == 0 else "kphi1")
                        act.activation(kphi, PB[2 + (j % 2)][:, 0:256], AF.Exp,
                                       bias=nsqk[:, tc * 16 + h:tc * 16 + h + 1]
                                       ).inc("act", 1)
                        c.mark(f"k_exp_{cc}_{j}", "act")
                        if kstop >= 4:
                            pe.wait(f"k_exp_{cc}_{j}")
                            if tc == 0 and h2 == 0 and cc > 0:
                                pe.wait(f"s_ev_{cc - 1}")
                            pe.matmul(PB[5][0:65, h2 * 256:(h2 + 1) * 256],
                                      v_sb[:, tc * 1040 + 65 * h:
                                           tc * 1040 + 65 * h + 65],
                                      kphi, start=(tc == 0), stop=(tc == 15),
                                      skip_group_check=True).inc("pe", 1)
                            c.mark(f"k_smm_{cc}_{j}", "pe")
                if kstop >= 4:
                    c.mark(f"s_acc_{cc}", "pe")
                    dve.wait(f"s_acc_{cc}")
                    dve.tensor_copy(rfv("spre", 65)[:, cc * 512:(cc + 1) * 512],
                                    PB[5][0:65, :]).inc("dve", 1)
                    c.mark(f"s_ev_{cc}", "dve")
            c.mark("spre_done", "dve")

        if stop >= 4:
            gp.wait("spre_done")
            gp.dma_start(out=ar_in, in_=rfv("spre", 65)).inc("wd", 16)
            c.mark("ar_staged", "wd")
            gp.wait("ar_staged")
            gp.collective_compute(
                "AllReduce", ALU.add,
                replica_groups=[[0, 1, 2, 3], [4, 5, 6, 7]],
                ins=[ar_in], outs=[ar_out]).inc("cc", 1)
            gp.wait_raw("cc", 1)
            gp.dma_start(out=rbv("spostb", 65), in_=ar_out).inc("wd", 16)
            c.mark("spost_loaded", "wd")
            if stop >= 9:
                gp.wait("oh_done")
                wov = woutT_in.rearrange("(c p) o -> p c o", p=128)
                gp.dma_start(out=rbv("wout").rearrange("p (c o) -> p c o",
                                                       o=1024),
                             in_=wov).inc("wd", 16)
                c.mark("wout_loaded", "wd")
            if stop >= 10:
                w2v = w2_in.rearrange("(c p) d -> p c d", p=128)
                gp.dma_start(out=rbv("w2full").rearrange("p (c d) -> p c d",
                                                         d=1024),
                             in_=w2v).inc("wd", 16)
                c.mark("w2_loaded", "wd")

        if stop >= 3:
            if debug:
                sp.wait("v_done")
                sp.dma_start(out=dbg["d_v"], in_=v_sb).inc("st", 16)
                c.mark("dbg_v", "st")
                sp.wait("spre_done")
                sp.dma_start(out=dbg["d_kc"], in_=kc).inc("st", 16)
                sp.dma_start(out=dbg["d_spre"], in_=rfv("spre", 65)).inc("st", 16)
                sp.dma_start(out=dbg["d_nsqk"], in_=nsqk).inc("st", 16)
                c.mark("dbg_p2", "st")

        if stop >= 5:
            qc3 = rbv("qc").rearrange("p (cc t) -> p cc t", t=2048)
            qs3 = rbv("qs").rearrange("p (cc t) -> p cc t", t=2048)
            qsq = rbv("qsq")
            nsqq = rfv("nsqq")
            nsqq3 = nsqq.rearrange("p (t h) -> p t h", h=16)
            pe.wait("wqkv_q_loaded")
            if debug:
                pe.wait("dbg_v")
                pe.wait("dbg_p2")
            for cc in range(8):
                for nt in range(4):
                    i = cc * 4 + nt
                    if i >= 2:
                        pe.wait(f"q_rope_{i - 2}")
                        pe.wait(f"q_sq_{i - 2}")
                    else:
                        pe.wait(f"k_rope_{30 + i}")
                        pe.wait(f"k_sq_{30 + i}")
                    for dc in range(8):
                        pe.matmul(PB[i % 2][:, :],
                                  rbv("wqkv_q")[:, dc * 1024 + cc * 128:
                                                dc * 1024 + (cc + 1) * 128],
                                  u1Tv[:, dc, nt * 512:(nt + 1) * 512],
                                  start=(dc == 0), stop=(dc == 7)).inc("pe", 1)
                    c.mark(f"q_mm_{i}", "pe")
                    dve.wait(f"q_mm_{i}")
                    dve.tensor_mul(qc3[:, cc, nt * 512:(nt + 1) * 512],
                                   PB[i % 2][:, :],
                                   rfv("cosT")[:, nt * 512:(nt + 1) * 512])
                    dve.tensor_mul(qs3[:, cc, nt * 512:(nt + 1) * 512],
                                   PB[i % 2][:, :],
                                   rfv("sinT")[:, nt * 512:(nt + 1) * 512]
                                   ).inc("dve", 1)
                    c.mark(f"q_rope_{i}", "dve")
                    act.wait(f"q_mm_{i}")
                    act.wait(f"q_rope_{i}")
                    if cc > 0 and nt == 0:
                        act.wait(f"nsq_mm_q_{cc - 1}")
                    act.activation(qsq[:, nt * 512:(nt + 1) * 512], PB[i % 2][:, :],
                                   AF.Square).inc("act", 1)
                    c.mark(f"q_sq_{i}", "act")
                pe.wait(f"q_sq_{cc * 4 + 3}")
                pe.wait(f"nsq_ev_q_{cc - 1}" if cc > 0 else "nsq_ev_k_7")
                for tc in range(NTC):
                    pe.matmul(PB[4][:, 2 * tc:2 * tc + 2],
                              qsq[:, tc * 128:(tc + 1) * 128], nh2_sb[:, :],
                              start=True, stop=True).inc("pe", 1)
                c.mark(f"nsq_mm_q_{cc}", "pe")
                dve.wait(f"nsq_mm_q_{cc}")
                dve.tensor_copy(
                    nsqq3[:, :, 2 * cc:2 * cc + 2],
                    PB[4][:, 0:32].rearrange("p (t h) -> p t h", h=2)).inc("dve", 1)
                c.mark(f"nsq_ev_q_{cc}", "dve")
            c.mark("qside_done", "dve")

            act.wait("qside_done")
            dve.wait("qside_done")
            for tc in range(NTC):
                ftm = rfv("ftm0") if tc % 2 == 0 else rfv("ftm1")
                fcl = rfv("fcl0") if tc % 2 == 0 else rfv("fcl1")
                if tc >= 2:
                    dve.wait(f"f_tm_{tc - 2}")
                dve.tensor_scalar(fcl, nsqq[:, tc * 16:(tc + 1) * 16],
                                  -1.0, 85.0, ALU.mult,
                                  ALU.min).inc("dve", 1)
                c.mark(f"f_cl_{tc}", "dve")
                act.wait(f"f_cl_{tc}")
                if tc >= 2:
                    act.wait(f"f_tr_{tc - 2}")
                act.activation(ftm, fcl, AF.Exp,
                               bias=actc_sb[:, 3:4]).inc("act", 1)
                c.mark(f"f_tm_{tc}", "act")
                pe.wait(f"f_tm_{tc}")
                if tc >= 2:
                    pe.wait(f"f_ev_{tc - 2}")
                else:
                    pe.wait("nsq_ev_q_7")
                    pe.wait("spre_done")
                pe.transpose(PB[4 + (tc % 2)][0:16, 0:128], ftm,
                             rfv("i128f")).inc("pe", 1)
                c.mark(f"f_tr_{tc}", "pe")
                dve.wait(f"f_tr_{tc}")
                if tc < 2:
                    dve.wait("ar_staged")
                dve.tensor_copy(rfv("f_fm", 16)[:, tc * 128:(tc + 1) * 128],
                                PB[4 + (tc % 2)][0:16, 0:128]).inc("dve", 1)
                c.mark(f"f_ev_{tc}", "dve")
            c.mark("f_done", "dve")

        if stop >= 6:
            pe.wait("spost_loaded")
            pe.wait("u1T_done")
            for h in range(NH):
                for half in range(2):
                    i = h * 2 + half
                    if i >= 2:
                        pe.wait(f"spost_ev_{i - 2}")
                    pe.transpose(PT[i % 2][:, 0:65],
                                 rbv("spostb", 65)[:, h * 256 + half * 128:
                                                   h * 256 + (half + 1) * 128],
                                 i128b_sb[0:65, 0:65]).inc("pe", 1)
                    c.mark(f"spost_tr_{i}", "pe")
                    dve.wait(f"spost_tr_{i}")
                    dve.tensor_copy(rbv("S_sb")[:, i * 65:(i + 1) * 65],
                                    PT[i % 2][:, 0:65]).inc("dve", 1)
                    c.mark(f"spost_ev_{i}", "dve")
            dve.wait("spost_ev_31")
            dve.memset(rbv("sumk"), 0.0).fence("dve")
            for h in range(NH):
                for half in range(2):
                    i = h * 2 + half
                    ins = dve.tensor_copy(rbv("sumk")[:, i * 16 + h:i * 16 + h + 1],
                                          rbv("S_sb")[:, i * 65 + 64:i * 65 + 65])
                    if i == 31:
                        ins.inc("dve", 1)
            c.mark("sumk_done", "dve")
            if debug:
                sp.wait("sumk_done")
                sp.dma_start(out=dbg["d_ssb"], in_=rbv("S_sb")).inc("st", 16)
                sp.dma_start(out=dbg["d_nsqq"], in_=nsqq).inc("st", 16)
                sp.wait("f_done")
                sp.dma_start(out=dbg["d_f"], in_=rfv("f_fm", 16)).inc("st", 16)
                c.mark("dbg_s", "st")

        if stop >= 7:
            def phi_q_tile(tag, h, mc, nt, jg):
                cchunk, hr = h // 2, (h % 2) * 64
                if jg >= 2:
                    pe.wait(f"{tag}_exp_{jg - 2}")
                pe.matmul(PB[jg % 2][:, :],
                          ab.view2(A_BF, "Wc", hr, hr + 64)
                          [:, h * 256 + mc * 128:h * 256 + (mc + 1) * 128],
                          qc3[hr:hr + 64, cchunk, nt * 512:(nt + 1) * 512],
                          start=True, stop=False).inc("pe", 1)
                pe.matmul(PB[jg % 2][:, :],
                          ab.view2(A_BF, "Ws", hr, hr + 64)
                          [:, h * 256 + mc * 128:h * 256 + (mc + 1) * 128],
                          qs3[hr:hr + 64, cchunk, nt * 512:(nt + 1) * 512],
                          start=False, stop=True).inc("pe", 1)
                c.mark(f"{tag}_phi_{jg}", "pe")
                act.wait(f"{tag}_phi_{jg}")
                qp = rbv("Qp0" if h % 2 == 0 else "Qp1")
                act.activation(qp[:, mc * 2048 + nt * 512:mc * 2048 + (nt + 1) * 512],
                               PB[jg % 2][:, :], AF.Exp).inc("act", 1)
                c.mark(f"{tag}_exp_{jg}", "act")

            jg = 0
            pe.wait("qside_done")
            pe.wait("sumk_done")
            pe.wait("f_done")
            for h in range(NH):
                qp = rbv("Qp0" if h % 2 == 0 else "Qp1")
                if h >= 2:
                    act.wait(f"a0_den_{h - 2}")
                for mc in range(2):
                    for nt in range(4):
                        phi_q_tile("a0", h, mc, nt, jg)
                        jg += 1
                pe.wait(f"a0_exp_{jg - 1}")
                for mc in range(2):
                    for nt in range(4):
                        pe.matmul(PB[2 + nt][0:16, :],
                                  rbv("sumk")[:, (h * 2 + mc) * 16:
                                              (h * 2 + mc + 1) * 16],
                                  qp[:, mc * 2048 + nt * 512:
                                     mc * 2048 + (nt + 1) * 512],
                                  start=(h == 0 and mc == 0),
                                  stop=(h == NH - 1 and mc == 1),
                                  skip_group_check=True).inc("pe", 1)
                c.mark(f"a0_den_{h}", "pe")
            c.mark("a0_done", "pe")
            dve.wait("a0_done")
            for nt in range(4):
                ins = dve.tensor_copy(
                    rfv("den", 16)[:, nt * 512:(nt + 1) * 512],
                    PB[2 + nt][0:16, :])
                ins.fence("dve") if nt == 3 else ins.inc("dve", 1)
            dve.tensor_max(rfv("dmx", 16), rfv("den", 16),
                           rfv("f_fm", 16)).inc("dve", 1)
            c.mark("recip_f1", "dve")
            dve.wait("recip_f1")
            dve.reciprocal(rfv("den", 16), rfv("dmx", 16)).inc("dve", 1)
            c.mark("recip_f2", "dve")
            dve.wait("recip_f2")
            dve.tensor_copy(rbv("recipb", 16), rfv("den", 16)).inc("dve", 1)
            c.mark("recip_done", "dve")
            if debug:
                sp.wait("recip_done")
                sp.dma_start(out=dbg["d_den"], in_=rfv("dmx", 16)).inc("st", 16)
                sp.dma_start(out=dbg["d_recip"], in_=rbv("recipb", 16)).inc("st", 16)
                c.mark("dbg_recip", "st")

        if stop >= 8:
            jg = 0
            for h in range(NH):
                qp = rbv("Qp0" if h % 2 == 0 else "Qp1")
                numer = rbv("numer0" if h % 2 == 0 else "numer1")
                if h >= 2:
                    act.wait(f"a1_nmm_{h - 2}_3")
                else:
                    act.wait(f"a0_den_{14 + h}")
                for mc in range(2):
                    for nt in range(4):
                        phi_q_tile("a1", h, mc, nt, jg)
                        jg += 1
                pe.wait(f"a1_exp_{jg - 1}")
                if h == 0:
                    pe.wait("recip_done")
                for nt in range(4):
                    gidx = h * 4 + nt
                    if gidx >= 2:
                        pe.wait(f"a1_oh_{gidx - 2}")
                    for half in range(2):
                        pe.matmul(PB[2 + (nt % 2)][0:65, :],
                                  rbv("S_sb")[:, (h * 2 + half) * 65:
                                              (h * 2 + half + 1) * 65],
                                  qp[:, half * 2048 + nt * 512:
                                     half * 2048 + (nt + 1) * 512],
                                  start=(half == 0), stop=(half == 1)).inc("pe", 1)
                    c.mark(f"a1_nmm_{h}_{nt}", "pe")
                    act.wait(f"a1_nmm_{h}_{nt}")
                    if h >= 2:
                        act.wait(f"a1_oh_{(h - 2) * 4 + nt}")
                    act.activation(numer[0:64, nt * 512:(nt + 1) * 512],
                                   PB[2 + (nt % 2)][0:64, :],
                                   AF.Copy).inc("act", 1)
                    c.mark(f"a1_nev_{h}_{nt}", "act")
                    pe.matmul(PB[4 + (nt % 2)][0:64, :],
                              repl_sb[:, h * 64:(h + 1) * 64],
                              rbv("recipb", 16)[:, nt * 512:(nt + 1) * 512],
                              start=True, stop=True).inc("pe", 1)
                    c.mark(f"a1_bc_{h}_{nt}", "pe")
                    dve.wait(f"a1_nev_{h}_{nt}")
                    dve.wait(f"a1_bc_{h}_{nt}")
                    hr = (h % 2) * 64
                    dve.tensor_mul(
                        rbv("outheads")[hr:hr + 64,
                                        (h // 2) * 2048 + nt * 512:
                                        (h // 2) * 2048 + (nt + 1) * 512],
                        PB[4 + (nt % 2)][0:64, :],
                        numer[0:64, nt * 512:(nt + 1) * 512]).inc("dve", 1)
                    c.mark(f"a1_oh_{gidx}", "dve")
            c.mark("oh_done", "dve")
            if debug:
                sp.wait("oh_done")
                sp.dma_start(out=dbg["d_oh"], in_=rbv("outheads")).inc("st", 16)
                c.mark("dbg_oh", "st")

        if stop >= 9:
            oh3 = rbv("outheads").rearrange("p (e t) -> p e t", t=2048)
            u2T3 = rbv("u2T").rearrange("p (d t) -> p d t", t=2048)
            pe.wait("wout_loaded")
            for tc in range(NTC):
                xr = rfv("xr0") if tc % 2 == 0 else rfv("xr1")
                if tc >= 2:
                    sp.wait(f"x1_half_{(tc - 2) * 2 + 1}")
                else:
                    sp.wait(f"lnout_ln1_{14 + tc}")
                    sp.wait("oh_done")
                sp.dma_start(out=xr,
                             in_=x_in[tc * 128:(tc + 1) * 128, :]).inc("ld", 16)
                c.mark(f"x2_tile_{tc}", "ld")
                x1t = rfv("x1t0") if tc % 2 == 0 else rfv("x1t1")
                for ont in range(2):
                    j = tc * 2 + ont
                    if j == 0:
                        pe.wait("oh_done")
                    if j >= 2:
                        pe.wait(f"x1_half_{j - 2}")
                    for ec in range(8):
                        pe.matmul(PB[j % 2][:, :],
                                  oh3[:, ec, tc * 128:(tc + 1) * 128],
                                  rbv("wout")[:, ec * 1024 + ont * 512:
                                              ec * 1024 + (ont + 1) * 512],
                                  start=(ec == 0), stop=(ec == 7)).inc("pe", 1)
                    c.mark(f"att_mm_{j}", "pe")
                    dve.wait(f"att_mm_{j}")
                    dve.wait(f"x2_tile_{tc}")
                    if tc >= 2 and ont == 0:
                        dve.wait(f"x1_spill_{tc - 2}")
                        dve.wait(f"ln2_tr_{(tc - 2) * 2 + 1}")
                    dve.tensor_add(x1t[:, ont * 512:(ont + 1) * 512],
                                   PB[j % 2][:, :],
                                   xr[:, ont * 512:(ont + 1) * 512]).inc("dve", 1)
                    c.mark(f"x1_half_{j}", "dve")
                sp.wait(f"x1_half_{tc * 2 + 1}")
                sp.dma_start(out=x1_dram[tc * 128:(tc + 1) * 128, :],
                             in_=x1t).inc("ld", 16)
                c.mark(f"x1_spill_{tc}", "ld")
                act.wait(f"x1_half_{tc * 2 + 1}")
                u2r = rbv("u2r0") if tc % 2 == 0 else rbv("u2r1")
                if tc >= 2:
                    dve.wait(f"ln2_tr_{(tc - 2) * 2 + 1}")
                ln_tile(f"ln2_{tc}", x1t, u2r,
                        save_cols=rfv("ln2cols")[:, 2 * tc:2 * tc + 2])
                u_transpose("ln2", u2r, u2T3, tc)
            c.mark("u2T_done", "dve")
            if debug:
                sp.wait("u2T_done")
                sp.dma_start(out=dbg["d_x1"], in_=x1_dram).inc("st", 16)
                sp.dma_start(out=dbg["d_u2T"], in_=rbv("u2T")).inc("st", 16)
                c.mark("dbg_u2", "st")

        if stop >= 10:
            if router >= 2:
                pe.wait("u2T_done")
                for nt in range(4):
                    if nt >= 1 and router >= 3:
                        pe.wait(f"rlog_ev_{nt - 1}")
                    for dc in range(8):
                        pe.matmul(PB[4][0:16, :],
                                  rwT_sb[:, dc * 16:(dc + 1) * 16],
                                  u2T3[:, dc, nt * 512:(nt + 1) * 512],
                                  start=(dc == 0), stop=(dc == 7),
                                  skip_group_check=True).inc("pe", 1)
                    c.mark(f"rlog_mm_{nt}", "pe")
                    if router >= 3:
                        dve.wait(f"rlog_mm_{nt}")
                        dve.tensor_scalar_add(
                            rlog_sb[:, nt * 512:(nt + 1) * 512],
                            PB[4][0:16, :], rb_sb[:, :]).inc("dve", 1)
                        c.mark(f"rlog_ev_{nt}", "dve")

            h3 = rbv("h_sb").rearrange("p (kf t) -> p kf t", t=512)
            for th in range(4):
                for fb in range(8):
                    gi = th * 8 + fb
                    w1r = rbv("w1r0" if gi % 2 == 0 else "w1r1")
                    if gi < 2:
                        gp.wait("u2T_done")
                    else:
                        gp.wait(f"h_mm_{(gi - 2) * 4 + 3}")
                    w1v = w1_in[:, fb * 512:(fb + 1) * 512].rearrange(
                        "(cp p) f -> p cp f", p=128)
                    gp.dma_start(out=w1r.rearrange("p (cp f) -> p cp f", f=512),
                                 in_=w1v).inc("wd", 16)
                    c.mark(f"w1_ld_{gi}", "wd")
                    for fc in range(4):
                        j = gi * 4 + fc
                        pe.wait(f"w1_ld_{gi}")
                        if j == 0:
                            pe.wait("u2T_done")
                            pe.wait("w2_loaded")
                        if j >= 2:
                            pe.wait(f"h_relu_{j - 2}")
                        for dc in range(8):
                            pe.matmul(PB[j % 2][:, :],
                                      w1r[:, dc * 512 + fc * 128:
                                          dc * 512 + (fc + 1) * 128],
                                      u2T3[:, dc, th * 512:(th + 1) * 512],
                                      start=(dc == 0), stop=(dc == 7)).inc("pe", 1)
                        c.mark(f"h_mm_{j}", "pe")
                        act.wait(f"h_mm_{j}")
                        kf = fb * 4 + fc
                        act.activation(h3[:, kf, :], PB[j % 2][:, :], AF.Relu,
                                       bias=rfv("b1c")[:, kf:kf + 1]).inc("act", 1)
                        c.mark(f"h_relu_{j}", "act")
                for ts in range(4):
                    tc = th * 4 + ts
                    x1r = rfv("x1r0") if tc % 2 == 0 else rfv("x1r1")
                    if tc >= 2:
                        sp.wait(f"fin_tile_{tc - 2}")
                    else:
                        sp.wait("u2T_done")
                    sp.wait(f"x1_spill_{tc}")
                    sp.dma_start(out=x1r, in_=x1_dram[tc * 128:(tc + 1) * 128, :]
                                 ).inc("ld", 16)
                    c.mark(f"x1_reload_{tc}", "ld")
                    for dnt in range(2):
                        j2 = tc * 2 + dnt
                        pe.wait(f"h_relu_{th * 32 + 31}")
                        if j2 >= 2:
                            pe.wait(f"y_half_{j2 - 2}")
                        for kf in range(32):
                            pe.matmul(PB[2 + (j2 % 2)][:, :],
                                      h3[:, kf, ts * 128:(ts + 1) * 128],
                                      rbv("w2full")[:, kf * 1024 + dnt * 512:
                                                    kf * 1024 + (dnt + 1) * 512],
                                      start=(kf == 0),
                                      stop=(kf == 31)).inc("pe", 1)
                        pe.matmul(PB[2 + (j2 % 2)][:, :], ones1r_sb[:, :],
                                  rfv("b2row", 1)[:, dnt * 512:(dnt + 1) * 512],
                                  start=False, stop=True,
                                  skip_group_check=True).inc("pe", 1)
                        c.mark(f"eo_mm_{j2}", "pe")
                    dve.wait(f"x1_reload_{tc}")
                    dve.tensor_scalar(rfv("x1b"), x1r,
                                      rfv("ln2cols")[:, 2 * tc:2 * tc + 1],
                                      rfv("ln2cols")[:, 2 * tc + 1:2 * tc + 2],
                                      ALU.add, ALU.mult).inc("dve", 1)
                    c.mark(f"u2rc_{tc}", "dve")
                    dve.wait(f"u2rc_{tc}")
                    for dnt in range(2):
                        j2 = tc * 2 + dnt
                        dve.wait(f"eo_mm_{j2}")
                        dve.tensor_add(rfv("ytile")[:, dnt * 512:(dnt + 1) * 512],
                                       PB[2 + (j2 % 2)][:, :],
                                       rfv("x1b")[:, dnt * 512:(dnt + 1) * 512]
                                       ).inc("dve", 1)
                        c.mark(f"y_half_{j2}", "dve")
                    act.wait(f"y_half_{tc * 2 + 1}")
                    ln_tile(f"ln3_{tc}", rfv("ytile"), rfv("yn"))
                    dve.wait(f"lnout_ln3_{tc}")
                    dve.tensor_mul(rfv("t2c"), rfv("yn"), rfv("gam_bc"))
                    dve.tensor_add(rfv("x1b"), x1r,
                                   rfv("bet_bc")).fence("dve")
                    c.mark(f"fin_pre_{tc}", "dve")
                    dve.tensor_add(rfv("ytile"), rfv("t2c"), rfv("x1b")).inc("dve", 1)
                    c.mark(f"fin_dve_{tc}", "dve")
                    sp.wait(f"fin_dve_{tc}")
                    sp.dma_start(out=y_out[tc * 128:(tc + 1) * 128, :],
                                 in_=rfv("ytile")).inc("st", 16)
                    c.mark(f"fin_tile_{tc}", "st")

        if c.live == "count":
            c.marks["__st_total"] = ("st", c.counts.get("st", 0))
        sp.wait_raw("st", c.marks["__st_total"][1])

    program(ctx)
    with (
        nc.semaphore("s_ld") as s_ld,
        nc.semaphore("s_wd") as s_wd,
        nc.semaphore("s_pe") as s_pe,
        nc.semaphore("s_act") as s_act,
        nc.semaphore("s_dve") as s_dve,
        nc.semaphore("s_cc") as s_cc,
        nc.semaphore("s_st") as s_st,
        nc.Block() as block,
    ):
        ctx.semh = {"ld": s_ld, "wd": s_wd, "pe": s_pe, "act": s_act,
                    "dve": s_dve, "cc": s_cc, "st": s_st}

        @block.sync
        def _(eng):
            ctx.real["sp"] = eng
            ctx.begin("sp")
            program(ctx)
            ctx.finish()

        @block.gpsimd
        def _(eng):
            ctx.real["gp"] = eng
            ctx.begin("gp")
            program(ctx)
            ctx.finish()

        @block.tensor
        def _(eng):
            ctx.real["pe"] = eng
            ctx.begin("pe")
            program(ctx)
            ctx.finish()

        @block.scalar
        def _(eng):
            ctx.real["act"] = eng
            ctx.begin("act")
            program(ctx)
            ctx.finish()

        @block.vector
        def _(eng):
            ctx.real["dve"] = eng
            ctx.begin("dve")
            program(ctx)
            ctx.finish()

    return nc


def make_in_maps(inputs):
    x = np.asarray(inputs["L0_attn_x"], dtype=np.float32)
    w_qkv = np.asarray(inputs["L0_attn_w_qkv"], dtype=np.float32)
    w_out = np.asarray(inputs["L0_attn_w_out"], dtype=np.float32)
    signs = np.asarray(inputs["L0_attn_sorf_signs"], dtype=np.float32)
    scales = np.asarray(inputs["L0_attn_sorf_scales"], dtype=np.float32)
    w1 = np.asarray(inputs["L0_ffn_w1"], dtype=np.float32)
    b1 = np.asarray(inputs["L0_ffn_b1"], dtype=np.float32)
    w2 = np.asarray(inputs["L0_ffn_w2"], dtype=np.float32)
    b2 = np.asarray(inputs["L0_ffn_b2"], dtype=np.float32)
    gamma = np.asarray(inputs["L0_ffn_gamma"], dtype=np.float32)
    beta = np.asarray(inputs["L0_ffn_beta"], dtype=np.float32)

    hm = _hadamard(HD)
    swap = np.zeros((HD, HD), dtype=np.float32)
    for i in range(0, HD, 2):
        swap[i, i + 1] = 1.0
        swap[i + 1, i] = 1.0
    rep = np.zeros((NH, NH * HD), dtype=np.float32)
    for h in range(NH):
        rep[h, h * HD:(h + 1) * HD] = 1.0
    nh2 = np.zeros((128, 2), dtype=np.float32)
    nh2[0:64, 0] = -0.5
    nh2[64:128, 1] = -0.5

    sgn = signs.reshape(64, 3, 64)
    router_w = np.asarray(inputs["L0_ffn_router_w"], dtype=np.float32)
    router_b = np.asarray(inputs["L0_ffn_router_b"], dtype=np.float32)
    common = {
        "rwT": np.ascontiguousarray(np.concatenate(
            [router_w.T, np.zeros((E, 8), np.float32)], axis=1)),
        "rbcol": np.ascontiguousarray(np.concatenate(
            [router_b, np.zeros(8, np.float32)]).reshape(16, 1)),
        "wqkvT": np.ascontiguousarray(w_qkv.T),
        "woutT": np.ascontiguousarray(w_out.T),
        "w1_0": np.ascontiguousarray(w1[0]),
        "w2_0": np.ascontiguousarray(w2[0]),
        "b1c": np.ascontiguousarray(b1[0].reshape(32, 128).T),
        "b2row": np.ascontiguousarray(b2[0].reshape(1, E)),
        "grow": np.ascontiguousarray(gamma.reshape(1, E)),
        "brow": np.ascontiguousarray(beta.reshape(1, E)),
        "g_tiled": np.ascontiguousarray(
            np.tile(scales.reshape(1, 4096), (64, 1)).astype(np.float32)),
        "s1_tiled": np.ascontiguousarray(
            np.repeat(sgn[:, 0, :].T, 64, axis=1).astype(np.float32)),
        "s2_tiled": np.ascontiguousarray(
            np.repeat(sgn[:, 1, :].T, 64, axis=1).astype(np.float32)),
        "s3_tiled": np.ascontiguousarray(
            np.repeat(sgn[:, 2, :].T, 64, axis=1).astype(np.float32)),
        "h_tiled": np.ascontiguousarray(
            np.tile(hm, (1, 64)).astype(ml_dtypes.bfloat16)),
        "hmat": np.ascontiguousarray(hm.astype(ml_dtypes.bfloat16)),
        "hs": np.ascontiguousarray((hm / 4096.0).astype(ml_dtypes.bfloat16)),
        "swapm": np.ascontiguousarray(swap.astype(ml_dtypes.bfloat16)),
        "i128f": np.eye(128, dtype=np.float32),
        "actconsts": np.ascontiguousarray(np.tile(
            np.array([LN_EPS, -0.5, -1.0, LN_EPS_SC], dtype=np.float32),
            (128, 1))),
        "i128b": np.ascontiguousarray(
            np.eye(128, dtype=np.float32).astype(ml_dtypes.bfloat16)),
        "repl": np.ascontiguousarray(rep.astype(ml_dtypes.bfloat16)),
        "neghalf2": np.ascontiguousarray(nh2.astype(ml_dtypes.bfloat16)),
        "ones1r": np.ones((1, 128), dtype=np.float32),
    }

    in_maps = []
    for core in range(8):
        bb = core // 4
        t0 = (core % 4) * NTOK
        cosT, sinT = _rope_tables(core)
        m = dict(common)
        m["x_slice"] = np.ascontiguousarray(x[bb, t0:t0 + NTOK, :])
        m["cosT"] = cosT
        m["sinT"] = sinT
        in_maps.append(m)
    return in_maps


def kernel(**inputs):
    nc = build_program(debug=False)
    in_maps = make_in_maps(inputs)
    res = run_bass_kernel_spmd(nc, in_maps, core_ids=list(range(8)))
    out = np.empty((B, L, E), dtype=np.float32)
    for core in range(8):
        bb = core // 4
        t0 = (core % 4) * NTOK
        out[bb, t0:t0 + NTOK, :] = np.asarray(res.results[core]["y_out"],
                                              dtype=np.float32)
    return out


# version 07
# speedup vs baseline: 1.1546x; 1.1546x over previous
import math
import numpy as np
import ml_dtypes
import concourse.bass as bass
import concourse.mybir as mybir
from concourse.bass_utils import run_bass_kernel_spmd

F32 = mybir.dt.float32
BF16 = mybir.dt.bfloat16
AF = mybir.ActivationFunctionType
ALU = mybir.AluOpType
AX = mybir.AxisListType

B, L, E = 2, 8192, 1024
NTOK = 2048
NTC = 16
NH, HD = 16, 64
FF = 4096
LN_EPS = 1e-5
EPS_SC = float(np.float32(1e-6) * np.float32(256.0))
LN_EPS_SC = float(np.float32(np.log(np.float64(EPS_SC))))


class _Ins:
    __slots__ = ("ctx", "ins", "eng", "is_dma")

    def __init__(self, ctx, ins, eng=None, is_dma=False):
        self.ctx, self.ins = ctx, ins
        self.eng, self.is_dma = eng, is_dma

    def inc(self, semname, k=1):
        ctx = self.ctx
        ctx.counts[semname] = ctx.counts.get(semname, 0) + k
        if self.ins is not None:
            self.ins.then_inc(ctx.semh[semname], k)
        if self.is_dma:
            if ctx.live == self.eng:
                ctx.real[self.eng].wait_ge(ctx.semh[semname],
                                           ctx.counts[semname])
        return self

    def fence(self, semname):
        ctx = self.ctx
        ctx.counts[semname] = ctx.counts.get(semname, 0) + 1
        if self.ins is not None:
            self.ins.then_inc(ctx.semh[semname], 1)
        if ctx.live == self.eng:
            ctx.real[self.eng].wait_ge(ctx.semh[semname], ctx.counts[semname])
        return self


class _Eng:
    def __init__(self, ctx, name):
        self._ctx, self._name = ctx, name

    def wait(self, markname, delta=0):
        ctx = self._ctx
        if ctx.live == "count":
            return
        sem, val = ctx.marks[markname]
        if ctx.live == self._name:
            ctx.real[self._name].wait_ge(ctx.semh[sem], val + delta)

    def wait_raw(self, semname, val):
        if self._ctx.live == self._name:
            self._ctx.real[self._name].wait_ge(self._ctx.semh[semname], val)

    def __getattr__(self, attr):
        ctx, name = self._ctx, self._name
        is_dma = attr == "dma_start"

        def call(*a, **k):
            if ctx.live == name:
                return _Ins(ctx, getattr(ctx.real[name], attr)(*a, **k),
                            name, is_dma)
            return _Ins(ctx, None, name, is_dma)

        return call


class Ctx:
    ENGINES = ("sp", "gp", "pe", "act", "dve")

    def __init__(self):
        self.marks = {}
        self.counts = {}
        self.live = "count"
        self.real = {}
        self.semh = {}
        self.first_counts = None
        for e in self.ENGINES:
            setattr(self, e, _Eng(self, e))

    def mark(self, markname, semname):
        v = self.counts.get(semname, 0)
        if self.live == "count":
            assert markname not in self.marks, f"dup mark {markname}"
            self.marks[markname] = (semname, v)
        else:
            assert self.marks[markname] == (semname, v), (
                markname, self.marks[markname], (semname, v))

    def begin(self, live):
        if self.live == "count":
            self.first_counts = dict(self.counts)
        self.live = live
        self.counts = {}

    def finish(self):
        assert self.counts == self.first_counts, (
            {k: (self.counts.get(k), self.first_counts.get(k))
             for k in set(self.counts) | set(self.first_counts)
             if self.counts.get(k) != self.first_counts.get(k)})


class Arena:
    def __init__(self, esize):
        self.esize = esize
        self.pending = []
        self.regions = {}
        self.top = 0

    def add(self, name, cols, live):
        self.pending.append((name, cols, frozenset(live)))

    def finalize(self):
        for name, cols, live in sorted(self.pending,
                                       key=lambda t: -t[1]):
            spans = sorted((o, o + cc) for (o, cc, lv) in
                           self.regions.values() if lv & live)
            off = 0
            for (s, e) in spans:
                if off + cols <= s:
                    break
                off = max(off, e)
            self.regions[name] = (off, cols, live)
            self.top = max(self.top, off + cols)

    def view(self, tensor, name, rows=128):
        off, cols, _ = self.regions[name]
        return tensor[0:rows, off:off + cols]

    def view2(self, tensor, name, r0, r1):
        off, cols, _ = self.regions[name]
        return tensor[r0:r1, off:off + cols]


def _hadamard(n):
    h = np.array([[1.0]], dtype=np.float32)
    while h.shape[0] < n:
        h = np.block([[h, h], [h, -h]])
    return h.astype(np.float32)


def _rope_tables(core_id):
    freqs = 10000.0 ** (-np.arange(0, HD, 2, dtype=np.float64) / HD)
    t0 = (core_id % 4) * NTOK
    lpos = t0 + np.arange(NTOK)
    ang = (lpos - 1.0)[:, None] * freqs[None, :]
    cos = np.repeat(np.cos(ang), 2, axis=1)
    sin = np.repeat(np.sin(ang), 2, axis=1)
    z = lpos == 0
    cos[z, :] = 1.0
    sin[z, :] = 0.0
    sinpp = sin.copy()
    sinpp[:, 1::2] *= -1.0
    cosT = np.tile(cos.T, (2, 1)).astype(np.float32)
    sinT = np.tile(sinpp.T, (2, 1)).astype(np.float32)
    return np.ascontiguousarray(cosT), np.ascontiguousarray(sinT)


def build_program(debug=False, stop=10, kstop=4, p1n=NTC, router=3):
    nc = bass.Bass("TRN2", target_bir_lowering=False)

    def din(name, shape, dt=F32):
        return nc.dram_tensor(name, shape, dt, kind="ExternalInput").ap()

    def dnout(name, shape, dt=F32):
        return nc.dram_tensor(name, shape, dt, kind="ExternalOutput").ap()

    x_in = din("x_slice", [NTOK, E])
    wqkvT_in = din("wqkvT", [E, 3 * E])
    woutT_in = din("woutT", [E, E])
    w1_in = din("w1_0", [E, FF])
    w2_in = din("w2_0", [FF, E])
    b1c_in = din("b1c", [128, 32])
    b2row_in = din("b2row", [1, E])
    grow_in = din("grow", [1, E])
    brow_in = din("brow", [1, E])
    gt_in = din("g_tiled", [64, 4096])
    s1_in = din("s1_tiled", [64, 4096])
    s2_in = din("s2_tiled", [64, 4096])
    s3_in = din("s3_tiled", [64, 4096])
    ht_in = din("h_tiled", [64, 4096], BF16)
    hmat_in = din("hmat", [64, 64], BF16)
    hs_in = din("hs", [64, 64], BF16)
    swap_in = din("swapm", [64, 64], BF16)
    i128f_in = din("i128f", [128, 128])
    i128b_in = din("i128b", [128, 128], BF16)
    repl_in = din("repl", [16, 1024], BF16)
    nh2_in = din("neghalf2", [128, 2], BF16)
    ones1r_in = din("ones1r", [1, 128])
    actc_in = din("actconsts", [128, 4])
    rwT_in = din("rwT", [E, 16])
    rb_in = din("rbcol", [16, 1])
    cosT_in = din("cosT", [128, NTOK])
    sinT_in = din("sinT", [128, NTOK])

    y_out = dnout("y_out", [NTOK, E])

    x1_dram = nc.dram_tensor("x1_dram", [NTOK, E], F32).ap()
    ar_in = nc.dram_tensor("ar_in", [65, NH * 256], F32).ap()
    ar_out = nc.dram_tensor("ar_out", [65, NH * 256], F32).ap()

    dbg = {}
    if debug:
        for nm, shape, dt in [
            ("d_u1T", [128, 8 * 2048], BF16), ("d_v", [128, NTC * 1040], BF16),
            ("d_spre", [65, 4096], F32), ("d_ssb", [128, 2080], BF16),
            ("d_nsqk", [128, 256], F32), ("d_den", [16, 2048], F32),
            ("d_f", [16, 2048], F32), ("d_recip", [16, 2048], BF16),
            ("d_oh", [128, 16384], BF16), ("d_u2T", [128, 16384], BF16),
            ("d_kc", [128, 2048], BF16), ("d_nsqq", [128, 256], F32),
            ("d_x1", [NTOK, E], F32),
            ("d_xr", [128, 1024], F32), ("d_xsq", [128, 1024], F32),
            ("d_cols", [128, 32], F32), ("d_u1r", [128, 1024], BF16),
        ]:
            dbg[nm] = dnout(nm, shape, dt)

    ab = Arena(2)
    af = Arena(4)

    ab.add("u1T", 8 * 2048, {1, 2, 3, 4})
    ab.add("wqkv_kv", 8 * 2048, {0, 1, 2})
    ab.add("wqkv_q", 8 * 1024, {0, 1, 2, 3, 4})
    ab.add("Wc", 4096, {0, 1, 2, 3, 4, 5, 6})
    ab.add("Ws", 4096, {0, 1, 2, 3, 4, 5, 6})
    ab.add("Xa", 4096, {0})
    ab.add("Xb", 4096, {0})
    ab.add("gt", 4096, {0})
    ab.add("s1t", 4096, {0})
    ab.add("s2t", 4096, {0})
    ab.add("s3t", 4096, {0})
    ab.add("htl", 4096, {0})
    ab.add("v_sb", NTC * 1040, {2})
    ab.add("kc", 2048, {2})
    ab.add("ks", 2048, {2})
    ab.add("ksq", 2048, {2})
    ab.add("kphi0", 256, {2})
    ab.add("kphi1", 256, {2})
    ab.add("spostb", 4096, {3, 4})
    ab.add("S_sb", 32 * 65, {3, 4, 5, 6})
    ab.add("sumk", 512, {3, 4, 5, 6})
    ab.add("qc", 8 * 2048, {3, 4, 5, 6})
    ab.add("qs", 8 * 2048, {3, 4, 5, 6})
    ab.add("qsq", 2048, {3, 4})
    ab.add("Qp0", 4096, {5, 6})
    ab.add("Qp1", 4096, {5, 6})
    ab.add("numer0", 2048, {6})
    ab.add("numer1", 2048, {6})
    ab.add("recipb", 2048, {5, 6})
    ab.add("outheads", 8 * 2048, {6, 7})
    ab.add("wout", 8 * 1024, {7})
    ab.add("u2T", 8 * 2048, {7, 8})
    ab.add("w2full", 32 * 1024, {7, 8})
    ab.add("w1r0", 8 * 512, {8})
    ab.add("w1r1", 8 * 512, {8})
    ab.add("h_sb", 32 * 512, {8})
    ab.add("rlog", 2048, {8})
    ab.add("u1r0", 1024, {1})
    ab.add("u1r1", 1024, {1})
    ab.add("u2r0", 1024, {7})
    ab.add("u2r1", 1024, {7})

    af.add("cosT", NTOK, {0, 1, 2, 3, 4})
    af.add("sinT", NTOK, {0, 1, 2, 3, 4})
    af.add("gam_bc", 1024, set(range(9)))
    af.add("bet_bc", 1024, set(range(9)))
    af.add("grow", 1024, {0})
    af.add("brow", 1024, {0})
    af.add("b2row", 1024, set(range(9)))
    af.add("b1c", 32, set(range(9)))
    af.add("i128f", 128, set(range(9)))
    af.add("xr0", 1024, {1, 7})
    af.add("xr1", 1024, {1, 7})
    af.add("x1t0", 1024, {7})
    af.add("x1t1", 1024, {7})
    af.add("xsq", 1024, {1, 7, 8})
    af.add("cols0", 32, {1, 7, 8})
    af.add("cols1", 32, {1, 7, 8})
    af.add("ln2cols", 32, {7, 8})
    af.add("nsqk", 256, {2})
    af.add("nsqq", 256, {3, 4})
    af.add("ftm0", 16, {3, 4})
    af.add("ftm1", 16, {3, 4})
    af.add("fcl0", 16, {3, 4})
    af.add("fcl1", 16, {3, 4})
    af.add("f_fm", 2048, {3, 4, 5})
    af.add("spre", 4096, {2, 3})
    af.add("den", 2048, {5, 6} if debug else {5})
    af.add("dmx", 2048, {5, 6} if debug else {5})
    af.add("x1r0", 1024, {8})
    af.add("x1r1", 1024, {8})
    af.add("ytile", 1024, {8})
    af.add("yn", 1024, {8})
    af.add("t2c", 1024, {8})
    af.add("x1b", 1024, {8})

    ab.finalize()
    af.finalize()
    cap = 206 * 1024
    need = ab.top * 2 + af.top * 4
    assert need <= cap, (f"SBUF over budget: {need} > {cap} "
                         f"(bf {ab.top * 2}, f32 {af.top * 4})")

    A_BF = nc.alloc_sbuf_tensor("arena_bf", [128, ab.top], BF16)
    A_F = nc.alloc_sbuf_tensor("arena_f32", [128, af.top], F32)

    def rbv(name, rows=128):
        return ab.view(A_BF, name, rows)

    def rfv(name, rows=128):
        return af.view(A_F, name, rows)

    hmat_sb = nc.alloc_sbuf_tensor("hmat_sb", [64, 64], BF16)
    hs_sb = nc.alloc_sbuf_tensor("hs_sb", [64, 64], BF16)
    swap_sb = nc.alloc_sbuf_tensor("swap_sb", [64, 64], BF16)
    i128b_sb = nc.alloc_sbuf_tensor("i128b_sb", [128, 128], BF16)
    repl_sb = nc.alloc_sbuf_tensor("repl_sb", [16, 1024], BF16)
    nh2_sb = nc.alloc_sbuf_tensor("nh2_sb", [128, 2], BF16)
    ones1r_sb = nc.alloc_sbuf_tensor("ones1r_sb", [1, 128], F32)
    actc_sb = nc.alloc_sbuf_tensor("actc_sb", [128, 4], F32)
    rwT_sb = nc.alloc_sbuf_tensor("rwT_sb", [128, 128], BF16)
    rb_sb = nc.alloc_sbuf_tensor("rb_sb", [16, 1], F32)
    rlog_sb = ab.view(A_BF, "rlog", 16)

    PB = [nc.alloc_psum_tensor(f"pb{i}", [128, 512], F32) for i in range(6)]
    PT = [nc.alloc_psum_tensor(f"pt{i}", [128, 512], BF16) for i in range(2)]
    assert nc.psum_banks_remaining >= 0

    ctx = Ctx()

    def program(c):
        sp, gp, pe, act, dve = c.sp, c.gp, c.pe, c.act, c.dve
        ln_prev = []

        for d, s in [
            (rfv("cosT"), cosT_in), (rfv("sinT"), sinT_in),
            (rfv("grow", 1), grow_in), (rfv("brow", 1), brow_in),
            (rfv("b2row", 1), b2row_in), (rfv("b1c"), b1c_in),
            (rfv("i128f"), i128f_in),
            (hmat_sb[:, :], hmat_in), (hs_sb[:, :], hs_in),
            (swap_sb[:, :], swap_in), (i128b_sb[:, :], i128b_in),
            (repl_sb[:, :], repl_in), (nh2_sb[:, :], nh2_in),
            (ones1r_sb[:, :], ones1r_in),
            (actc_sb[:, :], actc_in),
            (rb_sb[:, :], rb_in),
            (rbv("htl", 64), ht_in),
        ]:
            sp.dma_start(out=d, in_=s).inc("ld", 16)
        c.mark("consts_loaded", "ld")

        for nm, src in [("gt", gt_in), ("s1t", s1_in),
                        ("s2t", s2_in), ("s3t", s3_in)]:
            gp.dma_start(out=rbv(nm, 64), in_=src).inc("wd", 16)
        rwv = rwT_in.rearrange("(cp p) e -> p cp e", p=128)
        gp.dma_start(out=rwT_sb[:, :].rearrange("p (cp e) -> p cp e", e=16),
                     in_=rwv).inc("wd", 16)
        c.mark("tiled_loaded", "wd")
        kvv = wqkvT_in[:, 1024:3072].rearrange("(c p) o -> p c o", p=128)
        kvd = rbv("wqkv_kv").rearrange("p (c o) -> p c o", o=2048)
        gp.dma_start(out=kvd, in_=kvv).inc("wd", 16)
        c.mark("wqkv_kv_loaded", "wd")
        qv = wqkvT_in[:, 0:1024].rearrange("(c p) o -> p c o", p=128)
        qd = rbv("wqkv_q").rearrange("p (c o) -> p c o", o=1024)
        gp.dma_start(out=qd, in_=qv).inc("wd", 16)
        c.mark("wqkv_q_loaded", "wd")

        pe.wait("consts_loaded")
        for i, rowname in enumerate(["grow", "brow"]):
            for half in range(2):
                k = i * 2 + half
                if k >= 2:
                    pe.wait(f"gb_ev_{k - 2}")
                pe.matmul(PB[2 + (k % 2)][:, :], ones1r_sb[:, :],
                          rfv(rowname, 1)[:, half * 512:(half + 1) * 512],
                          start=True, stop=True).inc("pe", 1)
                c.mark(f"gb_mm_{k}", "pe")
                dve.wait(f"gb_mm_{k}")
                dst = rfv("gam_bc" if i == 0 else "bet_bc")
                dve.tensor_copy(dst[:, half * 512:(half + 1) * 512],
                                PB[2 + (k % 2)][:, :]).inc("dve", 1)
                c.mark(f"gb_ev_{k}", "dve")
        c.mark("gb_done", "dve")

        dve.wait("tiled_loaded")
        dve.wait("consts_loaded")
        dve.tensor_mul(rbv("Xa", 64), rbv("htl", 64),
                       rbv("gt", 64)).fence("dve")
        dve.tensor_mul(rbv("Xb", 64), rbv("Xa", 64), rbv("s3t", 64)).inc("dve", 1)
        c.mark("wpre_x2", "dve")

        def wpre_stage(tag, prev_done, lhsT, src_name, scale_name, dst_name):
            for j in range(8):
                pe.wait(prev_done if j < 2 else f"{tag}_ev_{j - 2}")
                pe.matmul(PB[j % 2][0:64, :], lhsT,
                          rbv(src_name, 64)[:, j * 512:(j + 1) * 512],
                          start=True, stop=True).inc("pe", 1)
                c.mark(f"{tag}_mm_{j}", "pe")
                dve.wait(f"{tag}_mm_{j}")
                if scale_name is None:
                    dve.tensor_copy(rbv(dst_name, 64)[:, j * 512:(j + 1) * 512],
                                    PB[j % 2][0:64, :]).inc("dve", 1)
                else:
                    dve.tensor_mul(rbv(dst_name, 64)[:, j * 512:(j + 1) * 512],
                                   PB[j % 2][0:64, :],
                                   rbv(scale_name, 64)[:, j * 512:(j + 1) * 512]
                                   ).inc("dve", 1)
                c.mark(f"{tag}_ev_{j}", "dve")
            c.mark(f"{tag}_done", "dve")

        wpre_stage("wx3", "wpre_x2", hmat_sb[:, :], "Xb", "s2t", "Xa")
        wpre_stage("wx5", "wx3_done", hs_sb[:, :], "Xa", "s1t", "Wc")
        wpre_stage("wx7", "wx5_done", swap_sb[:, :], "Wc", None, "Ws")
        c.mark("wpre_done", "dve")
        sp.wait("wpre_done")
        sp.dma_start(out=ab.view2(A_BF, "Wc", 64, 128),
                     in_=rbv("Wc", 64)).inc("ld", 16)
        sp.dma_start(out=ab.view2(A_BF, "Ws", 64, 128),
                     in_=rbv("Ws", 64)).inc("ld", 16)
        c.mark("wrep_done", "ld")

        def ln_tile(tag, x_ap, out_ap, parity=0, save_cols=None):
            cols = rfv("cols0" if parity == 0 else "cols1")
            if len(ln_prev) >= 2:
                act.wait(f"lnout_{ln_prev[-2]}")
            act.activation(rfv("xsq"), x_ap, AF.Square,
                           accum_out=cols[:, 0:1]).inc("act", 1)
            c.mark(f"lnsq_{tag}", "act")
            dve.wait(f"lnsq_{tag}")
            dve.reduce_sum(cols[:, 4:5], x_ap, axis=AX.X).inc("dve", 1)
            c.mark(f"lnred_{tag}", "dve")
            act.wait(f"lnred_{tag}")
            act.mul(cols[:, 12:13], cols[:, 4:5], -1.0 / 1024.0)
            act.mul(cols[:, 8:9], cols[:, 4:5], 1.0 / 1024.0).fence("act")
            act.activation(cols[:, 16:17], cols[:, 8:9],
                           AF.Square).inc("act", 1)
            c.mark(f"lnmid_{tag}", "act")
            dve.wait(f"lnmid_{tag}")
            dve.scalar_tensor_tensor(cols[:, 20:21], cols[:, 0:1], 1.0 / 1024.0,
                                     cols[:, 16:17], ALU.mult,
                                     ALU.subtract).inc("dve", 1)
            c.mark(f"lnvar_{tag}", "dve")
            act.wait(f"lnvar_{tag}")
            act.activation(cols[:, 24:25], cols[:, 20:21], AF.Ln,
                           bias=actc_sb[:, 0:1]).fence("act")
            act.activation(cols[:, 28:29], cols[:, 24:25], AF.Exp,
                           scale=actc_sb[:, 1:2]).inc("act", 1)
            c.mark(f"lnrstd_{tag}", "act")
            dve.wait(f"lnrstd_{tag}")
            if save_cols is not None:
                dve.tensor_copy(save_cols[:, 0:1], cols[:, 12:13])
                dve.tensor_copy(save_cols[:, 1:2], cols[:, 28:29])
            dve.tensor_scalar(out_ap, x_ap, cols[:, 12:13], cols[:, 28:29],
                              ALU.add, ALU.mult).inc("dve", 1)
            c.mark(f"lnout_{tag}", "dve")
            ln_prev.append(tag)

        def u_transpose(tag, u_tm, uT3, tc):
            for bat in range(2):
                gidx = tc * 2 + bat
                pe.wait(f"lnout_{tag}_{tc}")
                if gidx >= 2:
                    pe.wait(f"{tag}_trev_{gidx - 2}")
                for dc in range(4):
                    d = bat * 4 + dc
                    pe.transpose(PT[gidx % 2][:, dc * 128:(dc + 1) * 128],
                                 u_tm[:, d * 128:(d + 1) * 128],
                                 i128b_sb[:, :]).inc("pe", 1)
                c.mark(f"{tag}_tr_{gidx}", "pe")
                dve.wait(f"{tag}_tr_{gidx}")
                src = PT[gidx % 2][:, :].rearrange("p (d t) -> p d t", t=128)
                dve.tensor_copy(uT3[:, bat * 4:(bat + 1) * 4,
                                    tc * 128:(tc + 1) * 128],
                                src).inc("dve", 1)
                c.mark(f"{tag}_trev_{gidx}", "dve")

        if stop >= 1:
            u1T3 = rbv("u1T").rearrange("p (d t) -> p d t", t=2048)
            for tc in range(p1n):
                xr = rfv("xr0") if tc % 2 == 0 else rfv("xr1")
                if tc >= 2:
                    sp.wait(f"lnout_ln1_{tc - 2}")
                sp.dma_start(out=xr, in_=x_in[tc * 128:(tc + 1) * 128, :]).inc("ld", 16)
                c.mark(f"x_tile_{tc}", "ld")
                act.wait(f"x_tile_{tc}")
                dve.wait(f"x_tile_{tc}")
                if tc >= 2:
                    dve.wait(f"ln1_tr_{(tc - 2) * 2 + 1}")
                u1r = rbv("u1r0") if tc % 2 == 0 else rbv("u1r1")
                ln_tile(f"ln1_{tc}", xr, u1r, parity=tc % 2)
                u_transpose("ln1", u1r, u1T3, tc)
            c.mark("u1T_done", "dve")
            if debug:
                sp.wait("u1T_done")
                sp.dma_start(out=dbg["d_u1T"], in_=rbv("u1T")).inc("st", 16)
                if p1n <= 2:
                    sp.dma_start(out=dbg["d_xr"], in_=rfv("xr0")).inc("st", 16)
                    sp.dma_start(out=dbg["d_xsq"], in_=rfv("xsq")).inc("st", 16)
                    sp.dma_start(out=dbg["d_cols"], in_=rfv("cols")).inc("st", 16)
                    sp.dma_start(out=dbg["d_u1r"], in_=rbv("u1r0")).inc("st", 16)
                c.mark("dbg_u1t", "st")

        if stop >= 2:
            v_sb = rbv("v_sb")
            v4 = v_sb.rearrange("p (t h d) -> p t h d", h=16, d=65)
            dve.memset(v4[:, :, :, 64:65], 1.0).inc("dve", 1)
            c.mark("v_ones", "dve")
            u1Tv = u1T3
            pe.wait("u1T_done")
            pe.wait("wqkv_kv_loaded")
            for tc in range(NTC):
                for half in range(2):
                    i = tc * 2 + half
                    if i >= 2:
                        pe.wait(f"v_ev_{i - 2}")
                    for dc in range(8):
                        pe.matmul(PB[i % 2][:, :],
                                  u1Tv[:, dc, tc * 128:(tc + 1) * 128],
                                  rbv("wqkv_kv")[:, dc * 2048 + 1024 + half * 512:
                                                 dc * 2048 + 1024 + (half + 1) * 512],
                                  start=(dc == 0), stop=(dc == 7)).inc("pe", 1)
                    c.mark(f"v_mm_{i}", "pe")
                    dve.wait(f"v_mm_{i}")
                    dve.wait("v_ones")
                    dst = v_sb[:, tc * 1040 + half * 520:
                               tc * 1040 + (half + 1) * 520]
                    dst3 = dst.rearrange("p (h d) -> p h d", d=65)[:, :, 0:64]
                    src3 = PB[i % 2][:, :].rearrange("p (h d) -> p h d", d=64)
                    dve.tensor_copy(dst3, src3).inc("dve", 1)
                    c.mark(f"v_ev_{i}", "dve")
            c.mark("v_done", "dve")

        if stop >= 3:
            kc, ks, ksq = rbv("kc"), rbv("ks"), rbv("ksq")
            nsqk = rfv("nsqk")
            nsqk3 = nsqk.rearrange("p (t h) -> p t h", h=16)
            for cc in range(8):
                for nt in range(4):
                    i = cc * 4 + nt
                    if i >= 2:
                        pe.wait(f"k_rope_{i - 2}")
                        pe.wait(f"k_sq_{i - 2}")
                    else:
                        pe.wait(f"v_ev_{30 + i}")
                    for dc in range(8):
                        pe.matmul(PB[i % 2][:, :],
                                  rbv("wqkv_kv")[:, dc * 2048 + cc * 128:
                                                 dc * 2048 + (cc + 1) * 128],
                                  u1Tv[:, dc, nt * 512:(nt + 1) * 512],
                                  start=(dc == 0), stop=(dc == 7)).inc("pe", 1)
                    c.mark(f"k_mm_{i}", "pe")
                    dve.wait(f"k_mm_{i}")
                    if cc > 0 and nt == 0 and kstop >= 4:
                        dve.wait(f"s_acc_{cc - 1}")
                    elif cc > 0 and nt == 0 and kstop == 3:
                        dve.wait(f"k_phi_{cc - 1}_31")
                    dve.tensor_mul(kc[:, nt * 512:(nt + 1) * 512], PB[i % 2][:, :],
                                   rfv("cosT")[:, nt * 512:(nt + 1) * 512])
                    dve.tensor_mul(ks[:, nt * 512:(nt + 1) * 512], PB[i % 2][:, :],
                                   rfv("sinT")[:, nt * 512:(nt + 1) * 512]
                                   ).inc("dve", 1)
                    c.mark(f"k_rope_{i}", "dve")
                    act.wait(f"k_mm_{i}")
                    act.wait(f"k_rope_{i}")
                    if cc > 0 and nt == 0 and kstop >= 2:
                        act.wait(f"nsq_mm_k_{cc - 1}")
                    act.activation(ksq[:, nt * 512:(nt + 1) * 512], PB[i % 2][:, :],
                                   AF.Square).inc("act", 1)
                    c.mark(f"k_sq_{i}", "act")
                if kstop >= 2:
                    pe.wait(f"k_sq_{cc * 4 + 3}")
                    if cc > 0:
                        pe.wait(f"nsq_ev_k_{cc - 1}")
                    for tc in range(NTC):
                        pe.matmul(PB[4][:, 2 * tc:2 * tc + 2],
                                  ksq[:, tc * 128:(tc + 1) * 128], nh2_sb[:, :],
                                  start=True, stop=True).inc("pe", 1)
                    c.mark(f"nsq_mm_k_{cc}", "pe")
                    dve.wait(f"nsq_mm_k_{cc}")
                    dve.tensor_copy(
                        nsqk3[:, :, 2 * cc:2 * cc + 2],
                        PB[4][:, 0:32].rearrange("p (t h) -> p t h", h=2)).inc("dve", 1)
                    c.mark(f"nsq_ev_k_{cc}", "dve")
                if cc == 0:
                    pe.wait("wrep_done")
                    pe.wait("v_done")
                def k_smm(jj):
                    tcj, h2j = jj // 2, jj % 2
                    hj = 2 * cc + h2j
                    pe.wait(f"k_exp_{cc}_{jj}")
                    if jj == 0 and cc > 0:
                        pe.wait(f"s_ev_{cc - 1}")
                    kphi_j = rbv("kphi0" if jj % 2 == 0 else "kphi1")
                    pe.matmul(PB[5][0:65, h2j * 256:(h2j + 1) * 256],
                              v_sb[:, tcj * 1040 + 65 * hj:
                                   tcj * 1040 + 65 * hj + 65],
                              kphi_j, start=(tcj == 0), stop=(tcj == 15),
                              skip_group_check=True).inc("pe", 1)
                    c.mark(f"k_smm_{cc}_{jj}", "pe")

                for tc in (range(NTC) if kstop >= 3 else []):
                    for h2 in range(2):
                        j = tc * 2 + h2
                        h = 2 * cc + h2
                        if j >= 2:
                            pe.wait(f"k_exp_{cc}_{j - 2}")
                        elif cc > 0:
                            pe.wait(f"k_exp_{cc - 1}_{30 + j}")
                        hb = h2 * 64
                        pe.matmul(PB[2 + (j % 2)][:, 0:256],
                                  kc[hb:hb + 64, tc * 128:(tc + 1) * 128],
                                  ab.view2(A_BF, "Wc", hb, hb + 64)
                                  [:, h * 256:(h + 1) * 256],
                                  start=True, stop=False).inc("pe", 1)
                        pe.matmul(PB[2 + (j % 2)][:, 0:256],
                                  ks[hb:hb + 64, tc * 128:(tc + 1) * 128],
                                  ab.view2(A_BF, "Ws", hb, hb + 64)
                                  [:, h * 256:(h + 1) * 256],
                                  start=False, stop=True).inc("pe", 1)
                        c.mark(f"k_phi_{cc}_{j}", "pe")
                        act.wait(f"k_phi_{cc}_{j}")
                        act.wait(f"nsq_ev_k_{cc}")
                        if kstop >= 4 and j >= 2:
                            act.wait(f"k_smm_{cc}_{j - 2}")
                        elif kstop >= 4 and cc > 0:
                            act.wait(f"k_smm_{cc - 1}_{30 + j}")
                        kphi = rbv("kphi0" if j % 2 == 0 else "kphi1")
                        act.activation(kphi, PB[2 + (j % 2)][:, 0:256], AF.Exp,
                                       bias=nsqk[:, tc * 16 + h:tc * 16 + h + 1]
                                       ).inc("act", 1)
                        c.mark(f"k_exp_{cc}_{j}", "act")
                        if kstop >= 4 and j >= 1:
                            k_smm(j - 1)
                if kstop >= 4:
                    k_smm(31)
                    c.mark(f"s_acc_{cc}", "pe")
                    dve.wait(f"s_acc_{cc}")
                    dve.tensor_copy(rfv("spre", 65)[:, cc * 512:(cc + 1) * 512],
                                    PB[5][0:65, :]).inc("dve", 1)
                    c.mark(f"s_ev_{cc}", "dve")
            c.mark("spre_done", "dve")

        if stop >= 4:
            gp.wait("spre_done")
            gp.dma_start(out=ar_in, in_=rfv("spre", 65)).inc("wd", 16)
            c.mark("ar_staged", "wd")
            gp.wait("ar_staged")
            gp.collective_compute(
                "AllReduce", ALU.add,
                replica_groups=[[0, 1, 2, 3], [4, 5, 6, 7]],
                ins=[ar_in], outs=[ar_out]).inc("cc", 1)
            gp.wait_raw("cc", 1)
            gp.dma_start(out=rbv("spostb", 65), in_=ar_out).inc("wd", 16)
            c.mark("spost_loaded", "wd")
            if stop >= 9:
                gp.wait("oh_done")
                wov = woutT_in.rearrange("(c p) o -> p c o", p=128)
                gp.dma_start(out=rbv("wout").rearrange("p (c o) -> p c o",
                                                       o=1024),
                             in_=wov).inc("wd", 16)
                c.mark("wout_loaded", "wd")
            if stop >= 10:
                w2v = w2_in.rearrange("(c p) d -> p c d", p=128)
                gp.dma_start(out=rbv("w2full").rearrange("p (c d) -> p c d",
                                                         d=1024),
                             in_=w2v).inc("wd", 16)
                c.mark("w2_loaded", "wd")

        if stop >= 3:
            if debug:
                sp.wait("v_done")
                sp.dma_start(out=dbg["d_v"], in_=v_sb).inc("st", 16)
                c.mark("dbg_v", "st")
                sp.wait("spre_done")
                sp.dma_start(out=dbg["d_kc"], in_=kc).inc("st", 16)
                sp.dma_start(out=dbg["d_spre"], in_=rfv("spre", 65)).inc("st", 16)
                sp.dma_start(out=dbg["d_nsqk"], in_=nsqk).inc("st", 16)
                c.mark("dbg_p2", "st")

        if stop >= 5:
            qc3 = rbv("qc").rearrange("p (cc t) -> p cc t", t=2048)
            qs3 = rbv("qs").rearrange("p (cc t) -> p cc t", t=2048)
            qsq = rbv("qsq")
            nsqq = rfv("nsqq")
            nsqq3 = nsqq.rearrange("p (t h) -> p t h", h=16)
            pe.wait("wqkv_q_loaded")
            if debug:
                pe.wait("dbg_v")
                pe.wait("dbg_p2")
            for cc in range(8):
                for nt in range(4):
                    i = cc * 4 + nt
                    if i >= 2:
                        pe.wait(f"q_rope_{i - 2}")
                        pe.wait(f"q_sq_{i - 2}")
                    else:
                        pe.wait(f"k_rope_{30 + i}")
                        pe.wait(f"k_sq_{30 + i}")
                    for dc in range(8):
                        pe.matmul(PB[i % 2][:, :],
                                  rbv("wqkv_q")[:, dc * 1024 + cc * 128:
                                                dc * 1024 + (cc + 1) * 128],
                                  u1Tv[:, dc, nt * 512:(nt + 1) * 512],
                                  start=(dc == 0), stop=(dc == 7)).inc("pe", 1)
                    c.mark(f"q_mm_{i}", "pe")
                    dve.wait(f"q_mm_{i}")
                    dve.tensor_mul(qc3[:, cc, nt * 512:(nt + 1) * 512],
                                   PB[i % 2][:, :],
                                   rfv("cosT")[:, nt * 512:(nt + 1) * 512])
                    dve.tensor_mul(qs3[:, cc, nt * 512:(nt + 1) * 512],
                                   PB[i % 2][:, :],
                                   rfv("sinT")[:, nt * 512:(nt + 1) * 512]
                                   ).inc("dve", 1)
                    c.mark(f"q_rope_{i}", "dve")
                    act.wait(f"q_mm_{i}")
                    act.wait(f"q_rope_{i}")
                    if cc > 0 and nt == 0:
                        act.wait(f"nsq_mm_q_{cc - 1}")
                    act.activation(qsq[:, nt * 512:(nt + 1) * 512], PB[i % 2][:, :],
                                   AF.Square).inc("act", 1)
                    c.mark(f"q_sq_{i}", "act")
                pe.wait(f"q_sq_{cc * 4 + 3}")
                pe.wait(f"nsq_ev_q_{cc - 1}" if cc > 0 else "nsq_ev_k_7")
                for tc in range(NTC):
                    pe.matmul(PB[4][:, 2 * tc:2 * tc + 2],
                              qsq[:, tc * 128:(tc + 1) * 128], nh2_sb[:, :],
                              start=True, stop=True).inc("pe", 1)
                c.mark(f"nsq_mm_q_{cc}", "pe")
                dve.wait(f"nsq_mm_q_{cc}")
                dve.tensor_copy(
                    nsqq3[:, :, 2 * cc:2 * cc + 2],
                    PB[4][:, 0:32].rearrange("p (t h) -> p t h", h=2)).inc("dve", 1)
                c.mark(f"nsq_ev_q_{cc}", "dve")
            c.mark("qside_done", "dve")

            act.wait("qside_done")
            dve.wait("qside_done")
            for tc in range(NTC):
                ftm = rfv("ftm0") if tc % 2 == 0 else rfv("ftm1")
                fcl = rfv("fcl0") if tc % 2 == 0 else rfv("fcl1")
                if tc >= 2:
                    dve.wait(f"f_tm_{tc - 2}")
                dve.tensor_scalar(fcl, nsqq[:, tc * 16:(tc + 1) * 16],
                                  -1.0, 85.0, ALU.mult,
                                  ALU.min).inc("dve", 1)
                c.mark(f"f_cl_{tc}", "dve")
                act.wait(f"f_cl_{tc}")
                if tc >= 2:
                    act.wait(f"f_tr_{tc - 2}")
                act.activation(ftm, fcl, AF.Exp,
                               bias=actc_sb[:, 3:4]).inc("act", 1)
                c.mark(f"f_tm_{tc}", "act")
                pe.wait(f"f_tm_{tc}")
                if tc >= 2:
                    pe.wait(f"f_ev_{tc - 2}")
                else:
                    pe.wait("nsq_ev_q_7")
                    pe.wait("spre_done")
                pe.transpose(PB[4 + (tc % 2)][0:16, 0:128], ftm,
                             rfv("i128f")).inc("pe", 1)
                c.mark(f"f_tr_{tc}", "pe")
                dve.wait(f"f_tr_{tc}")
                if tc < 2:
                    dve.wait("ar_staged")
                dve.tensor_copy(rfv("f_fm", 16)[:, tc * 128:(tc + 1) * 128],
                                PB[4 + (tc % 2)][0:16, 0:128]).inc("dve", 1)
                c.mark(f"f_ev_{tc}", "dve")
            c.mark("f_done", "dve")

        if stop >= 6:
            pe.wait("spost_loaded")
            pe.wait("u1T_done")
            for h in range(NH):
                for half in range(2):
                    i = h * 2 + half
                    if i >= 2:
                        pe.wait(f"spost_ev_{i - 2}")
                    pe.transpose(PT[i % 2][:, 0:65],
                                 rbv("spostb", 65)[:, h * 256 + half * 128:
                                                   h * 256 + (half + 1) * 128],
                                 i128b_sb[0:65, 0:65]).inc("pe", 1)
                    c.mark(f"spost_tr_{i}", "pe")
                    dve.wait(f"spost_tr_{i}")
                    dve.tensor_copy(rbv("S_sb")[:, i * 65:(i + 1) * 65],
                                    PT[i % 2][:, 0:65]).inc("dve", 1)
                    c.mark(f"spost_ev_{i}", "dve")
            dve.wait("spost_ev_31")
            dve.memset(rbv("sumk"), 0.0).fence("dve")
            for h in range(NH):
                for half in range(2):
                    i = h * 2 + half
                    ins = dve.tensor_copy(rbv("sumk")[:, i * 16 + h:i * 16 + h + 1],
                                          rbv("S_sb")[:, i * 65 + 64:i * 65 + 65])
                    if i == 31:
                        ins.inc("dve", 1)
            c.mark("sumk_done", "dve")
            if debug:
                sp.wait("sumk_done")
                sp.dma_start(out=dbg["d_ssb"], in_=rbv("S_sb")).inc("st", 16)
                sp.dma_start(out=dbg["d_nsqq"], in_=nsqq).inc("st", 16)
                sp.wait("f_done")
                sp.dma_start(out=dbg["d_f"], in_=rfv("f_fm", 16)).inc("st", 16)
                c.mark("dbg_s", "st")

        if stop >= 7:
            def phi_q_tile(tag, h, mc, nt, jg):
                cchunk, hr = h // 2, (h % 2) * 64
                if jg >= 2:
                    pe.wait(f"{tag}_exp_{jg - 2}")
                pe.matmul(PB[jg % 2][:, :],
                          ab.view2(A_BF, "Wc", hr, hr + 64)
                          [:, h * 256 + mc * 128:h * 256 + (mc + 1) * 128],
                          qc3[hr:hr + 64, cchunk, nt * 512:(nt + 1) * 512],
                          start=True, stop=False).inc("pe", 1)
                pe.matmul(PB[jg % 2][:, :],
                          ab.view2(A_BF, "Ws", hr, hr + 64)
                          [:, h * 256 + mc * 128:h * 256 + (mc + 1) * 128],
                          qs3[hr:hr + 64, cchunk, nt * 512:(nt + 1) * 512],
                          start=False, stop=True).inc("pe", 1)
                c.mark(f"{tag}_phi_{jg}", "pe")
                act.wait(f"{tag}_phi_{jg}")
                qp = rbv("Qp0" if h % 2 == 0 else "Qp1")
                act.activation(qp[:, mc * 2048 + nt * 512:mc * 2048 + (nt + 1) * 512],
                               PB[jg % 2][:, :], AF.Exp).inc("act", 1)
                c.mark(f"{tag}_exp_{jg}", "act")

            jg = 0
            pe.wait("qside_done")
            pe.wait("sumk_done")
            pe.wait("f_done")

            def a0_den(hh):
                qph = rbv("Qp0" if hh % 2 == 0 else "Qp1")
                pe.wait(f"a0_exp_{(hh + 1) * 8 - 1}")
                for mc in range(2):
                    for nt in range(4):
                        pe.matmul(PB[2 + nt][0:16, :],
                                  rbv("sumk")[:, (hh * 2 + mc) * 16:
                                              (hh * 2 + mc + 1) * 16],
                                  qph[:, mc * 2048 + nt * 512:
                                      mc * 2048 + (nt + 1) * 512],
                                  start=(hh == 0 and mc == 0),
                                  stop=(hh == NH - 1 and mc == 1),
                                  skip_group_check=True).inc("pe", 1)
                c.mark(f"a0_den_{hh}", "pe")

            for h in range(NH):
                if h >= 2:
                    act.wait(f"a0_den_{h - 2}")
                for mc in range(2):
                    for nt in range(4):
                        phi_q_tile("a0", h, mc, nt, jg)
                        jg += 1
                if h >= 1:
                    a0_den(h - 1)
            a0_den(NH - 1)
            c.mark("a0_done", "pe")
            dve.wait("a0_done")
            for nt in range(4):
                ins = dve.tensor_copy(
                    rfv("den", 16)[:, nt * 512:(nt + 1) * 512],
                    PB[2 + nt][0:16, :])
                ins.fence("dve") if nt == 3 else ins.inc("dve", 1)
            dve.tensor_max(rfv("dmx", 16), rfv("den", 16),
                           rfv("f_fm", 16)).inc("dve", 1)
            c.mark("recip_f1", "dve")
            dve.wait("recip_f1")
            dve.reciprocal(rfv("den", 16), rfv("dmx", 16)).inc("dve", 1)
            c.mark("recip_f2", "dve")
            dve.wait("recip_f2")
            dve.tensor_copy(rbv("recipb", 16), rfv("den", 16)).inc("dve", 1)
            c.mark("recip_done", "dve")
            if debug:
                sp.wait("recip_done")
                sp.dma_start(out=dbg["d_den"], in_=rfv("dmx", 16)).inc("st", 16)
                sp.dma_start(out=dbg["d_recip"], in_=rbv("recipb", 16)).inc("st", 16)
                c.mark("dbg_recip", "st")

        if stop >= 8:
            jg = 0

            def a1_tail(hh):
                qph = rbv("Qp0" if hh % 2 == 0 else "Qp1")
                numer = rbv("numer0" if hh % 2 == 0 else "numer1")
                pe.wait(f"a1_exp_{(hh + 1) * 8 - 1}")
                if hh == 0:
                    pe.wait("recip_done")
                for nt in range(4):
                    gidx = hh * 4 + nt
                    if nt >= 2:
                        pe.wait(f"a1_nev_{hh}_{nt - 2}")
                    elif hh >= 1:
                        pe.wait(f"a1_nev_{hh - 1}_{nt + 2}")
                    for half in range(2):
                        pe.matmul(PB[2 + (nt % 2)][0:65, :],
                                  rbv("S_sb")[:, (hh * 2 + half) * 65:
                                              (hh * 2 + half + 1) * 65],
                                  qph[:, half * 2048 + nt * 512:
                                      half * 2048 + (nt + 1) * 512],
                                  start=(half == 0),
                                  stop=(half == 1)).inc("pe", 1)
                    c.mark(f"a1_nmm_{hh}_{nt}", "pe")
                    act.wait(f"a1_nmm_{hh}_{nt}")
                    if hh >= 2:
                        act.wait(f"a1_oh_{(hh - 2) * 4 + nt}")
                    act.activation(numer[0:64, nt * 512:(nt + 1) * 512],
                                   PB[2 + (nt % 2)][0:64, :],
                                   AF.Copy).inc("act", 1)
                    c.mark(f"a1_nev_{hh}_{nt}", "act")
                    if gidx >= 2:
                        pe.wait(f"a1_oh_{gidx - 2}")
                    pe.matmul(PB[4 + (nt % 2)][0:64, :],
                              repl_sb[:, hh * 64:(hh + 1) * 64],
                              rbv("recipb", 16)[:, nt * 512:(nt + 1) * 512],
                              start=True, stop=True).inc("pe", 1)
                    c.mark(f"a1_bc_{hh}_{nt}", "pe")
                    dve.wait(f"a1_nev_{hh}_{nt}")
                    dve.wait(f"a1_bc_{hh}_{nt}")
                    hr = (hh % 2) * 64
                    dve.tensor_mul(
                        rbv("outheads")[hr:hr + 64,
                                        (hh // 2) * 2048 + nt * 512:
                                        (hh // 2) * 2048 + (nt + 1) * 512],
                        PB[4 + (nt % 2)][0:64, :],
                        numer[0:64, nt * 512:(nt + 1) * 512]).inc("dve", 1)
                    c.mark(f"a1_oh_{gidx}", "dve")

            for h in range(NH):
                if h >= 2:
                    act.wait(f"a1_nmm_{h - 2}_3")
                else:
                    act.wait(f"a0_den_{14 + h}")
                for mc in range(2):
                    for nt in range(4):
                        phi_q_tile("a1", h, mc, nt, jg)
                        jg += 1
                if h >= 1:
                    a1_tail(h - 1)
            a1_tail(NH - 1)
            c.mark("oh_done", "dve")
            if debug:
                sp.wait("oh_done")
                sp.dma_start(out=dbg["d_oh"], in_=rbv("outheads")).inc("st", 16)
                c.mark("dbg_oh", "st")

        if stop >= 9:
            oh3 = rbv("outheads").rearrange("p (e t) -> p e t", t=2048)
            u2T3 = rbv("u2T").rearrange("p (d t) -> p d t", t=2048)
            pe.wait("wout_loaded")
            for tc in range(NTC):
                xr = rfv("xr0") if tc % 2 == 0 else rfv("xr1")
                if tc >= 2:
                    sp.wait(f"x1_half_{(tc - 2) * 2 + 1}")
                else:
                    sp.wait(f"lnout_ln1_{14 + tc}")
                    sp.wait("oh_done")
                sp.dma_start(out=xr,
                             in_=x_in[tc * 128:(tc + 1) * 128, :]).inc("ld", 16)
                c.mark(f"x2_tile_{tc}", "ld")
                x1t = rfv("x1t0") if tc % 2 == 0 else rfv("x1t1")
                for ont in range(2):
                    j = tc * 2 + ont
                    if j == 0:
                        pe.wait("oh_done")
                    if j >= 2:
                        pe.wait(f"x1_half_{j - 2}")
                    for ec in range(8):
                        pe.matmul(PB[j % 2][:, :],
                                  oh3[:, ec, tc * 128:(tc + 1) * 128],
                                  rbv("wout")[:, ec * 1024 + ont * 512:
                                              ec * 1024 + (ont + 1) * 512],
                                  start=(ec == 0), stop=(ec == 7)).inc("pe", 1)
                    c.mark(f"att_mm_{j}", "pe")
                    dve.wait(f"att_mm_{j}")
                    dve.wait(f"x2_tile_{tc}")
                    if tc >= 2 and ont == 0:
                        dve.wait(f"x1_spill_{tc - 2}")
                        dve.wait(f"ln2_tr_{(tc - 2) * 2 + 1}")
                    dve.tensor_add(x1t[:, ont * 512:(ont + 1) * 512],
                                   PB[j % 2][:, :],
                                   xr[:, ont * 512:(ont + 1) * 512]).inc("dve", 1)
                    c.mark(f"x1_half_{j}", "dve")
                sp.wait(f"x1_half_{tc * 2 + 1}")
                sp.dma_start(out=x1_dram[tc * 128:(tc + 1) * 128, :],
                             in_=x1t).inc("ld", 16)
                c.mark(f"x1_spill_{tc}", "ld")
                act.wait(f"x1_half_{tc * 2 + 1}")
                u2r = rbv("u2r0") if tc % 2 == 0 else rbv("u2r1")
                if tc >= 2:
                    dve.wait(f"ln2_tr_{(tc - 2) * 2 + 1}")
                ln_tile(f"ln2_{tc}", x1t, u2r, parity=tc % 2,
                        save_cols=rfv("ln2cols")[:, 2 * tc:2 * tc + 2])
                u_transpose("ln2", u2r, u2T3, tc)
            c.mark("u2T_done", "dve")
            if debug:
                sp.wait("u2T_done")
                sp.dma_start(out=dbg["d_x1"], in_=x1_dram).inc("st", 16)
                sp.dma_start(out=dbg["d_u2T"], in_=rbv("u2T")).inc("st", 16)
                c.mark("dbg_u2", "st")

        if stop >= 10:
            if router >= 2:
                pe.wait("u2T_done")
                for nt in range(4):
                    if nt >= 1 and router >= 3:
                        pe.wait(f"rlog_ev_{nt - 1}")
                    for dc in range(8):
                        pe.matmul(PB[4][0:16, :],
                                  rwT_sb[:, dc * 16:(dc + 1) * 16],
                                  u2T3[:, dc, nt * 512:(nt + 1) * 512],
                                  start=(dc == 0), stop=(dc == 7),
                                  skip_group_check=True).inc("pe", 1)
                    c.mark(f"rlog_mm_{nt}", "pe")
                    if router >= 3:
                        dve.wait(f"rlog_mm_{nt}")
                        dve.tensor_scalar_add(
                            rlog_sb[:, nt * 512:(nt + 1) * 512],
                            PB[4][0:16, :], rb_sb[:, :]).inc("dve", 1)
                        c.mark(f"rlog_ev_{nt}", "dve")

            h3 = rbv("h_sb").rearrange("p (kf t) -> p kf t", t=512)
            for th in range(4):
                for fb in range(8):
                    gi = th * 8 + fb
                    w1r = rbv("w1r0" if gi % 2 == 0 else "w1r1")
                    if gi < 2:
                        gp.wait("u2T_done")
                    else:
                        gp.wait(f"h_mm_{(gi - 2) * 4 + 3}")
                    w1v = w1_in[:, fb * 512:(fb + 1) * 512].rearrange(
                        "(cp p) f -> p cp f", p=128)
                    gp.dma_start(out=w1r.rearrange("p (cp f) -> p cp f", f=512),
                                 in_=w1v).inc("wd", 16)
                    c.mark(f"w1_ld_{gi}", "wd")
                    for fc in range(4):
                        j = gi * 4 + fc
                        pe.wait(f"w1_ld_{gi}")
                        if j == 0:
                            pe.wait("u2T_done")
                            pe.wait("w2_loaded")
                        if j >= 2:
                            pe.wait(f"h_relu_{j - 2}")
                        for dc in range(8):
                            pe.matmul(PB[j % 2][:, :],
                                      w1r[:, dc * 512 + fc * 128:
                                          dc * 512 + (fc + 1) * 128],
                                      u2T3[:, dc, th * 512:(th + 1) * 512],
                                      start=(dc == 0), stop=(dc == 7)).inc("pe", 1)
                        c.mark(f"h_mm_{j}", "pe")
                        act.wait(f"h_mm_{j}")
                        kf = fb * 4 + fc
                        act.activation(h3[:, kf, :], PB[j % 2][:, :], AF.Relu,
                                       bias=rfv("b1c")[:, kf:kf + 1]).inc("act", 1)
                        c.mark(f"h_relu_{j}", "act")
                for ts in range(4):
                    tc = th * 4 + ts
                    x1r = rfv("x1r0") if tc % 2 == 0 else rfv("x1r1")
                    if tc >= 2:
                        sp.wait(f"fin_tile_{tc - 2}")
                    else:
                        sp.wait("u2T_done")
                    sp.wait(f"x1_spill_{tc}")
                    sp.dma_start(out=x1r, in_=x1_dram[tc * 128:(tc + 1) * 128, :]
                                 ).inc("ld", 16)
                    c.mark(f"x1_reload_{tc}", "ld")
                    for dnt in range(2):
                        j2 = tc * 2 + dnt
                        pe.wait(f"h_relu_{th * 32 + 31}")
                        if j2 >= 2:
                            pe.wait(f"y_half_{j2 - 2}")
                        for kf in range(32):
                            pe.matmul(PB[2 + (j2 % 2)][:, :],
                                      h3[:, kf, ts * 128:(ts + 1) * 128],
                                      rbv("w2full")[:, kf * 1024 + dnt * 512:
                                                    kf * 1024 + (dnt + 1) * 512],
                                      start=(kf == 0),
                                      stop=(kf == 31)).inc("pe", 1)
                        pe.matmul(PB[2 + (j2 % 2)][:, :], ones1r_sb[:, :],
                                  rfv("b2row", 1)[:, dnt * 512:(dnt + 1) * 512],
                                  start=False, stop=True,
                                  skip_group_check=True).inc("pe", 1)
                        c.mark(f"eo_mm_{j2}", "pe")
                    dve.wait(f"x1_reload_{tc}")
                    if tc >= 1:
                        dve.wait(f"fin_tile_{tc - 1}")
                    dve.tensor_scalar(rfv("x1b"), x1r,
                                      rfv("ln2cols")[:, 2 * tc:2 * tc + 1],
                                      rfv("ln2cols")[:, 2 * tc + 1:2 * tc + 2],
                                      ALU.add, ALU.mult).inc("dve", 1)
                    c.mark(f"u2rc_{tc}", "dve")
                    dve.wait(f"u2rc_{tc}")
                    for dnt in range(2):
                        j2 = tc * 2 + dnt
                        dve.wait(f"eo_mm_{j2}")
                        dve.tensor_add(rfv("ytile")[:, dnt * 512:(dnt + 1) * 512],
                                       PB[2 + (j2 % 2)][:, :],
                                       rfv("x1b")[:, dnt * 512:(dnt + 1) * 512]
                                       ).inc("dve", 1)
                        c.mark(f"y_half_{j2}", "dve")
                    act.wait(f"y_half_{tc * 2 + 1}")
                    ln_tile(f"ln3_{tc}", rfv("ytile"), rfv("yn"), parity=tc % 2)
                    dve.wait(f"lnout_ln3_{tc}")
                    dve.tensor_mul(rfv("t2c"), rfv("yn"), rfv("gam_bc"))
                    dve.tensor_add(rfv("x1b"), x1r,
                                   rfv("bet_bc")).fence("dve")
                    c.mark(f"fin_pre_{tc}", "dve")
                    dve.tensor_add(rfv("ytile"), rfv("t2c"), rfv("x1b")).inc("dve", 1)
                    c.mark(f"fin_dve_{tc}", "dve")
                    sp.wait(f"fin_dve_{tc}")
                    sp.dma_start(out=y_out[tc * 128:(tc + 1) * 128, :],
                                 in_=rfv("ytile")).inc("st", 16)
                    c.mark(f"fin_tile_{tc}", "st")

        if c.live == "count":
            c.marks["__st_total"] = ("st", c.counts.get("st", 0))
        sp.wait_raw("st", c.marks["__st_total"][1])

    program(ctx)
    with (
        nc.semaphore("s_ld") as s_ld,
        nc.semaphore("s_wd") as s_wd,
        nc.semaphore("s_pe") as s_pe,
        nc.semaphore("s_act") as s_act,
        nc.semaphore("s_dve") as s_dve,
        nc.semaphore("s_cc") as s_cc,
        nc.semaphore("s_st") as s_st,
        nc.Block() as block,
    ):
        ctx.semh = {"ld": s_ld, "wd": s_wd, "pe": s_pe, "act": s_act,
                    "dve": s_dve, "cc": s_cc, "st": s_st}

        @block.sync
        def _(eng):
            ctx.real["sp"] = eng
            ctx.begin("sp")
            program(ctx)
            ctx.finish()

        @block.gpsimd
        def _(eng):
            ctx.real["gp"] = eng
            ctx.begin("gp")
            program(ctx)
            ctx.finish()

        @block.tensor
        def _(eng):
            ctx.real["pe"] = eng
            ctx.begin("pe")
            program(ctx)
            ctx.finish()

        @block.scalar
        def _(eng):
            ctx.real["act"] = eng
            ctx.begin("act")
            program(ctx)
            ctx.finish()

        @block.vector
        def _(eng):
            ctx.real["dve"] = eng
            ctx.begin("dve")
            program(ctx)
            ctx.finish()

    return nc


def make_in_maps(inputs):
    x = np.asarray(inputs["L0_attn_x"], dtype=np.float32)
    w_qkv = np.asarray(inputs["L0_attn_w_qkv"], dtype=np.float32)
    w_out = np.asarray(inputs["L0_attn_w_out"], dtype=np.float32)
    signs = np.asarray(inputs["L0_attn_sorf_signs"], dtype=np.float32)
    scales = np.asarray(inputs["L0_attn_sorf_scales"], dtype=np.float32)
    w1 = np.asarray(inputs["L0_ffn_w1"], dtype=np.float32)
    b1 = np.asarray(inputs["L0_ffn_b1"], dtype=np.float32)
    w2 = np.asarray(inputs["L0_ffn_w2"], dtype=np.float32)
    b2 = np.asarray(inputs["L0_ffn_b2"], dtype=np.float32)
    gamma = np.asarray(inputs["L0_ffn_gamma"], dtype=np.float32)
    beta = np.asarray(inputs["L0_ffn_beta"], dtype=np.float32)

    hm = _hadamard(HD)
    swap = np.zeros((HD, HD), dtype=np.float32)
    for i in range(0, HD, 2):
        swap[i, i + 1] = 1.0
        swap[i + 1, i] = 1.0
    rep = np.zeros((NH, NH * HD), dtype=np.float32)
    for h in range(NH):
        rep[h, h * HD:(h + 1) * HD] = 1.0
    nh2 = np.zeros((128, 2), dtype=np.float32)
    nh2[0:64, 0] = -0.5
    nh2[64:128, 1] = -0.5

    sgn = signs.reshape(64, 3, 64)
    router_w = np.asarray(inputs["L0_ffn_router_w"], dtype=np.float32)
    router_b = np.asarray(inputs["L0_ffn_router_b"], dtype=np.float32)
    common = {
        "rwT": np.ascontiguousarray(np.concatenate(
            [router_w.T, np.zeros((E, 8), np.float32)], axis=1)),
        "rbcol": np.ascontiguousarray(np.concatenate(
            [router_b, np.zeros(8, np.float32)]).reshape(16, 1)),
        "wqkvT": np.ascontiguousarray(w_qkv.T),
        "woutT": np.ascontiguousarray(w_out.T),
        "w1_0": np.ascontiguousarray(w1[0]),
        "w2_0": np.ascontiguousarray(w2[0]),
        "b1c": np.ascontiguousarray(b1[0].reshape(32, 128).T),
        "b2row": np.ascontiguousarray(b2[0].reshape(1, E)),
        "grow": np.ascontiguousarray(gamma.reshape(1, E)),
        "brow": np.ascontiguousarray(beta.reshape(1, E)),
        "g_tiled": np.ascontiguousarray(
            np.tile(scales.reshape(1, 4096), (64, 1)).astype(np.float32)),
        "s1_tiled": np.ascontiguousarray(
            np.repeat(sgn[:, 0, :].T, 64, axis=1).astype(np.float32)),
        "s2_tiled": np.ascontiguousarray(
            np.repeat(sgn[:, 1, :].T, 64, axis=1).astype(np.float32)),
        "s3_tiled": np.ascontiguousarray(
            np.repeat(sgn[:, 2, :].T, 64, axis=1).astype(np.float32)),
        "h_tiled": np.ascontiguousarray(
            np.tile(hm, (1, 64)).astype(ml_dtypes.bfloat16)),
        "hmat": np.ascontiguousarray(hm.astype(ml_dtypes.bfloat16)),
        "hs": np.ascontiguousarray((hm / 4096.0).astype(ml_dtypes.bfloat16)),
        "swapm": np.ascontiguousarray(swap.astype(ml_dtypes.bfloat16)),
        "i128f": np.eye(128, dtype=np.float32),
        "actconsts": np.ascontiguousarray(np.tile(
            np.array([LN_EPS, -0.5, -1.0, LN_EPS_SC], dtype=np.float32),
            (128, 1))),
        "i128b": np.ascontiguousarray(
            np.eye(128, dtype=np.float32).astype(ml_dtypes.bfloat16)),
        "repl": np.ascontiguousarray(rep.astype(ml_dtypes.bfloat16)),
        "neghalf2": np.ascontiguousarray(nh2.astype(ml_dtypes.bfloat16)),
        "ones1r": np.ones((1, 128), dtype=np.float32),
    }

    in_maps = []
    for core in range(8):
        bb = core // 4
        t0 = (core % 4) * NTOK
        cosT, sinT = _rope_tables(core)
        m = dict(common)
        m["x_slice"] = np.ascontiguousarray(x[bb, t0:t0 + NTOK, :])
        m["cosT"] = cosT
        m["sinT"] = sinT
        in_maps.append(m)
    return in_maps


def kernel(**inputs):
    nc = build_program(debug=False)
    in_maps = make_in_maps(inputs)
    res = run_bass_kernel_spmd(nc, in_maps, core_ids=list(range(8)))
    out = np.empty((B, L, E), dtype=np.float32)
    for core in range(8):
        bb = core // 4
        t0 = (core % 4) * NTOK
        out[bb, t0:t0 + NTOK, :] = np.asarray(res.results[core]["y_out"],
                                              dtype=np.float32)
    return out


# version 08
# speedup vs baseline: 1.1998x; 1.0391x over previous
import math
import numpy as np
import ml_dtypes
import concourse.bass as bass
import concourse.mybir as mybir
from concourse.bass_utils import run_bass_kernel_spmd

F32 = mybir.dt.float32
BF16 = mybir.dt.bfloat16
AF = mybir.ActivationFunctionType
ALU = mybir.AluOpType
AX = mybir.AxisListType

B, L, E = 2, 8192, 1024
NTOK = 2048
NTC = 16
NH, HD = 16, 64
FF = 4096
LN_EPS = 1e-5
EPS_SC = float(np.float32(1e-6) * np.float32(256.0))
LN_EPS_SC = float(np.float32(np.log(np.float64(EPS_SC))))


class _Ins:
    __slots__ = ("ctx", "ins", "eng", "is_dma")

    def __init__(self, ctx, ins, eng=None, is_dma=False):
        self.ctx, self.ins = ctx, ins
        self.eng, self.is_dma = eng, is_dma

    def inc(self, semname, k=1):
        ctx = self.ctx
        ctx.counts[semname] = ctx.counts.get(semname, 0) + k
        if self.ins is not None:
            self.ins.then_inc(ctx.semh[semname], k)
        if self.is_dma:
            if ctx.live == self.eng:
                ctx.real[self.eng].wait_ge(ctx.semh[semname],
                                           ctx.counts[semname])
        return self

    def incn(self, semname, k=1):
        ctx = self.ctx
        ctx.counts[semname] = ctx.counts.get(semname, 0) + k
        if self.ins is not None:
            self.ins.then_inc(ctx.semh[semname], k)
        return self

    def fence(self, semname):
        ctx = self.ctx
        ctx.counts[semname] = ctx.counts.get(semname, 0) + 1
        if self.ins is not None:
            self.ins.then_inc(ctx.semh[semname], 1)
        if ctx.live == self.eng:
            ctx.real[self.eng].wait_ge(ctx.semh[semname], ctx.counts[semname])
        return self


class _Eng:
    def __init__(self, ctx, name):
        self._ctx, self._name = ctx, name

    def wait(self, markname, delta=0):
        ctx = self._ctx
        if ctx.live == "count":
            return
        sem, val = ctx.marks[markname]
        if ctx.live == self._name:
            ctx.real[self._name].wait_ge(ctx.semh[sem], val + delta)

    def wait_raw(self, semname, val):
        if self._ctx.live == self._name:
            self._ctx.real[self._name].wait_ge(self._ctx.semh[semname], val)

    def __getattr__(self, attr):
        ctx, name = self._ctx, self._name
        is_dma = attr == "dma_start"

        def call(*a, **k):
            if ctx.live == name:
                return _Ins(ctx, getattr(ctx.real[name], attr)(*a, **k),
                            name, is_dma)
            return _Ins(ctx, None, name, is_dma)

        return call


class Ctx:
    ENGINES = ("sp", "gp", "pe", "act", "dve")

    def __init__(self):
        self.marks = {}
        self.counts = {}
        self.live = "count"
        self.real = {}
        self.semh = {}
        self.first_counts = None
        for e in self.ENGINES:
            setattr(self, e, _Eng(self, e))

    def mark(self, markname, semname):
        v = self.counts.get(semname, 0)
        if self.live == "count":
            assert markname not in self.marks, f"dup mark {markname}"
            self.marks[markname] = (semname, v)
        else:
            assert self.marks[markname] == (semname, v), (
                markname, self.marks[markname], (semname, v))

    def begin(self, live):
        if self.live == "count":
            self.first_counts = dict(self.counts)
        self.live = live
        self.counts = {}

    def finish(self):
        assert self.counts == self.first_counts, (
            {k: (self.counts.get(k), self.first_counts.get(k))
             for k in set(self.counts) | set(self.first_counts)
             if self.counts.get(k) != self.first_counts.get(k)})


class Arena:
    def __init__(self, esize):
        self.esize = esize
        self.pending = []
        self.regions = {}
        self.top = 0

    def add(self, name, cols, live):
        self.pending.append((name, cols, frozenset(live)))

    def finalize(self):
        for name, cols, live in sorted(self.pending,
                                       key=lambda t: -t[1]):
            spans = sorted((o, o + cc) for (o, cc, lv) in
                           self.regions.values() if lv & live)
            off = 0
            for (s, e) in spans:
                if off + cols <= s:
                    break
                off = max(off, e)
            self.regions[name] = (off, cols, live)
            self.top = max(self.top, off + cols)

    def view(self, tensor, name, rows=128):
        off, cols, _ = self.regions[name]
        return tensor[0:rows, off:off + cols]

    def view2(self, tensor, name, r0, r1):
        off, cols, _ = self.regions[name]
        return tensor[r0:r1, off:off + cols]


def _hadamard(n):
    h = np.array([[1.0]], dtype=np.float32)
    while h.shape[0] < n:
        h = np.block([[h, h], [h, -h]])
    return h.astype(np.float32)


def _rope_tables(core_id):
    freqs = 10000.0 ** (-np.arange(0, HD, 2, dtype=np.float64) / HD)
    t0 = (core_id % 4) * NTOK
    lpos = t0 + np.arange(NTOK)
    ang = (lpos - 1.0)[:, None] * freqs[None, :]
    cos = np.repeat(np.cos(ang), 2, axis=1)
    sin = np.repeat(np.sin(ang), 2, axis=1)
    z = lpos == 0
    cos[z, :] = 1.0
    sin[z, :] = 0.0
    sinpp = sin.copy()
    sinpp[:, 1::2] *= -1.0
    cosT = np.tile(cos.T, (2, 1)).astype(np.float32)
    sinT = np.tile(sinpp.T, (2, 1)).astype(np.float32)
    return np.ascontiguousarray(cosT), np.ascontiguousarray(sinT)


def build_program(debug=False, stop=10, kstop=4, p1n=NTC, router=3):
    nc = bass.Bass("TRN2", target_bir_lowering=False)

    def din(name, shape, dt=F32):
        return nc.dram_tensor(name, shape, dt, kind="ExternalInput").ap()

    def dnout(name, shape, dt=F32):
        return nc.dram_tensor(name, shape, dt, kind="ExternalOutput").ap()

    x_in = din("x_slice", [NTOK, E])
    wqkvT_in = din("wqkvT", [E, 3 * E])
    woutT_in = din("woutT", [E, E])
    w1_in = din("w1_0", [E, FF])
    w2_in = din("w2_0", [FF, E])
    b1c_in = din("b1c", [128, 32])
    b2row_in = din("b2row", [1, E])
    grow_in = din("grow", [1, E])
    brow_in = din("brow", [1, E])
    gt_in = din("g_tiled", [64, 4096])
    s1_in = din("s1_tiled", [64, 4096])
    s2_in = din("s2_tiled", [64, 4096])
    s3_in = din("s3_tiled", [64, 4096])
    ht_in = din("h_tiled", [64, 4096], BF16)
    hmat_in = din("hmat", [64, 64], BF16)
    hs_in = din("hs", [64, 64], BF16)
    swap_in = din("swapm", [64, 64], BF16)
    i128f_in = din("i128f", [128, 128])
    i128b_in = din("i128b", [128, 128], BF16)
    repl_in = din("repl", [16, 1024], BF16)
    nh2_in = din("neghalf2", [128, 2], BF16)
    ones1r_in = din("ones1r", [1, 128])
    actc_in = din("actconsts", [128, 4])
    rwT_in = din("rwT", [E, 16])
    rb_in = din("rbcol", [16, 1])
    cosT_in = din("cosT", [128, NTOK])
    sinT_in = din("sinT", [128, NTOK])

    y_out = dnout("y_out", [NTOK, E])

    x1_dram = nc.dram_tensor("x1_dram", [NTOK, E], F32).ap()
    ar_in = nc.dram_tensor("ar_in", [65, NH * 256], BF16).ap()
    ar_out = nc.dram_tensor("ar_out", [65, NH * 256], BF16).ap()

    dbg = {}
    if debug:
        for nm, shape, dt in [
            ("d_u1T", [128, 8 * 2048], BF16), ("d_v", [128, NTC * 1040], BF16),
            ("d_spre", [65, 4096], BF16), ("d_ssb", [128, 2080], BF16),
            ("d_nsqk", [128, 256], F32), ("d_den", [16, 2048], F32),
            ("d_f", [16, 2048], F32), ("d_recip", [16, 2048], BF16),
            ("d_oh", [128, 16384], BF16), ("d_u2T", [128, 16384], BF16),
            ("d_kc", [128, 2048], BF16), ("d_nsqq", [128, 256], F32),
            ("d_x1", [NTOK, E], F32),
            ("d_xr", [128, 1024], F32), ("d_xsq", [128, 1024], F32),
            ("d_cols", [128, 32], F32), ("d_u1r", [128, 1024], BF16),
        ]:
            dbg[nm] = dnout(nm, shape, dt)

    ab = Arena(2)
    af = Arena(4)

    ab.add("u1T", 8 * 2048, {1, 2, 3, 4})
    ab.add("wqkv_kv", 8 * 2048, {0, 1, 2})
    ab.add("wqkv_q", 8 * 1024, {0, 1, 2, 3, 4})
    ab.add("Wc", 4096, {0, 1, 2, 3, 4, 5, 6})
    ab.add("Ws", 4096, {0, 1, 2, 3, 4, 5, 6})
    ab.add("Xa", 4096, {0})
    ab.add("Xb", 4096, {0})
    ab.add("gt", 4096, {0})
    ab.add("s1t", 4096, {0})
    ab.add("s2t", 4096, {0})
    ab.add("s3t", 4096, {0})
    ab.add("htl", 4096, {0})
    ab.add("v_sb", NTC * 1040, {2})
    ab.add("kc", 2048, {2})
    ab.add("ks", 2048, {2})
    ab.add("ksq", 2048, {2})
    ab.add("kphi0", 256, {2})
    ab.add("kphi1", 256, {2})
    ab.add("spostb", 4096, {3, 4})
    ab.add("spre", 4096, {2, 3})
    ab.add("S_sb", 32 * 65, {3, 4, 5, 6})
    ab.add("sumk", 512, {3, 4, 5, 6})
    ab.add("qc", 8 * 2048, {3, 4, 5, 6})
    ab.add("qs", 8 * 2048, {3, 4, 5, 6})
    ab.add("qsq", 2048, {3, 4})
    ab.add("Qp0", 4096, {5, 6})
    ab.add("Qp1", 4096, {5, 6})
    ab.add("numer0", 2048, {6})
    ab.add("numer1", 2048, {6})
    ab.add("recipb", 2048, {5, 6})
    ab.add("outheads", 8 * 2048, {6, 7})
    ab.add("wout", 8 * 1024, {7})
    ab.add("u2T", 8 * 2048, {7, 8})
    ab.add("w2full", 32 * 1024, {7, 8})
    ab.add("w1r0", 8 * 512, {8})
    ab.add("w1r1", 8 * 512, {8})
    ab.add("w1r2", 8 * 512, {8})
    ab.add("h_sb", 32 * 512, {8})
    ab.add("rlog", 2048, {8})
    ab.add("u1r0", 1024, {1})
    ab.add("u1r1", 1024, {1})
    ab.add("u2r0", 1024, {7})
    ab.add("u2r1", 1024, {7})

    af.add("cosT", NTOK, {0, 1, 2, 3, 4})
    af.add("sinT", NTOK, {0, 1, 2, 3, 4})
    af.add("gam_bc", 1024, set(range(9)))
    af.add("bet_bc", 1024, set(range(9)))
    af.add("grow", 1024, {0})
    af.add("brow", 1024, {0})
    af.add("b2row", 1024, set(range(9)))
    af.add("b1c", 32, set(range(9)))
    af.add("i128f", 128, set(range(9)))
    af.add("xr0", 1024, {1, 7})
    af.add("xr1", 1024, {1, 7})
    af.add("x1t0", 1024, {7})
    af.add("x1t1", 1024, {7})
    af.add("xsq", 1024, {1, 7, 8})
    af.add("cols0", 32, {1, 7, 8})
    af.add("cols1", 32, {1, 7, 8})
    af.add("ln2cols", 32, {7, 8})
    af.add("nsqk", 256, {2})
    af.add("nsqq", 256, {3, 4})
    af.add("ftm0", 16, {3, 4})
    af.add("ftm1", 16, {3, 4})
    af.add("fcl0", 16, {3, 4})
    af.add("fcl1", 16, {3, 4})
    af.add("f_fm", 2048, {3, 4, 5})

    af.add("den", 2048, {5, 6} if debug else {5})
    af.add("dmx", 2048, {5, 6} if debug else {5})
    af.add("x1r0", 1024, {8})
    af.add("x1r1", 1024, {8})
    af.add("ytile", 1024, {8})
    af.add("yn", 1024, {8})
    af.add("t2c", 1024, {8})
    af.add("x1b", 1024, {8})

    ab.finalize()
    af.finalize()
    cap = 206 * 1024
    need = ab.top * 2 + af.top * 4
    assert need <= cap, (f"SBUF over budget: {need} > {cap} "
                         f"(bf {ab.top * 2}, f32 {af.top * 4})")

    A_BF = nc.alloc_sbuf_tensor("arena_bf", [128, ab.top], BF16)
    A_F = nc.alloc_sbuf_tensor("arena_f32", [128, af.top], F32)

    def rbv(name, rows=128):
        return ab.view(A_BF, name, rows)

    def rfv(name, rows=128):
        return af.view(A_F, name, rows)

    hmat_sb = nc.alloc_sbuf_tensor("hmat_sb", [64, 64], BF16)
    hs_sb = nc.alloc_sbuf_tensor("hs_sb", [64, 64], BF16)
    swap_sb = nc.alloc_sbuf_tensor("swap_sb", [64, 64], BF16)
    i128b_sb = nc.alloc_sbuf_tensor("i128b_sb", [128, 128], BF16)
    repl_sb = nc.alloc_sbuf_tensor("repl_sb", [16, 1024], BF16)
    nh2_sb = nc.alloc_sbuf_tensor("nh2_sb", [128, 2], BF16)
    ones1r_sb = nc.alloc_sbuf_tensor("ones1r_sb", [1, 128], F32)
    actc_sb = nc.alloc_sbuf_tensor("actc_sb", [128, 4], F32)
    rwT_sb = nc.alloc_sbuf_tensor("rwT_sb", [128, 128], BF16)
    rb_sb = nc.alloc_sbuf_tensor("rb_sb", [16, 1], F32)
    rlog_sb = ab.view(A_BF, "rlog", 16)

    PB = [nc.alloc_psum_tensor(f"pb{i}", [128, 512], F32) for i in range(6)]
    PT = [nc.alloc_psum_tensor(f"pt{i}", [128, 512], BF16) for i in range(2)]
    assert nc.psum_banks_remaining >= 0

    ctx = Ctx()

    def program(c):
        sp, gp, pe, act, dve = c.sp, c.gp, c.pe, c.act, c.dve
        ln_prev = []

        for d, s in [
            (rfv("cosT"), cosT_in), (rfv("sinT"), sinT_in),
            (rfv("grow", 1), grow_in), (rfv("brow", 1), brow_in),
            (rfv("b2row", 1), b2row_in), (rfv("b1c"), b1c_in),
            (rfv("i128f"), i128f_in),
            (hmat_sb[:, :], hmat_in), (hs_sb[:, :], hs_in),
            (swap_sb[:, :], swap_in), (i128b_sb[:, :], i128b_in),
            (repl_sb[:, :], repl_in), (nh2_sb[:, :], nh2_in),
            (ones1r_sb[:, :], ones1r_in),
            (actc_sb[:, :], actc_in),
            (rb_sb[:, :], rb_in),
            (rbv("htl", 64), ht_in),
        ]:
            sp.dma_start(out=d, in_=s).incn("ld", 16)
        sp.wait_raw("ld", 0)
        c.mark("consts_loaded", "ld")
        sp.wait("consts_loaded")

        for nm, src in [("gt", gt_in), ("s1t", s1_in),
                        ("s2t", s2_in), ("s3t", s3_in)]:
            gp.dma_start(out=rbv(nm, 64), in_=src).incn("wd", 16)
        rwv = rwT_in.rearrange("(cp p) e -> p cp e", p=128)
        gp.dma_start(out=rwT_sb[:, :].rearrange("p (cp e) -> p cp e", e=16),
                     in_=rwv).incn("wd", 16)
        c.mark("tiled_loaded", "wd")
        gp.wait("tiled_loaded")
        kvv = wqkvT_in[:, 1024:3072].rearrange("(c p) o -> p c o", p=128)
        kvd = rbv("wqkv_kv").rearrange("p (c o) -> p c o", o=2048)
        gp.dma_start(out=kvd, in_=kvv).inc("wd", 16)
        c.mark("wqkv_kv_loaded", "wd")
        qv = wqkvT_in[:, 0:1024].rearrange("(c p) o -> p c o", p=128)
        qd = rbv("wqkv_q").rearrange("p (c o) -> p c o", o=1024)
        gp.dma_start(out=qd, in_=qv).inc("wd", 16)
        c.mark("wqkv_q_loaded", "wd")

        pe.wait("consts_loaded")
        for i, rowname in enumerate(["grow", "brow"]):
            for half in range(2):
                k = i * 2 + half
                if k >= 2:
                    pe.wait(f"gb_ev_{k - 2}")
                pe.matmul(PB[2 + (k % 2)][:, :], ones1r_sb[:, :],
                          rfv(rowname, 1)[:, half * 512:(half + 1) * 512],
                          start=True, stop=True).inc("pe", 1)
                c.mark(f"gb_mm_{k}", "pe")
                dve.wait(f"gb_mm_{k}")
                dst = rfv("gam_bc" if i == 0 else "bet_bc")
                dve.tensor_copy(dst[:, half * 512:(half + 1) * 512],
                                PB[2 + (k % 2)][:, :]).inc("dve", 1)
                c.mark(f"gb_ev_{k}", "dve")
        c.mark("gb_done", "dve")

        dve.wait("tiled_loaded")
        dve.wait("consts_loaded")
        dve.tensor_mul(rbv("Xa", 64), rbv("htl", 64),
                       rbv("gt", 64)).fence("dve")
        dve.tensor_mul(rbv("Xb", 64), rbv("Xa", 64), rbv("s3t", 64)).inc("dve", 1)
        c.mark("wpre_x2", "dve")

        def wpre_stage(tag, prev_done, lhsT, src_name, scale_name, dst_name):
            for j in range(8):
                pe.wait(prev_done if j < 2 else f"{tag}_ev_{j - 2}")
                pe.matmul(PB[j % 2][0:64, :], lhsT,
                          rbv(src_name, 64)[:, j * 512:(j + 1) * 512],
                          start=True, stop=True).inc("pe", 1)
                c.mark(f"{tag}_mm_{j}", "pe")
                dve.wait(f"{tag}_mm_{j}")
                if scale_name is None:
                    dve.tensor_copy(rbv(dst_name, 64)[:, j * 512:(j + 1) * 512],
                                    PB[j % 2][0:64, :]).inc("dve", 1)
                else:
                    dve.tensor_mul(rbv(dst_name, 64)[:, j * 512:(j + 1) * 512],
                                   PB[j % 2][0:64, :],
                                   rbv(scale_name, 64)[:, j * 512:(j + 1) * 512]
                                   ).inc("dve", 1)
                c.mark(f"{tag}_ev_{j}", "dve")
            c.mark(f"{tag}_done", "dve")

        wpre_stage("wx3", "wpre_x2", hmat_sb[:, :], "Xb", "s2t", "Xa")
        wpre_stage("wx5", "wx3_done", hs_sb[:, :], "Xa", "s1t", "Wc")
        wpre_stage("wx7", "wx5_done", swap_sb[:, :], "Wc", None, "Ws")
        c.mark("wpre_done", "dve")
        sp.wait("wpre_done")
        sp.dma_start(out=ab.view2(A_BF, "Wc", 64, 128),
                     in_=rbv("Wc", 64)).inc("ld", 16)
        sp.dma_start(out=ab.view2(A_BF, "Ws", 64, 128),
                     in_=rbv("Ws", 64)).inc("ld", 16)
        c.mark("wrep_done", "ld")

        def ln_tile(tag, x_ap, out_ap, parity=0, save_cols=None):
            cols = rfv("cols0" if parity == 0 else "cols1")
            if len(ln_prev) >= 2:
                act.wait(f"lnout_{ln_prev[-2]}")
            act.activation(rfv("xsq"), x_ap, AF.Square,
                           accum_out=cols[:, 0:1]).inc("act", 1)
            c.mark(f"lnsq_{tag}", "act")
            dve.wait(f"lnsq_{tag}")
            dve.reduce_sum(cols[:, 4:5], x_ap, axis=AX.X).inc("dve", 1)
            c.mark(f"lnred_{tag}", "dve")
            act.wait(f"lnred_{tag}")
            act.mul(cols[:, 12:13], cols[:, 4:5], -1.0 / 1024.0)
            act.mul(cols[:, 8:9], cols[:, 4:5], 1.0 / 1024.0).fence("act")
            act.activation(cols[:, 16:17], cols[:, 8:9],
                           AF.Square).inc("act", 1)
            c.mark(f"lnmid_{tag}", "act")
            dve.wait(f"lnmid_{tag}")
            dve.scalar_tensor_tensor(cols[:, 20:21], cols[:, 0:1], 1.0 / 1024.0,
                                     cols[:, 16:17], ALU.mult,
                                     ALU.subtract).inc("dve", 1)
            c.mark(f"lnvar_{tag}", "dve")
            act.wait(f"lnvar_{tag}")
            act.activation(cols[:, 24:25], cols[:, 20:21], AF.Ln,
                           bias=actc_sb[:, 0:1]).fence("act")
            act.activation(cols[:, 28:29], cols[:, 24:25], AF.Exp,
                           scale=actc_sb[:, 1:2]).inc("act", 1)
            c.mark(f"lnrstd_{tag}", "act")
            dve.wait(f"lnrstd_{tag}")
            if save_cols is not None:
                dve.tensor_copy(save_cols[:, 0:1], cols[:, 12:13])
                dve.tensor_copy(save_cols[:, 1:2], cols[:, 28:29])
            dve.tensor_scalar(out_ap, x_ap, cols[:, 12:13], cols[:, 28:29],
                              ALU.add, ALU.mult).inc("dve", 1)
            c.mark(f"lnout_{tag}", "dve")
            ln_prev.append(tag)

        def u_transpose(tag, u_tm, uT3, tc):
            for bat in range(2):
                gidx = tc * 2 + bat
                pe.wait(f"lnout_{tag}_{tc}")
                if gidx >= 2:
                    pe.wait(f"{tag}_trev_{gidx - 2}")
                for dc in range(4):
                    d = bat * 4 + dc
                    ins = pe.transpose(PT[gidx % 2][:, dc * 128:(dc + 1) * 128],
                                 u_tm[:, d * 128:(d + 1) * 128],
                                 i128b_sb[:, :])
                    if dc == 3:
                        ins.inc("pe", 1)
                c.mark(f"{tag}_tr_{gidx}", "pe")
                dve.wait(f"{tag}_tr_{gidx}")
                src = PT[gidx % 2][:, :].rearrange("p (d t) -> p d t", t=128)
                dve.tensor_copy(uT3[:, bat * 4:(bat + 1) * 4,
                                    tc * 128:(tc + 1) * 128],
                                src).inc("dve", 1)
                c.mark(f"{tag}_trev_{gidx}", "dve")

        if stop >= 1:
            u1T3 = rbv("u1T").rearrange("p (d t) -> p d t", t=2048)
            for tc in range(p1n):
                xr = rfv("xr0") if tc % 2 == 0 else rfv("xr1")
                if tc >= 2:
                    sp.wait(f"lnout_ln1_{tc - 2}")
                sp.dma_start(out=xr, in_=x_in[tc * 128:(tc + 1) * 128, :]).inc("ld", 16)
                c.mark(f"x_tile_{tc}", "ld")
                act.wait(f"x_tile_{tc}")
                dve.wait(f"x_tile_{tc}")
                if tc >= 2:
                    dve.wait(f"ln1_tr_{(tc - 2) * 2 + 1}")
                u1r = rbv("u1r0") if tc % 2 == 0 else rbv("u1r1")
                ln_tile(f"ln1_{tc}", xr, u1r, parity=tc % 2)
                u_transpose("ln1", u1r, u1T3, tc)
            c.mark("u1T_done", "dve")
            if debug:
                sp.wait("u1T_done")
                sp.dma_start(out=dbg["d_u1T"], in_=rbv("u1T")).inc("st", 16)
                if p1n <= 2:
                    sp.dma_start(out=dbg["d_xr"], in_=rfv("xr0")).inc("st", 16)
                    sp.dma_start(out=dbg["d_xsq"], in_=rfv("xsq")).inc("st", 16)
                    sp.dma_start(out=dbg["d_cols"], in_=rfv("cols")).inc("st", 16)
                    sp.dma_start(out=dbg["d_u1r"], in_=rbv("u1r0")).inc("st", 16)
                c.mark("dbg_u1t", "st")

        if stop >= 2:
            v_sb = rbv("v_sb")
            v4 = v_sb.rearrange("p (t h d) -> p t h d", h=16, d=65)
            dve.memset(v4[:, :, :, 64:65], 1.0).inc("dve", 1)
            c.mark("v_ones", "dve")
            u1Tv = u1T3
            pe.wait("u1T_done")
            pe.wait("wqkv_kv_loaded")
            for tc in range(NTC):
                for half in range(2):
                    i = tc * 2 + half
                    if i >= 2:
                        pe.wait(f"v_ev_{i - 2}")
                    for dc in range(8):
                        ins = pe.matmul(PB[i % 2][:, :],
                                  u1Tv[:, dc, tc * 128:(tc + 1) * 128],
                                  rbv("wqkv_kv")[:, dc * 2048 + 1024 + half * 512:
                                                 dc * 2048 + 1024 + (half + 1) * 512],
                                  start=(dc == 0), stop=(dc == 7))
                        if dc == 7:
                            ins.inc("pe", 1)
                    c.mark(f"v_mm_{i}", "pe")
                    dve.wait(f"v_mm_{i}")
                    dve.wait("v_ones")
                    dst = v_sb[:, tc * 1040 + half * 520:
                               tc * 1040 + (half + 1) * 520]
                    dst3 = dst.rearrange("p (h d) -> p h d", d=65)[:, :, 0:64]
                    src3 = PB[i % 2][:, :].rearrange("p (h d) -> p h d", d=64)
                    dve.tensor_copy(dst3, src3).inc("dve", 1)
                    c.mark(f"v_ev_{i}", "dve")
            c.mark("v_done", "dve")

        if stop >= 3:
            kc, ks, ksq = rbv("kc"), rbv("ks"), rbv("ksq")
            nsqk = rfv("nsqk")
            nsqk3 = nsqk.rearrange("p (t h) -> p t h", h=16)
            for cc in range(8):
                for nt in range(4):
                    i = cc * 4 + nt
                    if i >= 2:
                        pe.wait(f"k_rope_{i - 2}")
                        pe.wait(f"k_sq_{i - 2}")
                    else:
                        pe.wait(f"v_ev_{30 + i}")
                    for dc in range(8):
                        ins = pe.matmul(PB[i % 2][:, :],
                                  rbv("wqkv_kv")[:, dc * 2048 + cc * 128:
                                                 dc * 2048 + (cc + 1) * 128],
                                  u1Tv[:, dc, nt * 512:(nt + 1) * 512],
                                  start=(dc == 0), stop=(dc == 7))
                        if dc == 7:
                            ins.inc("pe", 1)
                    c.mark(f"k_mm_{i}", "pe")
                    dve.wait(f"k_mm_{i}")
                    if cc > 0 and nt == 0 and kstop >= 4:
                        dve.wait(f"s_acc_{cc - 1}")
                    elif cc > 0 and nt == 0 and kstop == 3:
                        dve.wait(f"k_phi_{cc - 1}_31")
                    dve.tensor_mul(kc[:, nt * 512:(nt + 1) * 512], PB[i % 2][:, :],
                                   rfv("cosT")[:, nt * 512:(nt + 1) * 512])
                    dve.tensor_mul(ks[:, nt * 512:(nt + 1) * 512], PB[i % 2][:, :],
                                   rfv("sinT")[:, nt * 512:(nt + 1) * 512]
                                   ).inc("dve", 1)
                    c.mark(f"k_rope_{i}", "dve")
                    act.wait(f"k_mm_{i}")
                    act.wait(f"k_rope_{i}")
                    if cc > 0 and nt == 0 and kstop >= 2:
                        act.wait(f"nsq_mm_k_{cc - 1}")
                    act.activation(ksq[:, nt * 512:(nt + 1) * 512], PB[i % 2][:, :],
                                   AF.Square).inc("act", 1)
                    c.mark(f"k_sq_{i}", "act")
                if kstop >= 2:
                    pe.wait(f"k_sq_{cc * 4 + 3}")
                    if cc > 0:
                        pe.wait(f"nsq_ev_k_{cc - 1}")
                    for tc in range(NTC):
                        ins = pe.matmul(PB[4][:, 2 * tc:2 * tc + 2],
                                  ksq[:, tc * 128:(tc + 1) * 128], nh2_sb[:, :],
                                  start=True, stop=True)
                        if tc == NTC - 1:
                            ins.inc("pe", 1)
                    c.mark(f"nsq_mm_k_{cc}", "pe")
                    dve.wait(f"nsq_mm_k_{cc}")
                    dve.tensor_copy(
                        nsqk3[:, :, 2 * cc:2 * cc + 2],
                        PB[4][:, 0:32].rearrange("p (t h) -> p t h", h=2)).inc("dve", 1)
                    c.mark(f"nsq_ev_k_{cc}", "dve")
                if cc == 0:
                    pe.wait("wrep_done")
                    pe.wait("v_done")
                def k_smm(jj):
                    tcj, h2j = jj // 2, jj % 2
                    hj = 2 * cc + h2j
                    pe.wait(f"k_exp_{cc}_{jj}")
                    if jj == 0 and cc > 0:
                        pe.wait(f"s_ev_{cc - 1}")
                    kphi_j = rbv("kphi0" if jj % 2 == 0 else "kphi1")
                    pe.matmul(PB[5][0:65, h2j * 256:(h2j + 1) * 256],
                              v_sb[:, tcj * 1040 + 65 * hj:
                                   tcj * 1040 + 65 * hj + 65],
                              kphi_j, start=(tcj == 0), stop=(tcj == 15),
                              skip_group_check=True).inc("pe", 1)
                    c.mark(f"k_smm_{cc}_{jj}", "pe")

                for tc in (range(NTC) if kstop >= 3 else []):
                    for h2 in range(2):
                        j = tc * 2 + h2
                        h = 2 * cc + h2
                        if j >= 2:
                            pe.wait(f"k_exp_{cc}_{j - 2}")
                        elif cc > 0:
                            pe.wait(f"k_exp_{cc - 1}_{30 + j}")
                        hb = h2 * 64
                        pe.matmul(PB[2 + (j % 2)][:, 0:256],
                                  kc[hb:hb + 64, tc * 128:(tc + 1) * 128],
                                  ab.view2(A_BF, "Wc", hb, hb + 64)
                                  [:, h * 256:(h + 1) * 256],
                                  start=True, stop=False)
                        pe.matmul(PB[2 + (j % 2)][:, 0:256],
                                  ks[hb:hb + 64, tc * 128:(tc + 1) * 128],
                                  ab.view2(A_BF, "Ws", hb, hb + 64)
                                  [:, h * 256:(h + 1) * 256],
                                  start=False, stop=True).inc("pe", 1)
                        c.mark(f"k_phi_{cc}_{j}", "pe")
                        act.wait(f"k_phi_{cc}_{j}")
                        act.wait(f"nsq_ev_k_{cc}")
                        if kstop >= 4 and j >= 2:
                            act.wait(f"k_smm_{cc}_{j - 2}")
                        elif kstop >= 4 and cc > 0:
                            act.wait(f"k_smm_{cc - 1}_{30 + j}")
                        kphi = rbv("kphi0" if j % 2 == 0 else "kphi1")
                        act.activation(kphi, PB[2 + (j % 2)][:, 0:256], AF.Exp,
                                       bias=nsqk[:, tc * 16 + h:tc * 16 + h + 1]
                                       ).inc("act", 1)
                        c.mark(f"k_exp_{cc}_{j}", "act")
                        if kstop >= 4 and j >= 1:
                            k_smm(j - 1)
                if kstop >= 4:
                    k_smm(31)
                    c.mark(f"s_acc_{cc}", "pe")
                    dve.wait(f"s_acc_{cc}")
                    dve.tensor_copy(rbv("spre", 65)[:, cc * 512:(cc + 1) * 512],
                                    PB[5][0:65, :]).inc("dve", 1)
                    c.mark(f"s_ev_{cc}", "dve")
            c.mark("spre_done", "dve")

        if stop >= 4:
            gp.wait("spre_done")
            gp.dma_start(out=ar_in, in_=rbv("spre", 65)).inc("wd", 16)
            c.mark("ar_staged", "wd")
            gp.wait("ar_staged")
            gp.collective_compute(
                "AllReduce", ALU.add,
                replica_groups=[[0, 1, 2, 3], [4, 5, 6, 7]],
                ins=[ar_in], outs=[ar_out]).inc("cc", 1)
            gp.wait_raw("cc", 1)
            gp.dma_start(out=rbv("spostb", 65), in_=ar_out).inc("wd", 16)
            c.mark("spost_loaded", "wd")
            if stop >= 9:
                gp.wait("oh_done")
                wov = woutT_in.rearrange("(c p) o -> p c o", p=128)
                gp.dma_start(out=rbv("wout").rearrange("p (c o) -> p c o",
                                                       o=1024),
                             in_=wov).inc("wd", 16)
                c.mark("wout_loaded", "wd")
            if stop >= 10:
                w2v = w2_in.rearrange("(c p) d -> p c d", p=128)
                gp.dma_start(out=rbv("w2full").rearrange("p (c d) -> p c d",
                                                         d=1024),
                             in_=w2v).inc("wd", 16)
                c.mark("w2_loaded", "wd")

        if stop >= 3:
            if debug:
                sp.wait("v_done")
                sp.dma_start(out=dbg["d_v"], in_=v_sb).inc("st", 16)
                c.mark("dbg_v", "st")
                sp.wait("spre_done")
                sp.dma_start(out=dbg["d_kc"], in_=kc).inc("st", 16)
                sp.dma_start(out=dbg["d_spre"], in_=rbv("spre", 65)).inc("st", 16)
                sp.dma_start(out=dbg["d_nsqk"], in_=nsqk).inc("st", 16)
                c.mark("dbg_p2", "st")

        if stop >= 5:
            qc3 = rbv("qc").rearrange("p (cc t) -> p cc t", t=2048)
            qs3 = rbv("qs").rearrange("p (cc t) -> p cc t", t=2048)
            qsq = rbv("qsq")
            nsqq = rfv("nsqq")
            nsqq3 = nsqq.rearrange("p (t h) -> p t h", h=16)
            pe.wait("wqkv_q_loaded")
            if debug:
                pe.wait("dbg_v")
                pe.wait("dbg_p2")
            for cc in range(8):
                for nt in range(4):
                    i = cc * 4 + nt
                    if i >= 2:
                        pe.wait(f"q_rope_{i - 2}")
                        pe.wait(f"q_sq_{i - 2}")
                    else:
                        pe.wait(f"k_rope_{30 + i}")
                        pe.wait(f"k_sq_{30 + i}")
                    for dc in range(8):
                        ins = pe.matmul(PB[i % 2][:, :],
                                  rbv("wqkv_q")[:, dc * 1024 + cc * 128:
                                                dc * 1024 + (cc + 1) * 128],
                                  u1Tv[:, dc, nt * 512:(nt + 1) * 512],
                                  start=(dc == 0), stop=(dc == 7))
                        if dc == 7:
                            ins.inc("pe", 1)
                    c.mark(f"q_mm_{i}", "pe")
                    dve.wait(f"q_mm_{i}")
                    dve.tensor_mul(qc3[:, cc, nt * 512:(nt + 1) * 512],
                                   PB[i % 2][:, :],
                                   rfv("cosT")[:, nt * 512:(nt + 1) * 512])
                    dve.tensor_mul(qs3[:, cc, nt * 512:(nt + 1) * 512],
                                   PB[i % 2][:, :],
                                   rfv("sinT")[:, nt * 512:(nt + 1) * 512]
                                   ).inc("dve", 1)
                    c.mark(f"q_rope_{i}", "dve")
                    act.wait(f"q_mm_{i}")
                    act.wait(f"q_rope_{i}")
                    if cc > 0 and nt == 0:
                        act.wait(f"nsq_mm_q_{cc - 1}")
                    act.activation(qsq[:, nt * 512:(nt + 1) * 512], PB[i % 2][:, :],
                                   AF.Square).inc("act", 1)
                    c.mark(f"q_sq_{i}", "act")
                pe.wait(f"q_sq_{cc * 4 + 3}")
                pe.wait(f"nsq_ev_q_{cc - 1}" if cc > 0 else "nsq_ev_k_7")
                for tc in range(NTC):
                    ins = pe.matmul(PB[4][:, 2 * tc:2 * tc + 2],
                              qsq[:, tc * 128:(tc + 1) * 128], nh2_sb[:, :],
                              start=True, stop=True)
                    if tc == NTC - 1:
                        ins.inc("pe", 1)
                c.mark(f"nsq_mm_q_{cc}", "pe")
                dve.wait(f"nsq_mm_q_{cc}")
                dve.tensor_copy(
                    nsqq3[:, :, 2 * cc:2 * cc + 2],
                    PB[4][:, 0:32].rearrange("p (t h) -> p t h", h=2)).inc("dve", 1)
                c.mark(f"nsq_ev_q_{cc}", "dve")
            c.mark("qside_done", "dve")

            act.wait("qside_done")
            dve.wait("qside_done")
            for tc in range(NTC):
                ftm = rfv("ftm0") if tc % 2 == 0 else rfv("ftm1")
                fcl = rfv("fcl0") if tc % 2 == 0 else rfv("fcl1")
                if tc >= 2:
                    dve.wait(f"f_tm_{tc - 2}")
                dve.tensor_scalar(fcl, nsqq[:, tc * 16:(tc + 1) * 16],
                                  -1.0, 85.0, ALU.mult,
                                  ALU.min).inc("dve", 1)
                c.mark(f"f_cl_{tc}", "dve")
                act.wait(f"f_cl_{tc}")
                if tc >= 2:
                    act.wait(f"f_tr_{tc - 2}")
                act.activation(ftm, fcl, AF.Exp,
                               bias=actc_sb[:, 3:4]).inc("act", 1)
                c.mark(f"f_tm_{tc}", "act")
                pe.wait(f"f_tm_{tc}")
                if tc >= 2:
                    pe.wait(f"f_ev_{tc - 2}")
                else:
                    pe.wait("nsq_ev_q_7")
                    pe.wait("spre_done")
                pe.transpose(PB[4 + (tc % 2)][0:16, 0:128], ftm,
                             rfv("i128f")).inc("pe", 1)
                c.mark(f"f_tr_{tc}", "pe")
                dve.wait(f"f_tr_{tc}")
                if tc < 2:
                    dve.wait("ar_staged")
                dve.tensor_copy(rfv("f_fm", 16)[:, tc * 128:(tc + 1) * 128],
                                PB[4 + (tc % 2)][0:16, 0:128]).inc("dve", 1)
                c.mark(f"f_ev_{tc}", "dve")
            c.mark("f_done", "dve")

        if stop >= 6:
            pe.wait("spost_loaded")
            pe.wait("u1T_done")
            for h in range(NH):
                for half in range(2):
                    i = h * 2 + half
                    if i >= 2:
                        pe.wait(f"spost_ev_{i - 2}")
                    pe.transpose(PT[i % 2][:, 0:65],
                                 rbv("spostb", 65)[:, h * 256 + half * 128:
                                                   h * 256 + (half + 1) * 128],
                                 i128b_sb[0:65, 0:65]).inc("pe", 1)
                    c.mark(f"spost_tr_{i}", "pe")
                    dve.wait(f"spost_tr_{i}")
                    dve.tensor_copy(rbv("S_sb")[:, i * 65:(i + 1) * 65],
                                    PT[i % 2][:, 0:65]).inc("dve", 1)
                    c.mark(f"spost_ev_{i}", "dve")
            dve.wait("spost_ev_31")
            dve.memset(rbv("sumk"), 0.0).fence("dve")
            for h in range(NH):
                for half in range(2):
                    i = h * 2 + half
                    ins = dve.tensor_copy(rbv("sumk")[:, i * 16 + h:i * 16 + h + 1],
                                          rbv("S_sb")[:, i * 65 + 64:i * 65 + 65])
                    if i == 31:
                        ins.inc("dve", 1)
            c.mark("sumk_done", "dve")
            if debug:
                sp.wait("sumk_done")
                sp.dma_start(out=dbg["d_ssb"], in_=rbv("S_sb")).inc("st", 16)
                sp.dma_start(out=dbg["d_nsqq"], in_=nsqq).inc("st", 16)
                sp.wait("f_done")
                sp.dma_start(out=dbg["d_f"], in_=rfv("f_fm", 16)).inc("st", 16)
                c.mark("dbg_s", "st")

        if stop >= 7:
            def phi_q_tile(tag, h, mc, nt, jg):
                cchunk, hr = h // 2, (h % 2) * 64
                if jg >= 2:
                    pe.wait(f"{tag}_exp_{jg - 2}")
                pe.matmul(PB[jg % 2][:, :],
                          ab.view2(A_BF, "Wc", hr, hr + 64)
                          [:, h * 256 + mc * 128:h * 256 + (mc + 1) * 128],
                          qc3[hr:hr + 64, cchunk, nt * 512:(nt + 1) * 512],
                          start=True, stop=False)
                pe.matmul(PB[jg % 2][:, :],
                          ab.view2(A_BF, "Ws", hr, hr + 64)
                          [:, h * 256 + mc * 128:h * 256 + (mc + 1) * 128],
                          qs3[hr:hr + 64, cchunk, nt * 512:(nt + 1) * 512],
                          start=False, stop=True).inc("pe", 1)
                c.mark(f"{tag}_phi_{jg}", "pe")
                act.wait(f"{tag}_phi_{jg}")
                qp = rbv("Qp0" if h % 2 == 0 else "Qp1")
                act.activation(qp[:, mc * 2048 + nt * 512:mc * 2048 + (nt + 1) * 512],
                               PB[jg % 2][:, :], AF.Exp).inc("act", 1)
                c.mark(f"{tag}_exp_{jg}", "act")

            jg = 0
            pe.wait("qside_done")
            pe.wait("sumk_done")
            pe.wait("f_done")

            def a0_den(hh):
                qph = rbv("Qp0" if hh % 2 == 0 else "Qp1")
                pe.wait(f"a0_exp_{(hh + 1) * 8 - 1}")
                for mc in range(2):
                    for nt in range(4):
                        ins = pe.matmul(PB[2 + nt][0:16, :],
                                  rbv("sumk")[:, (hh * 2 + mc) * 16:
                                              (hh * 2 + mc + 1) * 16],
                                  qph[:, mc * 2048 + nt * 512:
                                      mc * 2048 + (nt + 1) * 512],
                                  start=(hh == 0 and mc == 0),
                                  stop=(hh == NH - 1 and mc == 1),
                                  skip_group_check=True)
                        if mc == 1 and nt == 3:
                            ins.inc("pe", 1)
                c.mark(f"a0_den_{hh}", "pe")

            for h in range(NH):
                if h >= 2:
                    act.wait(f"a0_den_{h - 2}")
                for mc in range(2):
                    for nt in range(4):
                        phi_q_tile("a0", h, mc, nt, jg)
                        jg += 1
                if h >= 1:
                    a0_den(h - 1)
            a0_den(NH - 1)
            c.mark("a0_done", "pe")
            dve.wait("a0_done")
            for nt in range(4):
                ins = dve.tensor_copy(
                    rfv("den", 16)[:, nt * 512:(nt + 1) * 512],
                    PB[2 + nt][0:16, :])
                ins.fence("dve") if nt == 3 else ins.inc("dve", 1)
            dve.tensor_max(rfv("dmx", 16), rfv("den", 16),
                           rfv("f_fm", 16)).inc("dve", 1)
            c.mark("recip_f1", "dve")
            dve.wait("recip_f1")
            dve.reciprocal(rfv("den", 16), rfv("dmx", 16)).inc("dve", 1)
            c.mark("recip_f2", "dve")
            dve.wait("recip_f2")
            dve.tensor_copy(rbv("recipb", 16), rfv("den", 16)).inc("dve", 1)
            c.mark("recip_done", "dve")
            if debug:
                sp.wait("recip_done")
                sp.dma_start(out=dbg["d_den"], in_=rfv("dmx", 16)).inc("st", 16)
                sp.dma_start(out=dbg["d_recip"], in_=rbv("recipb", 16)).inc("st", 16)
                c.mark("dbg_recip", "st")

        if stop >= 8:
            jg = 0

            def a1_tail(hh):
                qph = rbv("Qp0" if hh % 2 == 0 else "Qp1")
                numer = rbv("numer0" if hh % 2 == 0 else "numer1")
                pe.wait(f"a1_exp_{(hh + 1) * 8 - 1}")
                if hh == 0:
                    pe.wait("recip_done")
                for nt in range(4):
                    gidx = hh * 4 + nt
                    if nt >= 2:
                        pe.wait(f"a1_nev_{hh}_{nt - 2}")
                    elif hh >= 1:
                        pe.wait(f"a1_nev_{hh - 1}_{nt + 2}")
                    for half in range(2):
                        ins = pe.matmul(PB[2 + (nt % 2)][0:65, :],
                                  rbv("S_sb")[:, (hh * 2 + half) * 65:
                                              (hh * 2 + half + 1) * 65],
                                  qph[:, half * 2048 + nt * 512:
                                      half * 2048 + (nt + 1) * 512],
                                  start=(half == 0),
                                  stop=(half == 1))
                        if half == 1:
                            ins.inc("pe", 1)
                    c.mark(f"a1_nmm_{hh}_{nt}", "pe")
                    act.wait(f"a1_nmm_{hh}_{nt}")
                    if hh >= 2:
                        act.wait(f"a1_oh_{(hh - 2) * 4 + nt}")
                    act.activation(numer[0:64, nt * 512:(nt + 1) * 512],
                                   PB[2 + (nt % 2)][0:64, :],
                                   AF.Copy).inc("act", 1)
                    c.mark(f"a1_nev_{hh}_{nt}", "act")
                    if gidx >= 2:
                        pe.wait(f"a1_oh_{gidx - 2}")
                    pe.matmul(PB[4 + (nt % 2)][0:64, :],
                              repl_sb[:, hh * 64:(hh + 1) * 64],
                              rbv("recipb", 16)[:, nt * 512:(nt + 1) * 512],
                              start=True, stop=True).inc("pe", 1)
                    c.mark(f"a1_bc_{hh}_{nt}", "pe")
                    dve.wait(f"a1_nev_{hh}_{nt}")
                    dve.wait(f"a1_bc_{hh}_{nt}")
                    hr = (hh % 2) * 64
                    dve.tensor_mul(
                        rbv("outheads")[hr:hr + 64,
                                        (hh // 2) * 2048 + nt * 512:
                                        (hh // 2) * 2048 + (nt + 1) * 512],
                        PB[4 + (nt % 2)][0:64, :],
                        numer[0:64, nt * 512:(nt + 1) * 512]).inc("dve", 1)
                    c.mark(f"a1_oh_{gidx}", "dve")

            for h in range(NH):
                if h >= 2:
                    act.wait(f"a1_nmm_{h - 2}_3")
                else:
                    act.wait(f"a0_den_{14 + h}")
                for mc in range(2):
                    for nt in range(4):
                        phi_q_tile("a1", h, mc, nt, jg)
                        jg += 1
                if h >= 1:
                    a1_tail(h - 1)
            a1_tail(NH - 1)
            c.mark("oh_done", "dve")
            if debug:
                sp.wait("oh_done")
                sp.dma_start(out=dbg["d_oh"], in_=rbv("outheads")).inc("st", 16)
                c.mark("dbg_oh", "st")

        if stop >= 9:
            oh3 = rbv("outheads").rearrange("p (e t) -> p e t", t=2048)
            u2T3 = rbv("u2T").rearrange("p (d t) -> p d t", t=2048)
            pe.wait("wout_loaded")
            for tc in range(NTC):
                xr = rfv("xr0") if tc % 2 == 0 else rfv("xr1")
                if tc >= 2:
                    sp.wait(f"x1_half_{(tc - 2) * 2 + 1}")
                else:
                    sp.wait(f"lnout_ln1_{14 + tc}")
                    sp.wait("oh_done")
                sp.dma_start(out=xr,
                             in_=x_in[tc * 128:(tc + 1) * 128, :]).inc("ld", 16)
                c.mark(f"x2_tile_{tc}", "ld")
                x1t = rfv("x1t0") if tc % 2 == 0 else rfv("x1t1")
                for ont in range(2):
                    j = tc * 2 + ont
                    if j == 0:
                        pe.wait("oh_done")
                    if j >= 2:
                        pe.wait(f"x1_half_{j - 2}")
                    for ec in range(8):
                        ins = pe.matmul(PB[j % 2][:, :],
                                  oh3[:, ec, tc * 128:(tc + 1) * 128],
                                  rbv("wout")[:, ec * 1024 + ont * 512:
                                              ec * 1024 + (ont + 1) * 512],
                                  start=(ec == 0), stop=(ec == 7))
                        if ec == 7:
                            ins.inc("pe", 1)
                    c.mark(f"att_mm_{j}", "pe")
                    dve.wait(f"att_mm_{j}")
                    dve.wait(f"x2_tile_{tc}")
                    if tc >= 2 and ont == 0:
                        dve.wait(f"x1_spill_{tc - 2}")
                        dve.wait(f"ln2_tr_{(tc - 2) * 2 + 1}")
                    dve.tensor_add(x1t[:, ont * 512:(ont + 1) * 512],
                                   PB[j % 2][:, :],
                                   xr[:, ont * 512:(ont + 1) * 512]).inc("dve", 1)
                    c.mark(f"x1_half_{j}", "dve")
                sp.wait(f"x1_half_{tc * 2 + 1}")
                sp.dma_start(out=x1_dram[tc * 128:(tc + 1) * 128, :],
                             in_=x1t).inc("ld", 16)
                c.mark(f"x1_spill_{tc}", "ld")
                act.wait(f"x1_half_{tc * 2 + 1}")
                u2r = rbv("u2r0") if tc % 2 == 0 else rbv("u2r1")
                if tc >= 2:
                    dve.wait(f"ln2_tr_{(tc - 2) * 2 + 1}")
                ln_tile(f"ln2_{tc}", x1t, u2r, parity=tc % 2,
                        save_cols=rfv("ln2cols")[:, 2 * tc:2 * tc + 2])
                u_transpose("ln2", u2r, u2T3, tc)
            c.mark("u2T_done", "dve")
            if debug:
                sp.wait("u2T_done")
                sp.dma_start(out=dbg["d_x1"], in_=x1_dram).inc("st", 16)
                sp.dma_start(out=dbg["d_u2T"], in_=rbv("u2T")).inc("st", 16)
                c.mark("dbg_u2", "st")

        if stop >= 10:
            if router >= 2:
                pe.wait("u2T_done")
                for nt in range(4):
                    if nt >= 1 and router >= 3:
                        pe.wait(f"rlog_ev_{nt - 1}")
                    for dc in range(8):
                        ins = pe.matmul(PB[4][0:16, :],
                                  rwT_sb[:, dc * 16:(dc + 1) * 16],
                                  u2T3[:, dc, nt * 512:(nt + 1) * 512],
                                  start=(dc == 0), stop=(dc == 7),
                                  skip_group_check=True)
                        if dc == 7:
                            ins.inc("pe", 1)
                    c.mark(f"rlog_mm_{nt}", "pe")
                    if router >= 3:
                        dve.wait(f"rlog_mm_{nt}")
                        dve.tensor_scalar_add(
                            rlog_sb[:, nt * 512:(nt + 1) * 512],
                            PB[4][0:16, :], rb_sb[:, :]).inc("dve", 1)
                        c.mark(f"rlog_ev_{nt}", "dve")

            h3 = rbv("h_sb").rearrange("p (kf t) -> p kf t", t=512)
            for th in range(4):
                for fb in range(8):
                    gi = th * 8 + fb
                    w1r = rbv(f"w1r{gi % 3}")
                    if gi < 3:
                        gp.wait("u2T_done")
                    else:
                        gp.wait(f"h_mm_{(gi - 3) * 4 + 3}")
                    w1v = w1_in[:, fb * 512:(fb + 1) * 512].rearrange(
                        "(cp p) f -> p cp f", p=128)
                    gp.dma_start(out=w1r.rearrange("p (cp f) -> p cp f", f=512),
                                 in_=w1v).inc("wd", 16)
                    c.mark(f"w1_ld_{gi}", "wd")
                    for fc in range(4):
                        j = gi * 4 + fc
                        pe.wait(f"w1_ld_{gi}")
                        if j == 0:
                            pe.wait("u2T_done")
                            pe.wait("w2_loaded")
                        if j >= 2:
                            pe.wait(f"h_relu_{j - 2}")
                        for dc in range(8):
                            ins = pe.matmul(PB[j % 2][:, :],
                                      w1r[:, dc * 512 + fc * 128:
                                          dc * 512 + (fc + 1) * 128],
                                      u2T3[:, dc, th * 512:(th + 1) * 512],
                                      start=(dc == 0), stop=(dc == 7))
                            if dc == 7:
                                ins.inc("pe", 1)
                        c.mark(f"h_mm_{j}", "pe")
                        act.wait(f"h_mm_{j}")
                        kf = fb * 4 + fc
                        act.activation(h3[:, kf, :], PB[j % 2][:, :], AF.Relu,
                                       bias=rfv("b1c")[:, kf:kf + 1]).inc("act", 1)
                        c.mark(f"h_relu_{j}", "act")
                for ts in range(4):
                    tc = th * 4 + ts
                    x1r = rfv("x1r0") if tc % 2 == 0 else rfv("x1r1")
                    if tc >= 2:
                        sp.wait(f"fin_tile_{tc - 2}")
                    else:
                        sp.wait("u2T_done")
                    sp.wait(f"x1_spill_{tc}")
                    sp.dma_start(out=x1r, in_=x1_dram[tc * 128:(tc + 1) * 128, :]
                                 ).inc("ld", 16)
                    c.mark(f"x1_reload_{tc}", "ld")
                    for dnt in range(2):
                        j2 = tc * 2 + dnt
                        pe.wait(f"h_relu_{th * 32 + 31}")
                        if j2 >= 2:
                            pe.wait(f"y_half_{j2 - 2}")
                        for kf in range(32):
                            pe.matmul(PB[2 + (j2 % 2)][:, :],
                                      h3[:, kf, ts * 128:(ts + 1) * 128],
                                      rbv("w2full")[:, kf * 1024 + dnt * 512:
                                                    kf * 1024 + (dnt + 1) * 512],
                                      start=(kf == 0),
                                      stop=(kf == 31))
                        pe.matmul(PB[2 + (j2 % 2)][:, :], ones1r_sb[:, :],
                                  rfv("b2row", 1)[:, dnt * 512:(dnt + 1) * 512],
                                  start=False, stop=True,
                                  skip_group_check=True).inc("pe", 1)
                        c.mark(f"eo_mm_{j2}", "pe")
                    dve.wait(f"x1_reload_{tc}")
                    if tc >= 1:
                        dve.wait(f"fin_tile_{tc - 1}")
                    dve.tensor_scalar(rfv("x1b"), x1r,
                                      rfv("ln2cols")[:, 2 * tc:2 * tc + 1],
                                      rfv("ln2cols")[:, 2 * tc + 1:2 * tc + 2],
                                      ALU.add, ALU.mult).inc("dve", 1)
                    c.mark(f"u2rc_{tc}", "dve")
                    dve.wait(f"u2rc_{tc}")
                    for dnt in range(2):
                        j2 = tc * 2 + dnt
                        dve.wait(f"eo_mm_{j2}")
                        dve.tensor_add(rfv("ytile")[:, dnt * 512:(dnt + 1) * 512],
                                       PB[2 + (j2 % 2)][:, :],
                                       rfv("x1b")[:, dnt * 512:(dnt + 1) * 512]
                                       ).inc("dve", 1)
                        c.mark(f"y_half_{j2}", "dve")
                    act.wait(f"y_half_{tc * 2 + 1}")
                    ln_tile(f"ln3_{tc}", rfv("ytile"), rfv("yn"), parity=tc % 2)
                    dve.wait(f"lnout_ln3_{tc}")
                    dve.tensor_mul(rfv("t2c"), rfv("yn"), rfv("gam_bc"))
                    dve.tensor_add(rfv("x1b"), x1r,
                                   rfv("bet_bc")).fence("dve")
                    c.mark(f"fin_pre_{tc}", "dve")
                    dve.tensor_add(rfv("ytile"), rfv("t2c"), rfv("x1b")).inc("dve", 1)
                    c.mark(f"fin_dve_{tc}", "dve")
                    sp.wait(f"fin_dve_{tc}")
                    sp.dma_start(out=y_out[tc * 128:(tc + 1) * 128, :],
                                 in_=rfv("ytile")).inc("st", 16)
                    c.mark(f"fin_tile_{tc}", "st")

        if c.live == "count":
            c.marks["__st_total"] = ("st", c.counts.get("st", 0))
        sp.wait_raw("st", c.marks["__st_total"][1])

    program(ctx)
    with (
        nc.semaphore("s_ld") as s_ld,
        nc.semaphore("s_wd") as s_wd,
        nc.semaphore("s_pe") as s_pe,
        nc.semaphore("s_act") as s_act,
        nc.semaphore("s_dve") as s_dve,
        nc.semaphore("s_cc") as s_cc,
        nc.semaphore("s_st") as s_st,
        nc.Block() as block,
    ):
        ctx.semh = {"ld": s_ld, "wd": s_wd, "pe": s_pe, "act": s_act,
                    "dve": s_dve, "cc": s_cc, "st": s_st}

        @block.sync
        def _(eng):
            ctx.real["sp"] = eng
            ctx.begin("sp")
            program(ctx)
            ctx.finish()

        @block.gpsimd
        def _(eng):
            ctx.real["gp"] = eng
            ctx.begin("gp")
            program(ctx)
            ctx.finish()

        @block.tensor
        def _(eng):
            ctx.real["pe"] = eng
            ctx.begin("pe")
            program(ctx)
            ctx.finish()

        @block.scalar
        def _(eng):
            ctx.real["act"] = eng
            ctx.begin("act")
            program(ctx)
            ctx.finish()

        @block.vector
        def _(eng):
            ctx.real["dve"] = eng
            ctx.begin("dve")
            program(ctx)
            ctx.finish()

    return nc


def make_in_maps(inputs):
    x = np.asarray(inputs["L0_attn_x"], dtype=np.float32)
    w_qkv = np.asarray(inputs["L0_attn_w_qkv"], dtype=np.float32)
    w_out = np.asarray(inputs["L0_attn_w_out"], dtype=np.float32)
    signs = np.asarray(inputs["L0_attn_sorf_signs"], dtype=np.float32)
    scales = np.asarray(inputs["L0_attn_sorf_scales"], dtype=np.float32)
    w1 = np.asarray(inputs["L0_ffn_w1"], dtype=np.float32)
    b1 = np.asarray(inputs["L0_ffn_b1"], dtype=np.float32)
    w2 = np.asarray(inputs["L0_ffn_w2"], dtype=np.float32)
    b2 = np.asarray(inputs["L0_ffn_b2"], dtype=np.float32)
    gamma = np.asarray(inputs["L0_ffn_gamma"], dtype=np.float32)
    beta = np.asarray(inputs["L0_ffn_beta"], dtype=np.float32)

    hm = _hadamard(HD)
    swap = np.zeros((HD, HD), dtype=np.float32)
    for i in range(0, HD, 2):
        swap[i, i + 1] = 1.0
        swap[i + 1, i] = 1.0
    rep = np.zeros((NH, NH * HD), dtype=np.float32)
    for h in range(NH):
        rep[h, h * HD:(h + 1) * HD] = 1.0
    nh2 = np.zeros((128, 2), dtype=np.float32)
    nh2[0:64, 0] = -0.5
    nh2[64:128, 1] = -0.5

    sgn = signs.reshape(64, 3, 64)
    router_w = np.asarray(inputs["L0_ffn_router_w"], dtype=np.float32)
    router_b = np.asarray(inputs["L0_ffn_router_b"], dtype=np.float32)
    common = {
        "rwT": np.ascontiguousarray(np.concatenate(
            [router_w.T, np.zeros((E, 8), np.float32)], axis=1)),
        "rbcol": np.ascontiguousarray(np.concatenate(
            [router_b, np.zeros(8, np.float32)]).reshape(16, 1)),
        "wqkvT": np.ascontiguousarray(w_qkv.T),
        "woutT": np.ascontiguousarray(w_out.T),
        "w1_0": np.ascontiguousarray(w1[0]),
        "w2_0": np.ascontiguousarray(w2[0]),
        "b1c": np.ascontiguousarray(b1[0].reshape(32, 128).T),
        "b2row": np.ascontiguousarray(b2[0].reshape(1, E)),
        "grow": np.ascontiguousarray(gamma.reshape(1, E)),
        "brow": np.ascontiguousarray(beta.reshape(1, E)),
        "g_tiled": np.ascontiguousarray(
            np.tile(scales.reshape(1, 4096), (64, 1)).astype(np.float32)),
        "s1_tiled": np.ascontiguousarray(
            np.repeat(sgn[:, 0, :].T, 64, axis=1).astype(np.float32)),
        "s2_tiled": np.ascontiguousarray(
            np.repeat(sgn[:, 1, :].T, 64, axis=1).astype(np.float32)),
        "s3_tiled": np.ascontiguousarray(
            np.repeat(sgn[:, 2, :].T, 64, axis=1).astype(np.float32)),
        "h_tiled": np.ascontiguousarray(
            np.tile(hm, (1, 64)).astype(ml_dtypes.bfloat16)),
        "hmat": np.ascontiguousarray(hm.astype(ml_dtypes.bfloat16)),
        "hs": np.ascontiguousarray((hm / 4096.0).astype(ml_dtypes.bfloat16)),
        "swapm": np.ascontiguousarray(swap.astype(ml_dtypes.bfloat16)),
        "i128f": np.eye(128, dtype=np.float32),
        "actconsts": np.ascontiguousarray(np.tile(
            np.array([LN_EPS, -0.5, -1.0, LN_EPS_SC], dtype=np.float32),
            (128, 1))),
        "i128b": np.ascontiguousarray(
            np.eye(128, dtype=np.float32).astype(ml_dtypes.bfloat16)),
        "repl": np.ascontiguousarray(rep.astype(ml_dtypes.bfloat16)),
        "neghalf2": np.ascontiguousarray(nh2.astype(ml_dtypes.bfloat16)),
        "ones1r": np.ones((1, 128), dtype=np.float32),
    }

    in_maps = []
    for core in range(8):
        bb = core // 4
        t0 = (core % 4) * NTOK
        cosT, sinT = _rope_tables(core)
        m = dict(common)
        m["x_slice"] = np.ascontiguousarray(x[bb, t0:t0 + NTOK, :])
        m["cosT"] = cosT
        m["sinT"] = sinT
        in_maps.append(m)
    return in_maps


def kernel(**inputs):
    nc = build_program(debug=False)
    in_maps = make_in_maps(inputs)
    res = run_bass_kernel_spmd(nc, in_maps, core_ids=list(range(8)))
    out = np.empty((B, L, E), dtype=np.float32)
    for core in range(8):
        bb = core // 4
        t0 = (core % 4) * NTOK
        out[bb, t0:t0 + NTOK, :] = np.asarray(res.results[core]["y_out"],
                                              dtype=np.float32)
    return out


# version 09
# speedup vs baseline: 1.2020x; 1.0018x over previous
import math
import numpy as np
import ml_dtypes
import concourse.bass as bass
import concourse.mybir as mybir
from concourse.bass_utils import run_bass_kernel_spmd

F32 = mybir.dt.float32
BF16 = mybir.dt.bfloat16
AF = mybir.ActivationFunctionType
ALU = mybir.AluOpType
AX = mybir.AxisListType

B, L, E = 2, 8192, 1024
NTOK = 2048
NTC = 16
NH, HD = 16, 64
FF = 4096
LN_EPS = 1e-5
EPS_SC = float(np.float32(1e-6) * np.float32(256.0))
LN_EPS_SC = float(np.float32(np.log(np.float64(EPS_SC))))


class _Ins:
    __slots__ = ("ctx", "ins", "eng", "is_dma")

    def __init__(self, ctx, ins, eng=None, is_dma=False):
        self.ctx, self.ins = ctx, ins
        self.eng, self.is_dma = eng, is_dma

    def inc(self, semname, k=1):
        ctx = self.ctx
        ctx.counts[semname] = ctx.counts.get(semname, 0) + k
        if self.ins is not None:
            self.ins.then_inc(ctx.semh[semname], k)
        if self.is_dma:
            if ctx.live == self.eng:
                ctx.real[self.eng].wait_ge(ctx.semh[semname],
                                           ctx.counts[semname])
        return self

    def incn(self, semname, k=1):
        ctx = self.ctx
        ctx.counts[semname] = ctx.counts.get(semname, 0) + k
        if self.ins is not None:
            self.ins.then_inc(ctx.semh[semname], k)
        return self

    def fence(self, semname):
        ctx = self.ctx
        ctx.counts[semname] = ctx.counts.get(semname, 0) + 1
        if self.ins is not None:
            self.ins.then_inc(ctx.semh[semname], 1)
        if ctx.live == self.eng:
            ctx.real[self.eng].wait_ge(ctx.semh[semname], ctx.counts[semname])
        return self


class _Eng:
    def __init__(self, ctx, name):
        self._ctx, self._name = ctx, name

    def wait(self, markname, delta=0):
        ctx = self._ctx
        if ctx.live == "count":
            return
        sem, val = ctx.marks[markname]
        if ctx.live == self._name:
            ctx.real[self._name].wait_ge(ctx.semh[sem], val + delta)

    def wait_raw(self, semname, val):
        if self._ctx.live == self._name:
            self._ctx.real[self._name].wait_ge(self._ctx.semh[semname], val)

    def __getattr__(self, attr):
        ctx, name = self._ctx, self._name
        is_dma = attr == "dma_start"

        def call(*a, **k):
            if ctx.live == name:
                return _Ins(ctx, getattr(ctx.real[name], attr)(*a, **k),
                            name, is_dma)
            return _Ins(ctx, None, name, is_dma)

        return call


class Ctx:
    ENGINES = ("sp", "gp", "pe", "act", "dve")

    def __init__(self):
        self.marks = {}
        self.counts = {}
        self.live = "count"
        self.real = {}
        self.semh = {}
        self.first_counts = None
        for e in self.ENGINES:
            setattr(self, e, _Eng(self, e))

    def mark(self, markname, semname):
        v = self.counts.get(semname, 0)
        if self.live == "count":
            assert markname not in self.marks, f"dup mark {markname}"
            self.marks[markname] = (semname, v)
        else:
            assert self.marks[markname] == (semname, v), (
                markname, self.marks[markname], (semname, v))

    def begin(self, live):
        if self.live == "count":
            self.first_counts = dict(self.counts)
        self.live = live
        self.counts = {}

    def finish(self):
        assert self.counts == self.first_counts, (
            {k: (self.counts.get(k), self.first_counts.get(k))
             for k in set(self.counts) | set(self.first_counts)
             if self.counts.get(k) != self.first_counts.get(k)})


class Arena:
    def __init__(self, esize):
        self.esize = esize
        self.pending = []
        self.regions = {}
        self.top = 0

    def add(self, name, cols, live):
        self.pending.append((name, cols, frozenset(live)))

    def finalize(self):
        for name, cols, live in sorted(self.pending,
                                       key=lambda t: -t[1]):
            spans = sorted((o, o + cc) for (o, cc, lv) in
                           self.regions.values() if lv & live)
            off = 0
            for (s, e) in spans:
                if off + cols <= s:
                    break
                off = max(off, e)
            self.regions[name] = (off, cols, live)
            self.top = max(self.top, off + cols)

    def view(self, tensor, name, rows=128):
        off, cols, _ = self.regions[name]
        return tensor[0:rows, off:off + cols]

    def view2(self, tensor, name, r0, r1):
        off, cols, _ = self.regions[name]
        return tensor[r0:r1, off:off + cols]


def _hadamard(n):
    h = np.array([[1.0]], dtype=np.float32)
    while h.shape[0] < n:
        h = np.block([[h, h], [h, -h]])
    return h.astype(np.float32)


def _rope_tables(core_id):
    freqs = 10000.0 ** (-np.arange(0, HD, 2, dtype=np.float64) / HD)
    t0 = (core_id % 4) * NTOK
    lpos = t0 + np.arange(NTOK)
    ang = (lpos - 1.0)[:, None] * freqs[None, :]
    cos = np.repeat(np.cos(ang), 2, axis=1)
    sin = np.repeat(np.sin(ang), 2, axis=1)
    z = lpos == 0
    cos[z, :] = 1.0
    sin[z, :] = 0.0
    sinpp = sin.copy()
    sinpp[:, 1::2] *= -1.0
    cosT = np.tile(cos.T, (2, 1)).astype(np.float32)
    sinT = np.tile(sinpp.T, (2, 1)).astype(np.float32)
    return np.ascontiguousarray(cosT), np.ascontiguousarray(sinT)


def build_program(debug=False, stop=10, kstop=4, p1n=NTC, router=3):
    nc = bass.Bass("TRN2", target_bir_lowering=False)

    def din(name, shape, dt=F32):
        return nc.dram_tensor(name, shape, dt, kind="ExternalInput").ap()

    def dnout(name, shape, dt=F32):
        return nc.dram_tensor(name, shape, dt, kind="ExternalOutput").ap()

    x_in = din("x_slice", [NTOK, E])
    wqkvT_in = din("wqkvT", [E, 3 * E])
    woutT_in = din("woutT", [E, E])
    w1_in = din("w1_0", [E, FF])
    w2_in = din("w2_0", [FF, E])
    b1c_in = din("b1c", [128, 32])
    b2row_in = din("b2row", [1, E])
    grow_in = din("grow", [1, E])
    brow_in = din("brow", [1, E])
    gt_in = din("g_tiled", [64, 4096])
    s1_in = din("s1_tiled", [64, 4096])
    s2_in = din("s2_tiled", [64, 4096])
    s3_in = din("s3_tiled", [64, 4096])
    ht_in = din("h_tiled", [64, 4096], BF16)
    hmat_in = din("hmat", [64, 64], BF16)
    hs_in = din("hs", [64, 64], BF16)
    swap_in = din("swapm", [64, 64], BF16)
    i128f_in = din("i128f", [128, 128])
    i128b_in = din("i128b", [128, 128], BF16)
    repl_in = din("repl", [16, 1024], BF16)
    nh2_in = din("neghalf2", [128, 2], BF16)
    ones1r_in = din("ones1r", [1, 128])
    actc_in = din("actconsts", [128, 4])
    rwT_in = din("rwT", [E, 16])
    rb_in = din("rbcol", [16, 1])
    cosT_in = din("cosT", [128, NTOK])
    sinT_in = din("sinT", [128, NTOK])

    y_out = dnout("y_out", [NTOK, E])

    x1_dram = nc.dram_tensor("x1_dram", [NTOK, E], F32).ap()
    ar_in = nc.dram_tensor("ar_in", [65, NH * 256], BF16).ap()
    ar_out = nc.dram_tensor("ar_out", [65, NH * 256], BF16).ap()

    dbg = {}
    if debug:
        for nm, shape, dt in [
            ("d_u1T", [128, 8 * 2048], BF16), ("d_v", [128, NTC * 1040], BF16),
            ("d_spre", [65, 4096], BF16), ("d_ssb", [128, 2080], BF16),
            ("d_nsqk", [128, 256], F32), ("d_den", [16, 2048], F32),
            ("d_f", [16, 2048], F32), ("d_recip", [16, 2048], BF16),
            ("d_oh", [128, 16384], BF16), ("d_u2T", [128, 16384], BF16),
            ("d_kc", [128, 2048], BF16), ("d_nsqq", [128, 256], F32),
            ("d_x1", [NTOK, E], F32),
            ("d_xr", [128, 1024], F32), ("d_xsq", [128, 1024], F32),
            ("d_cols", [128, 32], F32), ("d_u1r", [128, 1024], BF16),
        ]:
            dbg[nm] = dnout(nm, shape, dt)

    ab = Arena(2)
    af = Arena(4)

    ab.add("u1T", 8 * 2048, {1, 2, 3, 4})
    ab.add("wqkv_kv", 8 * 2048, {0, 1, 2})
    ab.add("wqkv_q", 8 * 1024, {0, 1, 2, 3, 4})
    ab.add("Wc", 4096, {0, 1, 2, 3, 4, 5, 6})
    ab.add("Ws", 4096, {0, 1, 2, 3, 4, 5, 6})
    ab.add("Xa", 4096, {0})
    ab.add("Xb", 4096, {0})
    ab.add("gt", 4096, {0})
    ab.add("s1t", 4096, {0})
    ab.add("s2t", 4096, {0})
    ab.add("s3t", 4096, {0})
    ab.add("htl", 4096, {0})
    ab.add("v_sb", NTC * 1040, {2})
    ab.add("kc", 2048, {2})
    ab.add("ks", 2048, {2})
    ab.add("ksq", 2048, {2})
    ab.add("kphi0", 256, {2})
    ab.add("kphi1", 256, {2})
    ab.add("spostb", 4096, {3, 4})
    ab.add("spre", 4096, {2, 3})
    ab.add("S_sb", 32 * 65, {3, 4, 5, 6})
    ab.add("sumk", 512, {3, 4, 5, 6})
    ab.add("qc", 8 * 2048, {3, 4, 5, 6})
    ab.add("qs", 8 * 2048, {3, 4, 5, 6})
    ab.add("qsq", 2048, {3, 4})
    ab.add("Qp0", 4096, {5, 6})
    ab.add("Qp1", 4096, {5, 6})
    ab.add("numer0", 2048, {6})
    ab.add("numer1", 2048, {6})
    ab.add("recipb", 2048, {5, 6})
    ab.add("outheads", 8 * 2048, {6, 7})
    ab.add("wout", 8 * 1024, {7})
    ab.add("u2T", 8 * 2048, {7, 8})
    ab.add("w2full", 32 * 1024, {7, 8})
    ab.add("w1r0", 8 * 512, {8})
    ab.add("w1r1", 8 * 512, {8})
    ab.add("w1r2", 8 * 512, {8})
    ab.add("h_sb", 32 * 512, {8})
    ab.add("rlog", 2048, {8})
    ab.add("u1r0", 1024, {1})
    ab.add("u1r1", 1024, {1})
    ab.add("u2r0", 1024, {7})
    ab.add("u2r1", 1024, {7})

    af.add("cosT", NTOK, {0, 1, 2, 3, 4})
    af.add("sinT", NTOK, {0, 1, 2, 3, 4})
    af.add("gam_bc", 1024, set(range(9)))
    af.add("bet_bc", 1024, set(range(9)))
    af.add("grow", 1024, {0})
    af.add("brow", 1024, {0})
    af.add("b2row", 1024, set(range(9)))
    af.add("b1c", 32, set(range(9)))
    af.add("i128f", 128, set(range(9)))
    af.add("xr0", 1024, {1, 7})
    af.add("xr1", 1024, {1, 7})
    af.add("x1t0", 1024, {7})
    af.add("x1t1", 1024, {7})
    af.add("xsq", 1024, {1, 7, 8})
    af.add("cols0", 32, {1, 7, 8})
    af.add("cols1", 32, {1, 7, 8})
    af.add("ln2cols", 32, {7, 8})
    af.add("nsqk", 256, {2})
    af.add("nsqq", 256, {3, 4})
    af.add("ftm0", 16, {3, 4})
    af.add("ftm1", 16, {3, 4})
    af.add("fcl0", 16, {3, 4})
    af.add("fcl1", 16, {3, 4})
    af.add("f_fm", 2048, {3, 4, 5})

    af.add("den", 2048, {5, 6} if debug else {5})
    af.add("dmx", 2048, {5, 6} if debug else {5})
    af.add("x1r0", 1024, {8})
    af.add("x1r1", 1024, {8})
    af.add("ytile", 1024, {8})
    af.add("yn", 1024, {8})
    af.add("t2c", 1024, {8})
    af.add("x1b", 1024, {8})

    ab.finalize()
    af.finalize()
    cap = 206 * 1024
    need = ab.top * 2 + af.top * 4
    assert need <= cap, (f"SBUF over budget: {need} > {cap} "
                         f"(bf {ab.top * 2}, f32 {af.top * 4})")

    A_BF = nc.alloc_sbuf_tensor("arena_bf", [128, ab.top], BF16)
    A_F = nc.alloc_sbuf_tensor("arena_f32", [128, af.top], F32)

    def rbv(name, rows=128):
        return ab.view(A_BF, name, rows)

    def rfv(name, rows=128):
        return af.view(A_F, name, rows)

    hmat_sb = nc.alloc_sbuf_tensor("hmat_sb", [64, 64], BF16)
    hs_sb = nc.alloc_sbuf_tensor("hs_sb", [64, 64], BF16)
    swap_sb = nc.alloc_sbuf_tensor("swap_sb", [64, 64], BF16)
    i128b_sb = nc.alloc_sbuf_tensor("i128b_sb", [128, 128], BF16)
    repl_sb = nc.alloc_sbuf_tensor("repl_sb", [16, 1024], BF16)
    nh2_sb = nc.alloc_sbuf_tensor("nh2_sb", [128, 2], BF16)
    ones1r_sb = nc.alloc_sbuf_tensor("ones1r_sb", [1, 128], F32)
    actc_sb = nc.alloc_sbuf_tensor("actc_sb", [128, 4], F32)
    rwT_sb = nc.alloc_sbuf_tensor("rwT_sb", [128, 128], BF16)
    rb_sb = nc.alloc_sbuf_tensor("rb_sb", [16, 1], F32)
    rlog_sb = ab.view(A_BF, "rlog", 16)

    PB = [nc.alloc_psum_tensor(f"pb{i}", [128, 512], F32) for i in range(6)]
    PT = [nc.alloc_psum_tensor(f"pt{i}", [128, 512], BF16) for i in range(2)]
    assert nc.psum_banks_remaining >= 0

    ctx = Ctx()

    def program(c):
        sp, gp, pe, act, dve = c.sp, c.gp, c.pe, c.act, c.dve
        ln_prev = []

        for d, s in [
            (rfv("cosT"), cosT_in), (rfv("sinT"), sinT_in),
            (rfv("grow", 1), grow_in), (rfv("brow", 1), brow_in),
            (rfv("b2row", 1), b2row_in), (rfv("b1c"), b1c_in),
            (rfv("i128f"), i128f_in),
            (hmat_sb[:, :], hmat_in), (hs_sb[:, :], hs_in),
            (swap_sb[:, :], swap_in), (i128b_sb[:, :], i128b_in),
            (repl_sb[:, :], repl_in), (nh2_sb[:, :], nh2_in),
            (ones1r_sb[:, :], ones1r_in),
            (actc_sb[:, :], actc_in),
            (rb_sb[:, :], rb_in),
            (rbv("htl", 64), ht_in),
        ]:
            sp.dma_start(out=d, in_=s).incn("ld", 16)
        sp.wait_raw("ld", 0)
        c.mark("consts_loaded", "ld")
        sp.wait("consts_loaded")

        for nm, src in [("gt", gt_in), ("s1t", s1_in),
                        ("s2t", s2_in), ("s3t", s3_in)]:
            gp.dma_start(out=rbv(nm, 64), in_=src).incn("wd", 16)
        rwv = rwT_in.rearrange("(cp p) e -> p cp e", p=128)
        gp.dma_start(out=rwT_sb[:, :].rearrange("p (cp e) -> p cp e", e=16),
                     in_=rwv).incn("wd", 16)
        c.mark("tiled_loaded", "wd")
        gp.wait("tiled_loaded")
        kvv = wqkvT_in[:, 1024:3072].rearrange("(c p) o -> p c o", p=128)
        kvd = rbv("wqkv_kv").rearrange("p (c o) -> p c o", o=2048)
        gp.dma_start(out=kvd, in_=kvv).inc("wd", 16)
        c.mark("wqkv_kv_loaded", "wd")
        qv = wqkvT_in[:, 0:1024].rearrange("(c p) o -> p c o", p=128)
        qd = rbv("wqkv_q").rearrange("p (c o) -> p c o", o=1024)
        gp.dma_start(out=qd, in_=qv).inc("wd", 16)
        c.mark("wqkv_q_loaded", "wd")

        pe.wait("consts_loaded")
        for i, rowname in enumerate(["grow", "brow"]):
            for half in range(2):
                k = i * 2 + half
                if k >= 2:
                    pe.wait(f"gb_ev_{k - 2}")
                pe.matmul(PB[2 + (k % 2)][:, :], ones1r_sb[:, :],
                          rfv(rowname, 1)[:, half * 512:(half + 1) * 512],
                          start=True, stop=True).inc("pe", 1)
                c.mark(f"gb_mm_{k}", "pe")
                dve.wait(f"gb_mm_{k}")
                dst = rfv("gam_bc" if i == 0 else "bet_bc")
                dve.tensor_copy(dst[:, half * 512:(half + 1) * 512],
                                PB[2 + (k % 2)][:, :]).inc("dve", 1)
                c.mark(f"gb_ev_{k}", "dve")
        c.mark("gb_done", "dve")

        dve.wait("tiled_loaded")
        dve.wait("consts_loaded")
        dve.tensor_mul(rbv("Xa", 64), rbv("htl", 64),
                       rbv("gt", 64)).fence("dve")
        dve.tensor_mul(rbv("Xb", 64), rbv("Xa", 64), rbv("s3t", 64)).inc("dve", 1)
        c.mark("wpre_x2", "dve")

        def wpre_stage(tag, prev_done, lhsT, src_name, scale_name, dst_name):
            for j in range(8):
                pe.wait(prev_done if j < 2 else f"{tag}_ev_{j - 2}")
                pe.matmul(PB[j % 2][0:64, :], lhsT,
                          rbv(src_name, 64)[:, j * 512:(j + 1) * 512],
                          start=True, stop=True).inc("pe", 1)
                c.mark(f"{tag}_mm_{j}", "pe")
                dve.wait(f"{tag}_mm_{j}")
                if scale_name is None:
                    dve.tensor_copy(rbv(dst_name, 64)[:, j * 512:(j + 1) * 512],
                                    PB[j % 2][0:64, :]).inc("dve", 1)
                else:
                    dve.tensor_mul(rbv(dst_name, 64)[:, j * 512:(j + 1) * 512],
                                   PB[j % 2][0:64, :],
                                   rbv(scale_name, 64)[:, j * 512:(j + 1) * 512]
                                   ).inc("dve", 1)
                c.mark(f"{tag}_ev_{j}", "dve")
            c.mark(f"{tag}_done", "dve")

        wpre_stage("wx3", "wpre_x2", hmat_sb[:, :], "Xb", "s2t", "Xa")
        wpre_stage("wx5", "wx3_done", hs_sb[:, :], "Xa", "s1t", "Wc")
        wpre_stage("wx7", "wx5_done", swap_sb[:, :], "Wc", None, "Ws")
        c.mark("wpre_done", "dve")
        sp.wait("wpre_done")
        sp.dma_start(out=ab.view2(A_BF, "Wc", 64, 128),
                     in_=rbv("Wc", 64)).inc("ld", 16)
        sp.dma_start(out=ab.view2(A_BF, "Ws", 64, 128),
                     in_=rbv("Ws", 64)).inc("ld", 16)
        c.mark("wrep_done", "ld")

        def ln_tile(tag, x_ap, out_ap, parity=0, save_cols=None):
            cols = rfv("cols0" if parity == 0 else "cols1")
            if len(ln_prev) >= 2:
                act.wait(f"lnout_{ln_prev[-2]}")
            act.activation(rfv("xsq"), x_ap, AF.Square,
                           accum_out=cols[:, 0:1]).inc("act", 1)
            c.mark(f"lnsq_{tag}", "act")
            dve.wait(f"lnsq_{tag}")
            dve.reduce_sum(cols[:, 4:5], x_ap, axis=AX.X).inc("dve", 1)
            c.mark(f"lnred_{tag}", "dve")
            act.wait(f"lnred_{tag}")
            act.mul(cols[:, 12:13], cols[:, 4:5], -1.0 / 1024.0)
            act.mul(cols[:, 8:9], cols[:, 4:5], 1.0 / 1024.0).fence("act")
            act.activation(cols[:, 16:17], cols[:, 8:9],
                           AF.Square).inc("act", 1)
            c.mark(f"lnmid_{tag}", "act")
            dve.wait(f"lnmid_{tag}")
            dve.scalar_tensor_tensor(cols[:, 20:21], cols[:, 0:1], 1.0 / 1024.0,
                                     cols[:, 16:17], ALU.mult,
                                     ALU.subtract).inc("dve", 1)
            c.mark(f"lnvar_{tag}", "dve")
            act.wait(f"lnvar_{tag}")
            act.activation(cols[:, 24:25], cols[:, 20:21], AF.Ln,
                           bias=actc_sb[:, 0:1]).fence("act")
            act.activation(cols[:, 28:29], cols[:, 24:25], AF.Exp,
                           scale=actc_sb[:, 1:2]).inc("act", 1)
            c.mark(f"lnrstd_{tag}", "act")
            dve.wait(f"lnrstd_{tag}")
            if save_cols is not None:
                dve.tensor_copy(save_cols[:, 0:1], cols[:, 12:13])
                dve.tensor_copy(save_cols[:, 1:2], cols[:, 28:29])
            dve.tensor_scalar(out_ap, x_ap, cols[:, 12:13], cols[:, 28:29],
                              ALU.add, ALU.mult).inc("dve", 1)
            c.mark(f"lnout_{tag}", "dve")
            ln_prev.append(tag)

        def u_transpose(tag, u_tm, uT3, tc):
            for bat in range(2):
                gidx = tc * 2 + bat
                pe.wait(f"lnout_{tag}_{tc}")
                if gidx >= 2:
                    pe.wait(f"{tag}_trev_{gidx - 2}")
                for dc in range(4):
                    d = bat * 4 + dc
                    ins = pe.transpose(PT[gidx % 2][:, dc * 128:(dc + 1) * 128],
                                 u_tm[:, d * 128:(d + 1) * 128],
                                 i128b_sb[:, :])
                    if dc == 3:
                        ins.inc("pe", 1)
                c.mark(f"{tag}_tr_{gidx}", "pe")
                dve.wait(f"{tag}_tr_{gidx}")
                src = PT[gidx % 2][:, :].rearrange("p (d t) -> p d t", t=128)
                dve.tensor_copy(uT3[:, bat * 4:(bat + 1) * 4,
                                    tc * 128:(tc + 1) * 128],
                                src).inc("dve", 1)
                c.mark(f"{tag}_trev_{gidx}", "dve")

        if stop >= 1:
            u1T3 = rbv("u1T").rearrange("p (d t) -> p d t", t=2048)
            for tc in range(p1n):
                xr = rfv("xr0") if tc % 2 == 0 else rfv("xr1")
                if tc >= 2:
                    sp.wait(f"lnout_ln1_{tc - 2}")
                sp.dma_start(out=xr, in_=x_in[tc * 128:(tc + 1) * 128, :]).inc("ld", 16)
                c.mark(f"x_tile_{tc}", "ld")
                act.wait(f"x_tile_{tc}")
                dve.wait(f"x_tile_{tc}")
                if tc >= 2:
                    dve.wait(f"ln1_tr_{(tc - 2) * 2 + 1}")
                u1r = rbv("u1r0") if tc % 2 == 0 else rbv("u1r1")
                ln_tile(f"ln1_{tc}", xr, u1r, parity=tc % 2)
                u_transpose("ln1", u1r, u1T3, tc)
            c.mark("u1T_done", "dve")
            if debug:
                sp.wait("u1T_done")
                sp.dma_start(out=dbg["d_u1T"], in_=rbv("u1T")).inc("st", 16)
                if p1n <= 2:
                    sp.dma_start(out=dbg["d_xr"], in_=rfv("xr0")).inc("st", 16)
                    sp.dma_start(out=dbg["d_xsq"], in_=rfv("xsq")).inc("st", 16)
                    sp.dma_start(out=dbg["d_cols"], in_=rfv("cols")).inc("st", 16)
                    sp.dma_start(out=dbg["d_u1r"], in_=rbv("u1r0")).inc("st", 16)
                c.mark("dbg_u1t", "st")

        if stop >= 2:
            v_sb = rbv("v_sb")
            v4 = v_sb.rearrange("p (t h d) -> p t h d", h=16, d=65)
            dve.memset(v4[:, :, :, 64:65], 1.0).inc("dve", 1)
            c.mark("v_ones", "dve")
            u1Tv = u1T3
            pe.wait("u1T_done")
            pe.wait("wqkv_kv_loaded")
            for tc in range(NTC):
                for half in range(2):
                    i = tc * 2 + half
                    if i >= 2:
                        pe.wait(f"v_ev_{i - 2}")
                    for dc in range(8):
                        ins = pe.matmul(PB[i % 2][:, :],
                                  u1Tv[:, dc, tc * 128:(tc + 1) * 128],
                                  rbv("wqkv_kv")[:, dc * 2048 + 1024 + half * 512:
                                                 dc * 2048 + 1024 + (half + 1) * 512],
                                  start=(dc == 0), stop=(dc == 7))
                        if dc == 7:
                            ins.inc("pe", 1)
                    c.mark(f"v_mm_{i}", "pe")
                    dve.wait(f"v_mm_{i}")
                    dve.wait("v_ones")
                    dst = v_sb[:, tc * 1040 + half * 520:
                               tc * 1040 + (half + 1) * 520]
                    dst3 = dst.rearrange("p (h d) -> p h d", d=65)[:, :, 0:64]
                    src3 = PB[i % 2][:, :].rearrange("p (h d) -> p h d", d=64)
                    dve.tensor_copy(dst3, src3).inc("dve", 1)
                    c.mark(f"v_ev_{i}", "dve")
            c.mark("v_done", "dve")

        if stop >= 3:
            kc, ks, ksq = rbv("kc"), rbv("ks"), rbv("ksq")
            nsqk = rfv("nsqk")
            nsqk3 = nsqk.rearrange("p (t h) -> p t h", h=16)
            for cc in range(8):
                for nt in range(4):
                    i = cc * 4 + nt
                    if i >= 2:
                        pe.wait(f"k_rope_{i - 2}")
                        pe.wait(f"k_sq_{i - 2}")
                    else:
                        pe.wait(f"v_ev_{30 + i}")
                    for dc in range(8):
                        ins = pe.matmul(PB[i % 2][:, :],
                                  rbv("wqkv_kv")[:, dc * 2048 + cc * 128:
                                                 dc * 2048 + (cc + 1) * 128],
                                  u1Tv[:, dc, nt * 512:(nt + 1) * 512],
                                  start=(dc == 0), stop=(dc == 7))
                        if dc == 7:
                            ins.inc("pe", 1)
                    c.mark(f"k_mm_{i}", "pe")
                    dve.wait(f"k_mm_{i}")
                    if cc > 0 and nt == 0 and kstop >= 4:
                        dve.wait(f"s_acc_{cc - 1}")
                    elif cc > 0 and nt == 0 and kstop == 3:
                        dve.wait(f"k_phi_{cc - 1}_31")
                    dve.tensor_mul(kc[:, nt * 512:(nt + 1) * 512], PB[i % 2][:, :],
                                   rfv("cosT")[:, nt * 512:(nt + 1) * 512])
                    dve.tensor_mul(ks[:, nt * 512:(nt + 1) * 512], PB[i % 2][:, :],
                                   rfv("sinT")[:, nt * 512:(nt + 1) * 512]
                                   ).inc("dve", 1)
                    c.mark(f"k_rope_{i}", "dve")
                    act.wait(f"k_mm_{i}")
                    act.wait(f"k_rope_{i}")
                    if cc > 0 and nt == 0 and kstop >= 2:
                        act.wait(f"nsq_mm_k_{cc - 1}")
                    act.activation(ksq[:, nt * 512:(nt + 1) * 512], PB[i % 2][:, :],
                                   AF.Square).inc("act", 1)
                    c.mark(f"k_sq_{i}", "act")
                if kstop >= 2:
                    pe.wait(f"k_sq_{cc * 4 + 3}")
                    if cc > 0:
                        pe.wait(f"nsq_ev_k_{cc - 1}")
                    for tc in range(NTC):
                        ins = pe.matmul(PB[4][:, 2 * tc:2 * tc + 2],
                                  ksq[:, tc * 128:(tc + 1) * 128], nh2_sb[:, :],
                                  start=True, stop=True)
                        if tc == NTC - 1:
                            ins.inc("pe", 1)
                    c.mark(f"nsq_mm_k_{cc}", "pe")
                    dve.wait(f"nsq_mm_k_{cc}")
                    dve.tensor_copy(
                        nsqk3[:, :, 2 * cc:2 * cc + 2],
                        PB[4][:, 0:32].rearrange("p (t h) -> p t h", h=2)).inc("dve", 1)
                    c.mark(f"nsq_ev_k_{cc}", "dve")
                if cc == 0:
                    pe.wait("wrep_done")
                    pe.wait("v_done")
                def k_smm(jj):
                    tcj, h2j = jj // 2, jj % 2
                    hj = 2 * cc + h2j
                    pe.wait(f"k_exp_{cc}_{jj}")
                    if jj == 0 and cc > 0:
                        pe.wait(f"s_ev_{cc - 1}")
                    kphi_j = rbv("kphi0" if jj % 2 == 0 else "kphi1")
                    pe.matmul(PB[5][0:65, h2j * 256:(h2j + 1) * 256],
                              v_sb[:, tcj * 1040 + 65 * hj:
                                   tcj * 1040 + 65 * hj + 65],
                              kphi_j, start=(tcj == 0), stop=(tcj == 15),
                              skip_group_check=True).inc("pe", 1)
                    c.mark(f"k_smm_{cc}_{jj}", "pe")

                for tc in (range(NTC) if kstop >= 3 else []):
                    for h2 in range(2):
                        j = tc * 2 + h2
                        h = 2 * cc + h2
                        if j >= 2:
                            pe.wait(f"k_exp_{cc}_{j - 2}")
                        elif cc > 0:
                            pe.wait(f"k_exp_{cc - 1}_{30 + j}")
                        hb = h2 * 64
                        pe.matmul(PB[2 + (j % 2)][:, 0:256],
                                  kc[hb:hb + 64, tc * 128:(tc + 1) * 128],
                                  ab.view2(A_BF, "Wc", hb, hb + 64)
                                  [:, h * 256:(h + 1) * 256],
                                  start=True, stop=False)
                        pe.matmul(PB[2 + (j % 2)][:, 0:256],
                                  ks[hb:hb + 64, tc * 128:(tc + 1) * 128],
                                  ab.view2(A_BF, "Ws", hb, hb + 64)
                                  [:, h * 256:(h + 1) * 256],
                                  start=False, stop=True).inc("pe", 1)
                        c.mark(f"k_phi_{cc}_{j}", "pe")
                        act.wait(f"k_phi_{cc}_{j}")
                        act.wait(f"nsq_ev_k_{cc}")
                        if kstop >= 4 and j >= 2:
                            act.wait(f"k_smm_{cc}_{j - 2}")
                        elif kstop >= 4 and cc > 0:
                            act.wait(f"k_smm_{cc - 1}_{30 + j}")
                        kphi = rbv("kphi0" if j % 2 == 0 else "kphi1")
                        act.activation(kphi, PB[2 + (j % 2)][:, 0:256], AF.Exp,
                                       bias=nsqk[:, tc * 16 + h:tc * 16 + h + 1]
                                       ).inc("act", 1)
                        c.mark(f"k_exp_{cc}_{j}", "act")
                        if kstop >= 4 and j >= 1:
                            k_smm(j - 1)
                if kstop >= 4:
                    k_smm(31)
                    c.mark(f"s_acc_{cc}", "pe")
                    dve.wait(f"s_acc_{cc}")
                    dve.tensor_copy(rbv("spre", 65)[:, cc * 512:(cc + 1) * 512],
                                    PB[5][0:65, :]).inc("dve", 1)
                    c.mark(f"s_ev_{cc}", "dve")
            c.mark("spre_done", "dve")

        if stop >= 4:
            gp.wait("spre_done")
            gp.dma_start(out=ar_in, in_=rbv("spre", 65)).inc("wd", 16)
            c.mark("ar_staged", "wd")
            gp.wait("ar_staged")
            gp.collective_compute(
                "AllReduce", ALU.add,
                replica_groups=[[0, 1, 2, 3], [4, 5, 6, 7]],
                ins=[ar_in], outs=[ar_out]).inc("cc", 1)
            gp.wait_raw("cc", 1)
            gp.dma_start(out=rbv("spostb", 65), in_=ar_out).inc("wd", 16)
            c.mark("spost_loaded", "wd")
            if stop >= 9:
                gp.wait("oh_done")
                wov = woutT_in.rearrange("(c p) o -> p c o", p=128)
                gp.dma_start(out=rbv("wout").rearrange("p (c o) -> p c o",
                                                       o=1024),
                             in_=wov).inc("wd", 16)
                c.mark("wout_loaded", "wd")
            if stop >= 10:
                w2v = w2_in.rearrange("(c p) d -> p c d", p=128)
                gp.dma_start(out=rbv("w2full").rearrange("p (c d) -> p c d",
                                                         d=1024),
                             in_=w2v).inc("wd", 16)
                c.mark("w2_loaded", "wd")

        if stop >= 3:
            if debug:
                sp.wait("v_done")
                sp.dma_start(out=dbg["d_v"], in_=v_sb).inc("st", 16)
                c.mark("dbg_v", "st")
                sp.wait("spre_done")
                sp.dma_start(out=dbg["d_kc"], in_=kc).inc("st", 16)
                sp.dma_start(out=dbg["d_spre"], in_=rbv("spre", 65)).inc("st", 16)
                sp.dma_start(out=dbg["d_nsqk"], in_=nsqk).inc("st", 16)
                c.mark("dbg_p2", "st")

        if stop >= 5:
            qc3 = rbv("qc").rearrange("p (cc t) -> p cc t", t=2048)
            qs3 = rbv("qs").rearrange("p (cc t) -> p cc t", t=2048)
            qsq = rbv("qsq")
            nsqq = rfv("nsqq")
            nsqq3 = nsqq.rearrange("p (t h) -> p t h", h=16)
            pe.wait("wqkv_q_loaded")
            if debug:
                pe.wait("dbg_v")
                pe.wait("dbg_p2")
            for cc in range(8):
                for nt in range(4):
                    i = cc * 4 + nt
                    if i >= 2:
                        pe.wait(f"q_rope_{i - 2}")
                        pe.wait(f"q_sq_{i - 2}")
                    else:
                        pe.wait(f"k_rope_{30 + i}")
                        pe.wait(f"k_sq_{30 + i}")
                    for dc in range(8):
                        ins = pe.matmul(PB[i % 2][:, :],
                                  rbv("wqkv_q")[:, dc * 1024 + cc * 128:
                                                dc * 1024 + (cc + 1) * 128],
                                  u1Tv[:, dc, nt * 512:(nt + 1) * 512],
                                  start=(dc == 0), stop=(dc == 7))
                        if dc == 7:
                            ins.inc("pe", 1)
                    c.mark(f"q_mm_{i}", "pe")
                    dve.wait(f"q_mm_{i}")
                    dve.tensor_mul(qc3[:, cc, nt * 512:(nt + 1) * 512],
                                   PB[i % 2][:, :],
                                   rfv("cosT")[:, nt * 512:(nt + 1) * 512])
                    dve.tensor_mul(qs3[:, cc, nt * 512:(nt + 1) * 512],
                                   PB[i % 2][:, :],
                                   rfv("sinT")[:, nt * 512:(nt + 1) * 512]
                                   ).inc("dve", 1)
                    c.mark(f"q_rope_{i}", "dve")
                    act.wait(f"q_mm_{i}")
                    act.wait(f"q_rope_{i}")
                    if cc > 0 and nt == 0:
                        act.wait(f"nsq_mm_q_{cc - 1}")
                    act.activation(qsq[:, nt * 512:(nt + 1) * 512], PB[i % 2][:, :],
                                   AF.Square).inc("act", 1)
                    c.mark(f"q_sq_{i}", "act")
                pe.wait(f"q_sq_{cc * 4 + 3}")
                pe.wait(f"nsq_ev_q_{cc - 1}" if cc > 0 else "nsq_ev_k_7")
                for tc in range(NTC):
                    ins = pe.matmul(PB[4][:, 2 * tc:2 * tc + 2],
                              qsq[:, tc * 128:(tc + 1) * 128], nh2_sb[:, :],
                              start=True, stop=True)
                    if tc == NTC - 1:
                        ins.inc("pe", 1)
                c.mark(f"nsq_mm_q_{cc}", "pe")
                dve.wait(f"nsq_mm_q_{cc}")
                dve.tensor_copy(
                    nsqq3[:, :, 2 * cc:2 * cc + 2],
                    PB[4][:, 0:32].rearrange("p (t h) -> p t h", h=2)).inc("dve", 1)
                c.mark(f"nsq_ev_q_{cc}", "dve")
            c.mark("qside_done", "dve")

            act.wait("qside_done")
            dve.wait("qside_done")
            for tc in range(NTC):
                ftm = rfv("ftm0") if tc % 2 == 0 else rfv("ftm1")
                fcl = rfv("fcl0") if tc % 2 == 0 else rfv("fcl1")
                if tc >= 2:
                    dve.wait(f"f_tm_{tc - 2}")
                dve.tensor_scalar(fcl, nsqq[:, tc * 16:(tc + 1) * 16],
                                  -1.0, 85.0, ALU.mult,
                                  ALU.min).inc("dve", 1)
                c.mark(f"f_cl_{tc}", "dve")
                act.wait(f"f_cl_{tc}")
                if tc >= 2:
                    act.wait(f"f_tr_{tc - 2}")
                act.activation(ftm, fcl, AF.Exp,
                               bias=actc_sb[:, 3:4]).inc("act", 1)
                c.mark(f"f_tm_{tc}", "act")
                pe.wait(f"f_tm_{tc}")
                if tc >= 2:
                    pe.wait(f"f_ev_{tc - 2}")
                else:
                    pe.wait("nsq_ev_q_7")
                    pe.wait("spre_done")
                pe.transpose(PB[4 + (tc % 2)][0:16, 0:128], ftm,
                             rfv("i128f")).inc("pe", 1)
                c.mark(f"f_tr_{tc}", "pe")
                dve.wait(f"f_tr_{tc}")
                if tc < 2:
                    dve.wait("ar_staged")
                dve.tensor_copy(rfv("f_fm", 16)[:, tc * 128:(tc + 1) * 128],
                                PB[4 + (tc % 2)][0:16, 0:128]).inc("dve", 1)
                c.mark(f"f_ev_{tc}", "dve")
            c.mark("f_done", "dve")

        if stop >= 6:
            pe.wait("spost_loaded")
            pe.wait("u1T_done")
            for h in range(NH):
                for half in range(2):
                    i = h * 2 + half
                    if i >= 2:
                        pe.wait(f"spost_ev_{i - 2}")
                    pe.transpose(PT[i % 2][:, 0:65],
                                 rbv("spostb", 65)[:, h * 256 + half * 128:
                                                   h * 256 + (half + 1) * 128],
                                 i128b_sb[0:65, 0:65]).inc("pe", 1)
                    c.mark(f"spost_tr_{i}", "pe")
                    dve.wait(f"spost_tr_{i}")
                    dve.tensor_copy(rbv("S_sb")[:, i * 65:(i + 1) * 65],
                                    PT[i % 2][:, 0:65]).inc("dve", 1)
                    c.mark(f"spost_ev_{i}", "dve")
            dve.wait("spost_ev_31")
            dve.memset(rbv("sumk"), 0.0).fence("dve")
            for h in range(NH):
                for half in range(2):
                    i = h * 2 + half
                    ins = dve.tensor_copy(rbv("sumk")[:, i * 16 + h:i * 16 + h + 1],
                                          rbv("S_sb")[:, i * 65 + 64:i * 65 + 65])
                    if i == 31:
                        ins.inc("dve", 1)
            c.mark("sumk_done", "dve")
            if debug:
                sp.wait("sumk_done")
                sp.dma_start(out=dbg["d_ssb"], in_=rbv("S_sb")).inc("st", 16)
                sp.dma_start(out=dbg["d_nsqq"], in_=nsqq).inc("st", 16)
                sp.wait("f_done")
                sp.dma_start(out=dbg["d_f"], in_=rfv("f_fm", 16)).inc("st", 16)
                c.mark("dbg_s", "st")

        if stop >= 7:
            def phi_q_tile(tag, h, mc, nt, jg):
                cchunk, hr = h // 2, (h % 2) * 64
                if jg >= 2:
                    pe.wait(f"{tag}_exp_{jg - 2}")
                pe.matmul(PB[jg % 2][:, :],
                          ab.view2(A_BF, "Wc", hr, hr + 64)
                          [:, h * 256 + mc * 128:h * 256 + (mc + 1) * 128],
                          qc3[hr:hr + 64, cchunk, nt * 512:(nt + 1) * 512],
                          start=True, stop=False)
                pe.matmul(PB[jg % 2][:, :],
                          ab.view2(A_BF, "Ws", hr, hr + 64)
                          [:, h * 256 + mc * 128:h * 256 + (mc + 1) * 128],
                          qs3[hr:hr + 64, cchunk, nt * 512:(nt + 1) * 512],
                          start=False, stop=True).inc("pe", 1)
                c.mark(f"{tag}_phi_{jg}", "pe")
                act.wait(f"{tag}_phi_{jg}")
                qp = rbv("Qp0" if h % 2 == 0 else "Qp1")
                act.activation(qp[:, mc * 2048 + nt * 512:mc * 2048 + (nt + 1) * 512],
                               PB[jg % 2][:, :], AF.Exp).inc("act", 1)
                c.mark(f"{tag}_exp_{jg}", "act")

            jg = 0
            pe.wait("qside_done")
            pe.wait("sumk_done")
            pe.wait("f_done")

            def a0_den(hh):
                qph = rbv("Qp0" if hh % 2 == 0 else "Qp1")
                pe.wait(f"a0_exp_{(hh + 1) * 8 - 1}")
                for mc in range(2):
                    for nt in range(4):
                        ins = pe.matmul(PB[2 + nt][0:16, :],
                                  rbv("sumk")[:, (hh * 2 + mc) * 16:
                                              (hh * 2 + mc + 1) * 16],
                                  qph[:, mc * 2048 + nt * 512:
                                      mc * 2048 + (nt + 1) * 512],
                                  start=(hh == 0 and mc == 0),
                                  stop=(hh == NH - 1 and mc == 1),
                                  skip_group_check=True)
                        if mc == 1 and nt == 3:
                            ins.inc("pe", 1)
                c.mark(f"a0_den_{hh}", "pe")

            for h in range(NH):
                if h >= 2:
                    act.wait(f"a0_den_{h - 2}")
                for mc in range(2):
                    for nt in range(4):
                        phi_q_tile("a0", h, mc, nt, jg)
                        jg += 1
                if h >= 1:
                    a0_den(h - 1)
            a0_den(NH - 1)
            c.mark("a0_done", "pe")
            dve.wait("a0_done")
            for nt in range(4):
                ins = dve.tensor_copy(
                    rfv("den", 16)[:, nt * 512:(nt + 1) * 512],
                    PB[2 + nt][0:16, :])
                ins.fence("dve") if nt == 3 else ins.inc("dve", 1)
            dve.tensor_max(rfv("dmx", 16), rfv("den", 16),
                           rfv("f_fm", 16)).inc("dve", 1)
            c.mark("recip_f1", "dve")
            dve.wait("recip_f1")
            dve.reciprocal(rfv("den", 16), rfv("dmx", 16)).inc("dve", 1)
            c.mark("recip_f2", "dve")
            dve.wait("recip_f2")
            dve.tensor_copy(rbv("recipb", 16), rfv("den", 16)).inc("dve", 1)
            c.mark("recip_done", "dve")
            if debug:
                sp.wait("recip_done")
                sp.dma_start(out=dbg["d_den"], in_=rfv("dmx", 16)).inc("st", 16)
                sp.dma_start(out=dbg["d_recip"], in_=rbv("recipb", 16)).inc("st", 16)
                c.mark("dbg_recip", "st")

        if stop >= 8:
            jg = 0

            def a1_tail(hh):
                qph = rbv("Qp0" if hh % 2 == 0 else "Qp1")
                numer = rbv("numer0" if hh % 2 == 0 else "numer1")
                pe.wait(f"a1_exp_{(hh + 1) * 8 - 1}")
                if hh == 0:
                    pe.wait("recip_done")
                for nt in range(4):
                    gidx = hh * 4 + nt
                    if nt >= 2:
                        pe.wait(f"a1_nev_{hh}_{nt - 2}")
                    elif hh >= 1:
                        pe.wait(f"a1_nev_{hh - 1}_{nt + 2}")
                    for half in range(2):
                        ins = pe.matmul(PB[2 + (nt % 2)][0:65, :],
                                  rbv("S_sb")[:, (hh * 2 + half) * 65:
                                              (hh * 2 + half + 1) * 65],
                                  qph[:, half * 2048 + nt * 512:
                                      half * 2048 + (nt + 1) * 512],
                                  start=(half == 0),
                                  stop=(half == 1))
                        if half == 1:
                            ins.inc("pe", 1)
                    c.mark(f"a1_nmm_{hh}_{nt}", "pe")
                    act.wait(f"a1_nmm_{hh}_{nt}")
                    if hh >= 2:
                        act.wait(f"a1_oh_{(hh - 2) * 4 + nt}")
                    act.activation(numer[0:64, nt * 512:(nt + 1) * 512],
                                   PB[2 + (nt % 2)][0:64, :],
                                   AF.Copy).inc("act", 1)
                    c.mark(f"a1_nev_{hh}_{nt}", "act")
                    if gidx >= 2:
                        pe.wait(f"a1_oh_{gidx - 2}")
                    pe.matmul(PB[4 + (nt % 2)][0:64, :],
                              repl_sb[:, hh * 64:(hh + 1) * 64],
                              rbv("recipb", 16)[:, nt * 512:(nt + 1) * 512],
                              start=True, stop=True).inc("pe", 1)
                    c.mark(f"a1_bc_{hh}_{nt}", "pe")
                    dve.wait(f"a1_nev_{hh}_{nt}")
                    dve.wait(f"a1_bc_{hh}_{nt}")
                    hr = (hh % 2) * 64
                    dve.tensor_mul(
                        rbv("outheads")[hr:hr + 64,
                                        (hh // 2) * 2048 + nt * 512:
                                        (hh // 2) * 2048 + (nt + 1) * 512],
                        PB[4 + (nt % 2)][0:64, :],
                        numer[0:64, nt * 512:(nt + 1) * 512]).inc("dve", 1)
                    c.mark(f"a1_oh_{gidx}", "dve")

            for h in range(NH):
                if h >= 2:
                    act.wait(f"a1_nmm_{h - 2}_3")
                else:
                    act.wait(f"a0_den_{14 + h}")
                for mc in range(2):
                    for nt in range(4):
                        phi_q_tile("a1", h, mc, nt, jg)
                        jg += 1
                if h >= 1:
                    a1_tail(h - 1)
            a1_tail(NH - 1)
            c.mark("oh_done", "dve")
            if debug:
                sp.wait("oh_done")
                sp.dma_start(out=dbg["d_oh"], in_=rbv("outheads")).inc("st", 16)
                c.mark("dbg_oh", "st")

        if stop >= 9:
            oh3 = rbv("outheads").rearrange("p (e t) -> p e t", t=2048)
            u2T3 = rbv("u2T").rearrange("p (d t) -> p d t", t=2048)
            pe.wait("wout_loaded")
            for tc in range(NTC):
                xr = rfv("xr0") if tc % 2 == 0 else rfv("xr1")
                if tc >= 2:
                    sp.wait(f"x1_half_{(tc - 2) * 2 + 1}")
                else:
                    sp.wait(f"lnout_ln1_{14 + tc}")
                    sp.wait("oh_done")
                sp.dma_start(out=xr,
                             in_=x_in[tc * 128:(tc + 1) * 128, :]).inc("ld", 16)
                c.mark(f"x2_tile_{tc}", "ld")
                x1t = rfv("x1t0") if tc % 2 == 0 else rfv("x1t1")
                for ont in range(2):
                    j = tc * 2 + ont
                    if j == 0:
                        pe.wait("oh_done")
                    if j >= 2:
                        pe.wait(f"x1_half_{j - 2}")
                    for ec in range(8):
                        ins = pe.matmul(PB[j % 2][:, :],
                                  oh3[:, ec, tc * 128:(tc + 1) * 128],
                                  rbv("wout")[:, ec * 1024 + ont * 512:
                                              ec * 1024 + (ont + 1) * 512],
                                  start=(ec == 0), stop=(ec == 7))
                        if ec == 7:
                            ins.inc("pe", 1)
                    c.mark(f"att_mm_{j}", "pe")
                    dve.wait(f"att_mm_{j}")
                    dve.wait(f"x2_tile_{tc}")
                    if tc >= 2 and ont == 0:
                        dve.wait(f"x1_spill_{tc - 2}")
                        dve.wait(f"ln2_tr_{(tc - 2) * 2 + 1}")
                    dve.tensor_add(x1t[:, ont * 512:(ont + 1) * 512],
                                   PB[j % 2][:, :],
                                   xr[:, ont * 512:(ont + 1) * 512]).inc("dve", 1)
                    c.mark(f"x1_half_{j}", "dve")
                sp.wait(f"x1_half_{tc * 2 + 1}")
                sp.dma_start(out=x1_dram[tc * 128:(tc + 1) * 128, :],
                             in_=x1t).inc("ld", 16)
                c.mark(f"x1_spill_{tc}", "ld")
                act.wait(f"x1_half_{tc * 2 + 1}")
                u2r = rbv("u2r0") if tc % 2 == 0 else rbv("u2r1")
                if tc >= 2:
                    dve.wait(f"ln2_tr_{(tc - 2) * 2 + 1}")
                ln_tile(f"ln2_{tc}", x1t, u2r, parity=tc % 2,
                        save_cols=rfv("ln2cols")[:, 2 * tc:2 * tc + 2])
                u_transpose("ln2", u2r, u2T3, tc)
            c.mark("u2T_done", "dve")
            if debug:
                sp.wait("u2T_done")
                sp.dma_start(out=dbg["d_x1"], in_=x1_dram).inc("st", 16)
                sp.dma_start(out=dbg["d_u2T"], in_=rbv("u2T")).inc("st", 16)
                c.mark("dbg_u2", "st")

        if stop >= 10:
            if router >= 2:
                pe.wait("u2T_done")
                for nt in range(4):
                    if nt >= 1 and router >= 3:
                        pe.wait(f"rlog_ev_{nt - 1}")
                    for dc in range(8):
                        ins = pe.matmul(PB[4][0:16, :],
                                  rwT_sb[:, dc * 16:(dc + 1) * 16],
                                  u2T3[:, dc, nt * 512:(nt + 1) * 512],
                                  start=(dc == 0), stop=(dc == 7),
                                  skip_group_check=True)
                        if dc == 7:
                            ins.inc("pe", 1)
                    c.mark(f"rlog_mm_{nt}", "pe")
                    if router >= 3:
                        dve.wait(f"rlog_mm_{nt}")
                        dve.tensor_scalar_add(
                            rlog_sb[:, nt * 512:(nt + 1) * 512],
                            PB[4][0:16, :], rb_sb[:, :]).inc("dve", 1)
                        c.mark(f"rlog_ev_{nt}", "dve")

            h3 = rbv("h_sb").rearrange("p (kf t) -> p kf t", t=512)
            for th in range(4):
                for fb in range(8):
                    gi = th * 8 + fb
                    w1r = rbv(f"w1r{gi % 3}")
                    if gi < 3:
                        gp.wait("u2T_done")
                    else:
                        gp.wait(f"h_mm_{(gi - 3) * 4 + 3}")
                    w1v = w1_in[:, fb * 512:(fb + 1) * 512].rearrange(
                        "(cp p) f -> p cp f", p=128)
                    wsem = f"w1{gi % 3}"
                    gp.wait_raw(wsem, 16 * (gi // 3))
                    gp.dma_start(out=w1r.rearrange("p (cp f) -> p cp f", f=512),
                                 in_=w1v).incn(wsem, 16)
                    c.mark(f"w1_ld_{gi}", wsem)
                    for fc in range(4):
                        j = gi * 4 + fc
                        pe.wait(f"w1_ld_{gi}")
                        if j == 0:
                            pe.wait("u2T_done")
                            pe.wait("w2_loaded")
                        if j >= 2:
                            pe.wait(f"h_relu_{j - 2}")
                        for dc in range(8):
                            ins = pe.matmul(PB[j % 2][:, :],
                                      w1r[:, dc * 512 + fc * 128:
                                          dc * 512 + (fc + 1) * 128],
                                      u2T3[:, dc, th * 512:(th + 1) * 512],
                                      start=(dc == 0), stop=(dc == 7))
                            if dc == 7:
                                ins.inc("pe", 1)
                        c.mark(f"h_mm_{j}", "pe")
                        act.wait(f"h_mm_{j}")
                        kf = fb * 4 + fc
                        act.activation(h3[:, kf, :], PB[j % 2][:, :], AF.Relu,
                                       bias=rfv("b1c")[:, kf:kf + 1]).inc("act", 1)
                        c.mark(f"h_relu_{j}", "act")
                for ts in range(4):
                    tc = th * 4 + ts
                    x1r = rfv("x1r0") if tc % 2 == 0 else rfv("x1r1")
                    if tc >= 2:
                        sp.wait(f"fin_tile_{tc - 2}")
                    else:
                        sp.wait("u2T_done")
                    sp.wait(f"x1_spill_{tc}")
                    sp.dma_start(out=x1r, in_=x1_dram[tc * 128:(tc + 1) * 128, :]
                                 ).inc("ld", 16)
                    c.mark(f"x1_reload_{tc}", "ld")
                    for dnt in range(2):
                        j2 = tc * 2 + dnt
                        pe.wait(f"h_relu_{th * 32 + 31}")
                        if j2 >= 2:
                            pe.wait(f"y_half_{j2 - 2}")
                        for kf in range(32):
                            pe.matmul(PB[2 + (j2 % 2)][:, :],
                                      h3[:, kf, ts * 128:(ts + 1) * 128],
                                      rbv("w2full")[:, kf * 1024 + dnt * 512:
                                                    kf * 1024 + (dnt + 1) * 512],
                                      start=(kf == 0),
                                      stop=(kf == 31))
                        pe.matmul(PB[2 + (j2 % 2)][:, :], ones1r_sb[:, :],
                                  rfv("b2row", 1)[:, dnt * 512:(dnt + 1) * 512],
                                  start=False, stop=True,
                                  skip_group_check=True).inc("pe", 1)
                        c.mark(f"eo_mm_{j2}", "pe")
                    dve.wait(f"x1_reload_{tc}")
                    if tc >= 1:
                        dve.wait(f"fin_tile_{tc - 1}")
                    dve.tensor_scalar(rfv("x1b"), x1r,
                                      rfv("ln2cols")[:, 2 * tc:2 * tc + 1],
                                      rfv("ln2cols")[:, 2 * tc + 1:2 * tc + 2],
                                      ALU.add, ALU.mult).inc("dve", 1)
                    c.mark(f"u2rc_{tc}", "dve")
                    dve.wait(f"u2rc_{tc}")
                    for dnt in range(2):
                        j2 = tc * 2 + dnt
                        dve.wait(f"eo_mm_{j2}")
                        dve.tensor_add(rfv("ytile")[:, dnt * 512:(dnt + 1) * 512],
                                       PB[2 + (j2 % 2)][:, :],
                                       rfv("x1b")[:, dnt * 512:(dnt + 1) * 512]
                                       ).inc("dve", 1)
                        c.mark(f"y_half_{j2}", "dve")
                    act.wait(f"y_half_{tc * 2 + 1}")
                    ln_tile(f"ln3_{tc}", rfv("ytile"), rfv("yn"), parity=tc % 2)
                    dve.wait(f"lnout_ln3_{tc}")
                    dve.tensor_mul(rfv("t2c"), rfv("yn"), rfv("gam_bc"))
                    dve.tensor_add(rfv("x1b"), x1r,
                                   rfv("bet_bc")).fence("dve")
                    c.mark(f"fin_pre_{tc}", "dve")
                    dve.tensor_add(rfv("ytile"), rfv("t2c"), rfv("x1b")).inc("dve", 1)
                    c.mark(f"fin_dve_{tc}", "dve")
                    sp.wait(f"fin_dve_{tc}")
                    sp.dma_start(out=y_out[tc * 128:(tc + 1) * 128, :],
                                 in_=rfv("ytile")).inc("st", 16)
                    c.mark(f"fin_tile_{tc}", "st")

        if c.live == "count":
            c.marks["__st_total"] = ("st", c.counts.get("st", 0))
        sp.wait_raw("st", c.marks["__st_total"][1])

    program(ctx)
    with (
        nc.semaphore("s_ld") as s_ld,
        nc.semaphore("s_wd") as s_wd,
        nc.semaphore("s_pe") as s_pe,
        nc.semaphore("s_act") as s_act,
        nc.semaphore("s_dve") as s_dve,
        nc.semaphore("s_cc") as s_cc,
        nc.semaphore("s_st") as s_st,
        nc.semaphore("s_w10") as s_w10,
        nc.semaphore("s_w11") as s_w11,
        nc.semaphore("s_w12") as s_w12,
        nc.Block() as block,
    ):
        ctx.semh = {"ld": s_ld, "wd": s_wd, "pe": s_pe, "act": s_act,
                    "dve": s_dve, "cc": s_cc, "st": s_st,
                    "w10": s_w10, "w11": s_w11, "w12": s_w12}

        @block.sync
        def _(eng):
            ctx.real["sp"] = eng
            ctx.begin("sp")
            program(ctx)
            ctx.finish()

        @block.gpsimd
        def _(eng):
            ctx.real["gp"] = eng
            ctx.begin("gp")
            program(ctx)
            ctx.finish()

        @block.tensor
        def _(eng):
            ctx.real["pe"] = eng
            ctx.begin("pe")
            program(ctx)
            ctx.finish()

        @block.scalar
        def _(eng):
            ctx.real["act"] = eng
            ctx.begin("act")
            program(ctx)
            ctx.finish()

        @block.vector
        def _(eng):
            ctx.real["dve"] = eng
            ctx.begin("dve")
            program(ctx)
            ctx.finish()

    return nc


def make_in_maps(inputs):
    x = np.asarray(inputs["L0_attn_x"], dtype=np.float32)
    w_qkv = np.asarray(inputs["L0_attn_w_qkv"], dtype=np.float32)
    w_out = np.asarray(inputs["L0_attn_w_out"], dtype=np.float32)
    signs = np.asarray(inputs["L0_attn_sorf_signs"], dtype=np.float32)
    scales = np.asarray(inputs["L0_attn_sorf_scales"], dtype=np.float32)
    w1 = np.asarray(inputs["L0_ffn_w1"], dtype=np.float32)
    b1 = np.asarray(inputs["L0_ffn_b1"], dtype=np.float32)
    w2 = np.asarray(inputs["L0_ffn_w2"], dtype=np.float32)
    b2 = np.asarray(inputs["L0_ffn_b2"], dtype=np.float32)
    gamma = np.asarray(inputs["L0_ffn_gamma"], dtype=np.float32)
    beta = np.asarray(inputs["L0_ffn_beta"], dtype=np.float32)

    hm = _hadamard(HD)
    swap = np.zeros((HD, HD), dtype=np.float32)
    for i in range(0, HD, 2):
        swap[i, i + 1] = 1.0
        swap[i + 1, i] = 1.0
    rep = np.zeros((NH, NH * HD), dtype=np.float32)
    for h in range(NH):
        rep[h, h * HD:(h + 1) * HD] = 1.0
    nh2 = np.zeros((128, 2), dtype=np.float32)
    nh2[0:64, 0] = -0.5
    nh2[64:128, 1] = -0.5

    sgn = signs.reshape(64, 3, 64)
    router_w = np.asarray(inputs["L0_ffn_router_w"], dtype=np.float32)
    router_b = np.asarray(inputs["L0_ffn_router_b"], dtype=np.float32)
    common = {
        "rwT": np.ascontiguousarray(np.concatenate(
            [router_w.T, np.zeros((E, 8), np.float32)], axis=1)),
        "rbcol": np.ascontiguousarray(np.concatenate(
            [router_b, np.zeros(8, np.float32)]).reshape(16, 1)),
        "wqkvT": np.ascontiguousarray(w_qkv.T),
        "woutT": np.ascontiguousarray(w_out.T),
        "w1_0": np.ascontiguousarray(w1[0]),
        "w2_0": np.ascontiguousarray(w2[0]),
        "b1c": np.ascontiguousarray(b1[0].reshape(32, 128).T),
        "b2row": np.ascontiguousarray(b2[0].reshape(1, E)),
        "grow": np.ascontiguousarray(gamma.reshape(1, E)),
        "brow": np.ascontiguousarray(beta.reshape(1, E)),
        "g_tiled": np.ascontiguousarray(
            np.tile(scales.reshape(1, 4096), (64, 1)).astype(np.float32)),
        "s1_tiled": np.ascontiguousarray(
            np.repeat(sgn[:, 0, :].T, 64, axis=1).astype(np.float32)),
        "s2_tiled": np.ascontiguousarray(
            np.repeat(sgn[:, 1, :].T, 64, axis=1).astype(np.float32)),
        "s3_tiled": np.ascontiguousarray(
            np.repeat(sgn[:, 2, :].T, 64, axis=1).astype(np.float32)),
        "h_tiled": np.ascontiguousarray(
            np.tile(hm, (1, 64)).astype(ml_dtypes.bfloat16)),
        "hmat": np.ascontiguousarray(hm.astype(ml_dtypes.bfloat16)),
        "hs": np.ascontiguousarray((hm / 4096.0).astype(ml_dtypes.bfloat16)),
        "swapm": np.ascontiguousarray(swap.astype(ml_dtypes.bfloat16)),
        "i128f": np.eye(128, dtype=np.float32),
        "actconsts": np.ascontiguousarray(np.tile(
            np.array([LN_EPS, -0.5, -1.0, LN_EPS_SC], dtype=np.float32),
            (128, 1))),
        "i128b": np.ascontiguousarray(
            np.eye(128, dtype=np.float32).astype(ml_dtypes.bfloat16)),
        "repl": np.ascontiguousarray(rep.astype(ml_dtypes.bfloat16)),
        "neghalf2": np.ascontiguousarray(nh2.astype(ml_dtypes.bfloat16)),
        "ones1r": np.ones((1, 128), dtype=np.float32),
    }

    in_maps = []
    for core in range(8):
        bb = core // 4
        t0 = (core % 4) * NTOK
        cosT, sinT = _rope_tables(core)
        m = dict(common)
        m["x_slice"] = np.ascontiguousarray(x[bb, t0:t0 + NTOK, :])
        m["cosT"] = cosT
        m["sinT"] = sinT
        in_maps.append(m)
    return in_maps


def kernel(**inputs):
    nc = build_program(debug=False)
    in_maps = make_in_maps(inputs)
    res = run_bass_kernel_spmd(nc, in_maps, core_ids=list(range(8)))
    out = np.empty((B, L, E), dtype=np.float32)
    for core in range(8):
        bb = core // 4
        t0 = (core % 4) * NTOK
        out[bb, t0:t0 + NTOK, :] = np.asarray(res.results[core]["y_out"],
                                              dtype=np.float32)
    return out
